# version 02
# baseline (speedup 1.0000x reference)
import numpy as np
from contextlib import ExitStack

import concourse.bass as bass
import concourse.mybir as mybir
from concourse.bass_utils import run_bass_kernel_spmd

F32 = mybir.dt.float32
BF16 = mybir.dt.bfloat16
AF = mybir.ActivationFunctionType
ALU = mybir.AluOpType
AX = mybir.AxisListType

NCORES = 8
D = 2048
DI = 3072
HID = 1408
E = 8
S = 8192
T = S // NCORES
TH = T // 2
DJ = D // 128
CI = DI // 128
HJ = HID // 128

ENGS = ("sync", "tensor", "vector", "scalar", "gpsimd")
DEBUG = False


class Plan:
    def __init__(self):
        self.q = {e: [] for e in ENGS}
        self.cnt = {}
        self.last_cell = {e: None for e in ENGS}
        self.last_inc = {e: None for e in ENGS}

    def inc(self, sem, by=1):
        self.cnt[sem] = self.cnt.get(sem, 0) + by
        return self.cnt[sem]

    def val(self, sem):
        return self.cnt.get(sem, 0)

    def op(self, eng, meth, *args, inc=None, **kw):
        v = None
        if inc is not None:
            sem, n = inc
            v = self.inc(sem, n)
        cell = [inc]

        def run(e, meth=meth, args=args, kw=kw, cell=cell):
            r = getattr(e, meth)(*args, **kw)
            if cell[0] is not None:
                r.then_inc(*cell[0])
        self.q[eng].append(run)
        self.last_cell[eng] = cell
        self.last_inc[eng] = (inc[0], v) if inc is not None else None
        return v

    def milestone(self, eng, default_sem):
        if self.last_cell[eng] is None:
            return None
        if self.last_inc[eng] is None:
            v = self.inc(default_sem, 1)
            self.last_cell[eng][0] = (default_sem, 1)
            self.last_inc[eng] = (default_sem, v)
        return self.last_inc[eng]

    def wait(self, eng, sem, v):
        if v <= 0:
            return
        self.q[eng].append(lambda e, s=sem, vv=v: e.wait_ge(s, vv))

    def dma(self, eng, out_ap, in_ap, sem):
        return self.op(eng, "dma_start", inc=(sem, 16), out=out_ap, in_=in_ap)


def build_nc():
    nc = bass.Bass("TRN2", target_bir_lowering=False)

    xT = nc.dram_tensor("xT", [D, T], F32, kind="ExternalInput")
    whg = nc.dram_tensor("whg", [D, 2 * DI], F32, kind="ExternalInput")
    wout = nc.dram_tensor("wout", [DI, D], F32, kind="ExternalInput")
    gwT = nc.dram_tensor("gwT", [D, E], F32, kind="ExternalInput")
    w1Tf = nc.dram_tensor("w1Tf", [E * D, HID], F32, kind="ExternalInput")
    w3Tf = nc.dram_tensor("w3Tf", [E * D, HID], F32, kind="ExternalInput")
    w2Tf = nc.dram_tensor("w2Tf", [E * HID, D], F32, kind="ExternalInput")
    sw1T = nc.dram_tensor("sw1T", [D, HID], F32, kind="ExternalInput")
    sw3T = nc.dram_tensor("sw3T", [D, HID], F32, kind="ExternalInput")
    sw2T = nc.dram_tensor("sw2T", [HID, D], F32, kind="ExternalInput")
    biasd = nc.dram_tensor("biasd", [1, E], F32, kind="ExternalInput")
    onesd = nc.dram_tensor("onesd", [128, 128], F32, kind="ExternalInput")
    identd = nc.dram_tensor("identd", [128, 128], F32, kind="ExternalInput")
    sel8d = nc.dram_tensor("sel8d", [E, E * 128], F32, kind="ExternalInput")
    maskMd = nc.dram_tensor("maskMd", [NCORES * 128, 48], F32, kind="ExternalInput")
    maskBd = nc.dram_tensor("maskBd", [NCORES * 128, 48], F32, kind="ExternalInput")
    yT = nc.dram_tensor("yT", [D, T], F32, kind="ExternalOutput")
    cc_in = nc.dram_tensor("cc_in", [128, 48], F32)
    cc_out = nc.dram_tensor("cc_out", [NCORES * 128, 48], F32)
    dbg = {}
    if DEBUG:
        for nm in ("dxn", "dres3", "dres5"):
            dbg[nm] = nc.dram_tensor(nm, [D, T], F32, kind="ExternalOutput")
        dbg["dch0"] = nc.dram_tensor("dch0", [128, 4 * T], F32, kind="ExternalOutput")
        dbg["dse"] = nc.dram_tensor("dse", [E, T], F32, kind="ExternalOutput")
        dbg["dagg"] = nc.dram_tensor("dagg", [128, 2 * CI], F32, kind="ExternalOutput")
        dbg["dcarry"] = nc.dram_tensor("dcarry", [128, CI], F32, kind="ExternalOutput")
        dbg["dgath"] = nc.dram_tensor("dgath", [128, NCORES * 48], F32, kind="ExternalOutput")
        dbg["dmskM"] = nc.dram_tensor("dmskM", [128, NCORES * 48], F32, kind="ExternalOutput")

    p = Plan()
    es = ExitStack()
    with es:
        ec = es.enter_context
        sW = ec(nc.semaphore("sW"))
        sX = ec(nc.semaphore("sX"))
        sG = ec(nc.semaphore("sG"))
        sCC = ec(nc.semaphore("sCC"))
        sPE = ec(nc.semaphore("sPE"))
        sACT = ec(nc.semaphore("sACT"))
        sDVE = ec(nc.semaphore("sDVE"))
        sPOOL = ec(nc.semaphore("sPOOL"))
        sOUT = ec(nc.semaphore("sOUT"))
        sBar = {e: ec(nc.semaphore("sB_" + e)) for e in ENGS}

        xn = ec(nc.sbuf_tensor("xn", [128, DJ * T], F32))
        resid = ec(nc.sbuf_tensor("resid", [128, DJ * T], F32))
        seT = ec(nc.sbuf_tensor("seT", [E, T], F32))
        ones = ec(nc.sbuf_tensor("ones", [128, 128], F32))
        sel8 = ec(nc.sbuf_tensor("sel8", [E, E * 128], F32))
        biasb = ec(nc.sbuf_tensor("biasb", [128, E], F32))
        bias1 = ec(nc.sbuf_tensor("bias1", [1, E], F32))
        carry = ec(nc.sbuf_tensor("carry", [128, CI], F32))
        hendA = ec(nc.sbuf_tensor("hendA", [128, CI], F32))
        aggO = ec(nc.sbuf_tensor("aggO", [128, 2 * CI], F32))
        barscr = ec(nc.sbuf_tensor("barscr", [1, 16], F32))
        zstack = ExitStack()
        zeros = zstack.enter_context(nc.sbuf_tensor("zeros", [128, T], F32))

        dsem = {"sync": sW, "tensor": sPE, "vector": sDVE,
                "scalar": sACT, "gpsimd": sPOOL}

        def barrier():
            marks = {}
            for eng in ENGS:
                m = p.milestone(eng, dsem[eng])
                if m is not None:
                    marks[eng] = m
            for eng in ENGS:
                for oth, (sem, v) in marks.items():
                    if oth != eng:
                        p.wait(eng, sem, v)

        def chunk(t, j, h=None):
            if h is None:
                return t[:, j * T:(j + 1) * T]
            return t[:, j * T + h * TH: j * T + (h + 1) * TH]

        def tile3(dram_slice, inner):
            return dram_slice.rearrange("(j p) x -> p j x", p=128)

        def sb3(sbuf_slice, inner):
            return sbuf_slice.rearrange("p (j x) -> p j x", x=inner)

        def dbg_big(nm, sb):
            if not DEBUG:
                return
            for gi in range(4):
                p.dma("gpsimd", tile3(dbg[nm].ap()[gi * 512:(gi + 1) * 512, :], T),
                      sb3(sb[:, gi * 4 * T:(gi + 1) * 4 * T], T), sG)

        def dbg_small(nm, sb_ap):
            if not DEBUG:
                return
            p.dma("gpsimd", dbg[nm].ap()[:, :], sb_ap, sG)

        p.dma("scalar", ones[:, :], onesd[0:128, :], sX)
        p.dma("scalar", sel8[:, :], sel8d[0:E, :], sX)
        p.dma("scalar", bias1[:, :], biasd[0:1, :], sX)
        for gi in range(4):
            p.dma("scalar", sb3(resid[:, gi * 4 * T:(gi + 1) * 4 * T], T),
                  tile3(xT.ap()[gi * 512:(gi + 1) * 512, :], T), sX)
        vX_pro = p.val(sX)
        p.op("gpsimd", "memset", zeros[:, :], 0.0, inc=(sPOOL, 1))
        p.op("vector", "memset", carry[:, :], 0.0, inc=(sDVE, 1))

        with ExitStack() as ph:
            pb_sum = ph.enter_context(nc.psum_tensor("pb_sum", [1, T], F32))
            pb_sumsq = ph.enter_context(nc.psum_tensor("pb_sumsq", [1, T], F32))
            pb_rstd = ph.enter_context(nc.psum_tensor("pb_rstd", [128, T], F32))
            pb_mrstd = ph.enter_context(nc.psum_tensor("pb_mrstd", [128, T], F32))
            sq_scr = ph.enter_context(nc.sbuf_tensor("sq_scr", [128, 2 * T], F32))
            mean_r = ph.enter_context(nc.sbuf_tensor("mean_r", [1, T], F32))
            var_r = ph.enter_context(nc.sbuf_tensor("var_r", [1, T], F32))
            sd_r = ph.enter_context(nc.sbuf_tensor("sd_r", [1, T], F32))
            rstd_r = ph.enter_context(nc.sbuf_tensor("rstd_r", [1, T], F32))
            mrstd_r = ph.enter_context(nc.sbuf_tensor("mrstd_r", [1, T], F32))

            p.wait("scalar", sX, vX_pro)
            p.wait("tensor", sX, vX_pro)
            act_sq = []
            pe_j = []
            for j in range(DJ):
                if j >= 2:
                    p.wait("scalar", sPE, pe_j[j - 2])
                act_sq.append(p.op("scalar", "square",
                                   sq_scr[:, (j % 2) * T:((j % 2) + 1) * T],
                                   chunk(resid, j), inc=(sACT, 1)))
                p.wait("tensor", sACT, act_sq[j])
                for h in range(2):
                    p.op("tensor", "matmul", pb_sum[:, h * TH:(h + 1) * TH],
                         ones[:, 0:1], chunk(resid, j, h),
                         start=(j == 0), stop=(j == DJ - 1))
                for h in range(2):
                    last = (j == DJ - 1) and (h == 1)
                    p.op("tensor", "matmul", pb_sumsq[:, h * TH:(h + 1) * TH],
                         ones[:, 0:1],
                         sq_scr[:, (j % 2) * T + h * TH:(j % 2) * T + (h + 1) * TH],
                         start=(j == 0), stop=(j == DJ - 1),
                         inc=(sPE, 1))
                pe_j.append(p.val(sPE))
            vPE_sums = p.val(sPE)

            p.wait("scalar", sPE, vPE_sums)
            p.op("scalar", "mul", mean_r[0:1, :], pb_sum[0:1, :], 1.0 / D)
            vACT_rows = p.op("scalar", "mul", var_r[0:1, :], pb_sumsq[0:1, :], 1.0 / D,
                             inc=(sACT, 1))
            p.wait("vector", sACT, vACT_rows)
            ve = p.op("vector", "tensor_mul", sd_r[0:1, :], mean_r[0:1, :],
                      mean_r[0:1, :], inc=(sDVE, 1))
            p.wait("vector", sDVE, ve)
            ve = p.op("vector", "tensor_sub", var_r[0:1, :], var_r[0:1, :], sd_r[0:1, :],
                      inc=(sDVE, 1))
            p.wait("vector", sDVE, ve)
            vDVE_var = p.op("vector", "tensor_scalar_add", var_r[0:1, :], var_r[0:1, :],
                            1e-5, inc=(sDVE, 1))
            p.wait("scalar", sDVE, vDVE_var)
            vACT_sd = p.op("scalar", "activation", sd_r[0:1, :], var_r[0:1, :],
                           AF.Sqrt, inc=(sACT, 1))
            p.wait("vector", sACT, vACT_sd)
            ve = p.op("vector", "reciprocal", rstd_r[0:1, :], sd_r[0:1, :],
                      inc=(sDVE, 1))
            p.wait("vector", sDVE, ve)
            vDVE_rstd = p.op("vector", "tensor_mul", mrstd_r[0:1, :], mean_r[0:1, :],
                             rstd_r[0:1, :], inc=(sDVE, 1))
            p.wait("tensor", sDVE, vDVE_rstd)
            for h in range(2):
                p.op("tensor", "matmul", pb_rstd[:, h * TH:(h + 1) * TH],
                     ones[0:1, :], rstd_r[0:1, h * TH:(h + 1) * TH], start=True, stop=True)
                last = (h == 1)
                p.op("tensor", "matmul", pb_mrstd[:, h * TH:(h + 1) * TH],
                     ones[0:1, :], mrstd_r[0:1, h * TH:(h + 1) * TH], start=True, stop=True,
                     inc=(sPE, 1) if last else None)
            vPE_bcast = p.val(sPE)
            p.wait("vector", sPE, vPE_bcast)
            for j in range(DJ):
                p.op("vector", "tensor_mul", chunk(xn, j), chunk(resid, j), pb_rstd[:, :])
                p.op("vector", "tensor_sub", chunk(xn, j), chunk(xn, j), pb_mrstd[:, :],
                     inc=(sDVE, 1) if j == DJ - 1 else None)
            barrier()

            dbg_big("dxn", xn)

        with ExitStack() as ph:
            pb_g0f = ph.enter_context(nc.psum_tensor("pb_g0", [128, 512], F32))
            pb_g1f = ph.enter_context(nc.psum_tensor("pb_g1", [128, 512], F32))
            pb_bbf = ph.enter_context(nc.psum_tensor("pb_bb", [128, 512], F32))
            pb_t0f = ph.enter_context(nc.psum_tensor("pb_t0", [128, 512], F32))
            pb_t1f = ph.enter_context(nc.psum_tensor("pb_t1", [128, 512], F32))
            pb_g0 = pb_g0f[:, 0:E]
            pb_g1 = pb_g1f[:, 0:E]
            pb_bb = pb_bbf[:, 0:E]
            pb_t0 = pb_t0f[0:E, 0:128]
            pb_t1 = pb_t1f[0:E, 0:128]
            gw_sb = ph.enter_context(nc.sbuf_tensor("gw_sb", [128, DJ * E], F32))
            ident = ph.enter_context(nc.sbuf_tensor("ident", [128, 128], F32))
            sc_sb = ph.enter_context(nc.sbuf_tensor("sc_sb", [128, 2 * E], F32))
            adj = ph.enter_context(nc.sbuf_tensor("adj", [128, E], F32))
            m1c = ph.enter_context(nc.sbuf_tensor("m1c", [128, 1], F32))
            mk1 = ph.enter_context(nc.sbuf_tensor("mk1", [128, E], F32))
            mk2 = ph.enter_context(nc.sbuf_tensor("mk2", [128, E], F32))
            ssel = ph.enter_context(nc.sbuf_tensor("ssel", [128, 2 * E], F32))
            lg_sb = ph.enter_context(nc.sbuf_tensor("lg_sb", [E, T], F32))

            vX_gw = p.dma("scalar", sb3(gw_sb[:, :], E),
                          tile3(gwT.ap()[:, :], E), sX)
            vX_id = p.dma("scalar", ident[:, :], identd[0:128, :], sX)

            p.op("tensor", "matmul", pb_bb, ones[0:1, :], bias1[0:1, :],
                 start=True, stop=True, inc=(sPE, 1))
            vPE_bb = p.val(sPE)
            p.wait("scalar", sPE, vPE_bb)
            vACT_bb = p.op("scalar", "copy", biasb[:, :], pb_bb, inc=(sACT, 1))

            pbg = [pb_g0f, pb_g1f]
            pbt = [pb_t0f, pb_t1f]
            lg_done = []
            p.wait("tensor", sX, vX_gw)
            for tb in range(2):
                bank = pbg[tb][0:E, 0:TH]
                for j in range(DJ):
                    p.op("tensor", "matmul", bank,
                         gw_sb[:, j * E:(j + 1) * E],
                         xn[:, j * T + tb * TH: j * T + (tb + 1) * TH],
                         start=(j == 0), stop=(j == DJ - 1),
                         inc=(sPE, 1) if j == DJ - 1 else None)
                vpe = p.val(sPE)
                p.wait("scalar", sPE, vpe)
                lg_done.append(p.op("scalar", "copy", lg_sb[:, tb * TH:(tb + 1) * TH],
                                    bank, inc=(sACT, 1)))
            act_sc = []
            dve_sel = []
            pe_tr1 = []
            pe_tr2 = []
            dve_cp = []
            p.wait("tensor", sX, vX_id)
            p.wait("tensor", sACT, lg_done[-1])
            for tc in range(8):
                t1 = pbg[tc % 2][0:128, 0:E]
                t2 = pbt[tc % 2][0:E, 0:128]
                if tc < 2:
                    pass
                else:
                    p.wait("tensor", sACT, act_sc[tc - 2])
                pe_tr1.append(p.op("tensor", "transpose", t1,
                                   lg_sb[0:E, tc * 128:(tc + 1) * 128],
                                   ident[0:E, 0:E], inc=(sPE, 1)))
                p.wait("scalar", sPE, pe_tr1[tc])
                scb = sc_sb[:, (tc % 2) * E:((tc % 2) + 1) * E]
                act_sc.append(p.op("scalar", "activation", scb, t1, AF.Sigmoid,
                                   inc=(sACT, 1)))
                p.wait("vector", sACT, act_sc[tc])
                if tc == 0:
                    p.wait("vector", sACT, vACT_bb)
                sselb = ssel[:, (tc % 2) * E:((tc % 2) + 1) * E]
                if tc >= 2:
                    p.wait("vector", sPE, pe_tr2[tc - 2])
                ve = p.op("vector", "tensor_add", adj[:, :], scb, biasb[:, :],
                          inc=(sDVE, 1))
                p.wait("vector", sDVE, ve)
                ve = p.op("vector", "tensor_reduce", m1c[:, :], adj[:, :], AX.X, ALU.max,
                          inc=(sDVE, 1))
                p.wait("vector", sDVE, ve)
                ve = p.op("vector", "tensor_single_scalar", mk1[:, :], adj[:, :],
                          m1c[:, 0:1], ALU.is_equal, inc=(sDVE, 1))
                p.wait("vector", sDVE, ve)
                ve = p.op("vector", "scalar_tensor_tensor", adj[:, :], mk1[:, :], -1e30,
                          adj[:, :], ALU.mult, ALU.add, inc=(sDVE, 1))
                p.wait("vector", sDVE, ve)
                ve = p.op("vector", "tensor_reduce", m1c[:, :], adj[:, :], AX.X, ALU.max,
                          inc=(sDVE, 1))
                p.wait("vector", sDVE, ve)
                ve = p.op("vector", "tensor_single_scalar", mk2[:, :], adj[:, :],
                          m1c[:, 0:1], ALU.is_equal, inc=(sDVE, 1))
                p.wait("vector", sDVE, ve)
                ve = p.op("vector", "tensor_add", mk1[:, :], mk1[:, :], mk2[:, :],
                          inc=(sDVE, 1))
                p.wait("vector", sDVE, ve)
                dve_sel.append(p.op("vector", "tensor_mul", sselb, mk1[:, :], scb,
                                    inc=(sDVE, 1)))
                p.wait("tensor", sDVE, dve_sel[tc])
                if tc >= 2:
                    p.wait("tensor", sDVE, dve_cp[tc - 2])
                pe_tr2.append(p.op("tensor", "transpose", t2, sselb, ident[:, :],
                                   inc=(sPE, 1)))
                p.wait("vector", sPE, pe_tr2[tc])
                dve_cp.append(p.op("vector", "tensor_copy",
                                   seT[:, tc * 128:(tc + 1) * 128], t2,
                                   inc=(sDVE, 1)))
            barrier()

            dbg_small("dse", seT[:, :])

        with ExitStack() as ph:
            pb_u1a = ph.enter_context(nc.psum_tensor("pb_u1a", [128, TH], F32))
            pb_u3a = ph.enter_context(nc.psum_tensor("pb_u3a", [128, TH], F32))
            pb_u1b = ph.enter_context(nc.psum_tensor("pb_u1b", [128, TH], F32))
            pb_u3b = ph.enter_context(nc.psum_tensor("pb_u3b", [128, TH], F32))
            pb_d0 = ph.enter_context(nc.psum_tensor("pb_d0", [128, TH], F32))
            pb_d1 = ph.enter_context(nc.psum_tensor("pb_d1", [128, TH], F32))
            pb_d2 = ph.enter_context(nc.psum_tensor("pb_d2", [128, TH], F32))
            pb_d3 = ph.enter_context(nc.psum_tensor("pb_d3", [128, TH], F32))
            w1s = ph.enter_context(nc.sbuf_tensor("w1s", [128, 2 * D], F32))
            w3s = ph.enter_context(nc.sbuf_tensor("w3s", [128, 2 * D], F32))
            w2s = ph.enter_context(nc.sbuf_tensor("w2s", [128, 2 * TH], F32))
            s1_scr = ph.enter_context(nc.sbuf_tensor("s1_scr", [128, TH], F32))
            g_s = ph.enter_context(nc.sbuf_tensor("g_s", [128, HJ * TH], F32))
            pb_u1 = [pb_u1a, pb_u1b]
            pb_u3 = [pb_u3a, pb_u3b]
            pb_dg = [pb_d0, pb_d1, pb_d2, pb_d3]

            def slab_dma(dram2d, buf, colbase, i, row0=0):
                vv = 0
                for hf in range(2):
                    src = tile3(dram2d.ap()[row0 + hf * 1024: row0 + (hf + 1) * 1024,
                                            i * 128:(i + 1) * 128], 128)
                    dst = sb3(buf[:, colbase + hf * 1024: colbase + (hf + 1) * 1024], 128)
                    vv = p.dma("sync", dst, src, sW)
                return vv

            pe_hist = []
            g_hist = []
            for h in range(2):
                for i in range(HJ):
                    k = h * HJ + i
                    b = k % 2
                    if k >= 2:
                        p.wait("sync", sPE, pe_hist[k - 2])
                    slab_dma(sw1T, w1s, b * D, i)
                    slabv = slab_dma(sw3T, w3s, b * D, i)
                    p.wait("tensor", sW, slabv)
                    if k >= 2:
                        p.wait("tensor", sDVE, g_hist[k - 2])
                    for j in range(DJ):
                        p.op("tensor", "matmul", pb_u1[b][:, :],
                             w1s[:, b * D + j * 128: b * D + (j + 1) * 128],
                             chunk(xn, j, h), start=(j == 0), stop=(j == DJ - 1))
                    for j in range(DJ):
                        p.op("tensor", "matmul", pb_u3[b][:, :],
                             w3s[:, b * D + j * 128: b * D + (j + 1) * 128],
                             chunk(xn, j, h), start=(j == 0), stop=(j == DJ - 1),
                             inc=(sPE, 1) if j == DJ - 1 else None)
                    pe_hist.append(p.val(sPE))
                    p.wait("scalar", sPE, pe_hist[k])
                    if k >= 1:
                        p.wait("scalar", sDVE, g_hist[k - 1])
                    vact = p.op("scalar", "activation", s1_scr[:, :], pb_u1[b][:, :],
                                AF.Silu, inc=(sACT, 1))
                    p.wait("vector", sACT, vact)
                    g_hist.append(p.op("vector", "tensor_mul",
                                       g_s[:, i * TH:(i + 1) * TH], s1_scr[:, :],
                                       pb_u3[b][:, :], inc=(sDVE, 1)))
                wq = []
                for dg in range(4):
                    for j in range(HJ):
                        b2 = (dg * HJ + j) % 2
                        if len(wq) >= 2:
                            p.wait("sync", sPE, wq[-2][1])
                        src = sw2T.ap()[j * 128:(j + 1) * 128, dg * TH:(dg + 1) * TH]
                        vw = p.dma("sync", w2s[:, b2 * TH:(b2 + 1) * TH], src, sW)
                        p.wait("tensor", sW, vw)
                        if dg == 0 and j == 0:
                            p.wait("tensor", sDVE, g_hist[h * HJ + HJ - 1])
                        for isub in range(4):
                            p.op("tensor", "matmul", pb_dg[isub][:, :],
                                 w2s[:, b2 * TH + isub * 128: b2 * TH + (isub + 1) * 128],
                                 g_s[:, j * TH:(j + 1) * TH],
                                 start=(j == 0), stop=(j == HJ - 1),
                                 inc=(sPE, 1) if (isub == 3) else None)
                        wq.append((None, p.val(sPE)))
                    vpe_dg = p.val(sPE)
                    p.wait("vector", sPE, vpe_dg)
                    for isub in range(4):
                        di = dg * 4 + isub
                        p.op("vector", "tensor_add", chunk(resid, di, h),
                             chunk(resid, di, h), pb_dg[isub][:, :],
                             inc=(sDVE, 1) if isub == 3 else None)
                    vdg = p.val(sDVE)
                    p.wait("tensor", sDVE, vdg)
            barrier()

            dbg_big("dres3", resid)

        with ExitStack() as ph:
            pa_h0 = ph.enter_context(nc.psum_tensor("pa_h0", [128, T], F32))
            pa_g0 = ph.enter_context(nc.psum_tensor("pa_g0", [128, T], F32))
            pa_h1 = ph.enter_context(nc.psum_tensor("pa_h1", [128, T], F32))
            pa_g1 = ph.enter_context(nc.psum_tensor("pa_g1", [128, T], F32))
            whbuf = ph.enter_context(nc.sbuf_tensor("whbuf", [128, 2 * D], F32))
            wgbuf = ph.enter_context(nc.sbuf_tensor("wgbuf", [128, 2 * D], F32))
            zbuf = ph.enter_context(nc.sbuf_tensor("zbuf", [128, T], F32))
            gbuf = ph.enter_context(nc.sbuf_tensor("gbuf", [128, T], F32))
            abuf = ph.enter_context(nc.sbuf_tensor("abuf", [128, T], F32))
            vbuf = ph.enter_context(nc.sbuf_tensor("vbuf", [128, T], F32))
            hscr = ph.enter_context(nc.sbuf_tensor("hscr", [128, T], F32))
            gscr = ph.enter_context(nc.sbuf_tensor("gscr", [128, T], F32))
            maskM = ph.enter_context(nc.sbuf_tensor("maskM", [128, NCORES * 48], F32))
            maskB = ph.enter_context(nc.sbuf_tensor("maskB", [128, NCORES * 48], F32))
            gathr = ph.enter_context(nc.sbuf_tensor("gathr", [128, NCORES * 48], F32))
            gmr = ph.enter_context(nc.sbuf_tensor("gmr", [128, 48], F32))
            ctmp = ph.enter_context(nc.sbuf_tensor("ctmp", [128, CI], F32))

            pa_h = [pa_h0, pa_h1]
            pa_g = [pa_g0, pa_g1]
            p.dma("gpsimd", sb3(maskM[:, :], 48),
                  tile3(maskMd.ap()[:, :], 48), sG)
            vG_masks = p.dma("gpsimd", sb3(maskB[:, :], 48),
                             tile3(maskBd.ap()[:, :], 48), sG)

            pe_c = []
            act_c = []
            dve_c = []
            pool_c = []
            slab_v = []
            for c in range(CI):
                b = c % 2
                if c >= 2:
                    p.wait("sync", sPE, pe_c[c - 2])
                for hf in range(2):
                    src = tile3(whg.ap()[hf * 1024:(hf + 1) * 1024, c * 128:(c + 1) * 128], 128)
                    p.dma("sync", sb3(whbuf[:, b * D + hf * 1024: b * D + (hf + 1) * 1024], 128), src, sW)
                for hf in range(2):
                    src = tile3(whg.ap()[hf * 1024:(hf + 1) * 1024, DI + c * 128: DI + (c + 1) * 128], 128)
                    p.dma("sync", sb3(wgbuf[:, b * D + hf * 1024: b * D + (hf + 1) * 1024], 128), src, sW)
                slab_v.append(p.val(sW))

                p.wait("tensor", sW, slab_v[c])
                if c >= 2:
                    p.wait("tensor", sDVE, dve_c[c - 2])
                    p.wait("tensor", sACT, act_c[c - 2])
                for h in range(2):
                    for j in range(DJ):
                        p.op("tensor", "matmul", pa_h[b][:, h * TH:(h + 1) * TH],
                             whbuf[:, b * D + j * 128: b * D + (j + 1) * 128],
                             chunk(xn, j, h), start=(j == 0), stop=(j == DJ - 1))
                for h in range(2):
                    for j in range(DJ):
                        p.op("tensor", "matmul", pa_g[b][:, h * TH:(h + 1) * TH],
                             wgbuf[:, b * D + j * 128: b * D + (j + 1) * 128],
                             chunk(xn, j, h), start=(j == 0), stop=(j == DJ - 1),
                             inc=(sPE, 1) if (h == 1 and j == DJ - 1) else None)
                pe_c.append(p.val(sPE))

                p.wait("scalar", sPE, pe_c[c])
                if c >= 1:
                    p.wait("scalar", sDVE, dve_c[c - 1])
                    p.wait("scalar", sPOOL, pool_c[c - 1])
                if DEBUG and c == 1:
                    p.wait("scalar", sG, vdump0)
                p.op("scalar", "activation", zbuf[:, :], pa_g[b][:, :], AF.Sigmoid)
                act_c.append(p.op("scalar", "activation", gbuf[:, :], pa_h[b][:, :],
                                  AF.Sigmoid, inc=(sACT, 1)))
                p.wait("gpsimd", sACT, act_c[c])
                if c >= 1:
                    p.wait("gpsimd", sDVE, dve_c[c - 1])
                pool_c.append(p.op("gpsimd", "tensor_scalar", abuf[:, :], zbuf[:, :],
                                   -1.0, 1.0, ALU.mult, ALU.add, inc=(sPOOL, 1)))
                p.wait("vector", sACT, act_c[c])
                ve = p.op("vector", "scalar_tensor_tensor", gbuf[:, :], pa_h[b][:, :],
                          0.5, gbuf[:, :], ALU.add, ALU.max, inc=(sDVE, 1))
                p.wait("vector", sDVE, ve)
                ve = p.op("vector", "tensor_mul", vbuf[:, :], zbuf[:, :], gbuf[:, :],
                          inc=(sDVE, 1))
                p.wait("vector", sDVE, ve)
                p.wait("vector", sPOOL, pool_c[c])
                ve = p.op("vector", "tensor_tensor_scan", hscr[:, :], abuf[:, :],
                          vbuf[:, :], 0.0, ALU.mult, ALU.add, inc=(sDVE, 1))
                p.wait("vector", sDVE, ve)
                vsc = p.op("vector", "tensor_tensor_scan", gscr[:, :], abuf[:, :],
                           zeros[:, :], 1.0, ALU.mult, ALU.add, inc=(sDVE, 1))
                p.wait("vector", sDVE, vsc)
                p.op("vector", "tensor_copy", aggO[:, c:c + 1], gscr[:, T - 1:T])
                dve_c.append(p.op("vector", "tensor_copy", aggO[:, CI + c: CI + c + 1],
                                  hscr[:, T - 1:T], inc=(sDVE, 1)))
                if DEBUG and c == 0:
                    p.wait("gpsimd", sDVE, dve_c[0])
                    for bi, bt in enumerate((zbuf, abuf, vbuf, gscr)):
                        p.dma("gpsimd", dbg["dch0"].ap()[:, bi * T:(bi + 1) * T],
                              bt[:, :], sG)
                    vdump0 = p.val(sG)

            vdve_agg = dve_c[-1]
            p.wait("gpsimd", sDVE, vdve_agg)
            vG_aggout = p.dma("gpsimd", cc_in.ap(), aggO[:, :], sG)
            p.wait("gpsimd", sG, vG_aggout)
            p.op("gpsimd", "collective_compute", "AllGather", ALU.bypass,
                 replica_groups=[list(range(NCORES))],
                 ins=[cc_in.ap()], outs=[cc_out.ap()], inc=(sCC, 1))
            vCC = p.val(sCC)
            p.wait("gpsimd", sCC, vCC)
            for r in range(NCORES):
                p.dma("gpsimd", gathr[:, r * 48:(r + 1) * 48],
                      cc_out.ap()[r * 128:(r + 1) * 128, :], sG)
            vG_gath = p.val(sG)

            p.wait("vector", sG, vG_gath)
            for r in range(NCORES):
                gr = gathr[:, r * 48:(r + 1) * 48]
                ve = p.op("vector", "tensor_mul", gmr[:, :], gr,
                          maskM[:, r * 48:(r + 1) * 48], inc=(sDVE, 1))
                p.wait("vector", sDVE, ve)
                ve = p.op("vector", "tensor_add", gmr[:, :], gmr[:, :],
                          maskB[:, r * 48:(r + 1) * 48], inc=(sDVE, 1))
                p.wait("vector", sDVE, ve)
                ve = p.op("vector", "tensor_mul", ctmp[:, :], gmr[:, 0:CI], carry[:, :],
                          inc=(sDVE, 1))
                p.wait("vector", sDVE, ve)
                ve = p.op("vector", "tensor_add", carry[:, :], ctmp[:, :],
                          gmr[:, CI:2 * CI], inc=(sDVE, 1))
                p.wait("vector", sDVE, ve)
            vDVE_carry = p.val(sDVE)
            barrier()

            dbg_small("dagg", aggO[:, :])
            dbg_small("dcarry", carry[:, :])
            dbg_small("dgath", gathr[:, :])
            dbg_small("dmskM", maskM[:, :])

        zstack.close()

        GRP = 12
        with ExitStack() as ph:
            pbh0 = ph.enter_context(nc.psum_tensor("pbh0", [128, TH], F32))
            pbg0 = ph.enter_context(nc.psum_tensor("pbg0", [128, TH], F32))
            pbh1 = ph.enter_context(nc.psum_tensor("pbh1", [128, TH], F32))
            pbg1 = ph.enter_context(nc.psum_tensor("pbg1", [128, TH], F32))
            po0 = ph.enter_context(nc.psum_tensor("po0", [128, TH], F32))
            po1 = ph.enter_context(nc.psum_tensor("po1", [128, TH], F32))
            po2 = ph.enter_context(nc.psum_tensor("po2", [128, TH], F32))
            po3 = ph.enter_context(nc.psum_tensor("po3", [128, TH], F32))
            whbufB = ph.enter_context(nc.sbuf_tensor("whbufB", [128, 2 * D], F32))
            wgbufB = ph.enter_context(nc.sbuf_tensor("wgbufB", [128, 2 * D], F32))
            woutbuf = ph.enter_context(nc.sbuf_tensor("woutbuf", [128, 2 * TH], F32))
            zbufB = ph.enter_context(nc.sbuf_tensor("zbufB", [128, TH], F32))
            gbufB = ph.enter_context(nc.sbuf_tensor("gbufB", [128, TH], F32))
            abufB = ph.enter_context(nc.sbuf_tensor("abufB", [128, TH], F32))
            vbufB = ph.enter_context(nc.sbuf_tensor("vbufB", [128, TH], F32))
            hgrp = ph.enter_context(nc.sbuf_tensor("hgrp", [128, GRP * TH], F32))

            pb_hB = [pbh0, pbh1]
            pb_gB = [pbg0, pbg1]
            po = [po0, po1, po2, po3]

            pe_cB = {}
            act_cB = {}
            dve_cB = {}
            pool_cB = {}
            vDVE_hendA = None
            seq = []
            for h in range(2):
                for c in range(CI):
                    seq.append((h, c))
            kidx = {hc: k for k, hc in enumerate(seq)}

            for h in range(2):
                for grp in range(2):
                    chunks = list(range(grp * GRP, (grp + 1) * GRP))
                    for ci_, c in enumerate(chunks):
                        k = kidx[(h, c)]
                        b = k % 2
                        if k >= 2:
                            p.wait("sync", sPE, pe_cB[seq[k - 2]])
                        for hf in range(2):
                            src = tile3(whg.ap()[hf * 1024:(hf + 1) * 1024,
                                                 c * 128:(c + 1) * 128], 128)
                            p.dma("sync", sb3(whbufB[:, b * D + hf * 1024: b * D + (hf + 1) * 1024], 128),
                                  src, sW)
                        for hf in range(2):
                            src = tile3(whg.ap()[hf * 1024:(hf + 1) * 1024,
                                                 DI + c * 128: DI + (c + 1) * 128], 128)
                            p.dma("sync", sb3(wgbufB[:, b * D + hf * 1024: b * D + (hf + 1) * 1024], 128),
                                  src, sW)
                        vw = p.val(sW)

                        p.wait("tensor", sW, vw)
                        if k >= 2:
                            p.wait("tensor", sDVE, dve_cB[seq[k - 2]])
                            p.wait("tensor", sACT, act_cB[seq[k - 2]])
                        for j in range(DJ):
                            p.op("tensor", "matmul", pb_hB[b][:, :],
                                 whbufB[:, b * D + j * 128: b * D + (j + 1) * 128],
                                 chunk(xn, j, h), start=(j == 0), stop=(j == DJ - 1))
                        for j in range(DJ):
                            p.op("tensor", "matmul", pb_gB[b][:, :],
                                 wgbufB[:, b * D + j * 128: b * D + (j + 1) * 128],
                                 chunk(xn, j, h), start=(j == 0), stop=(j == DJ - 1),
                                 inc=(sPE, 1) if j == DJ - 1 else None)
                        pe_cB[(h, c)] = p.val(sPE)

                        p.wait("scalar", sPE, pe_cB[(h, c)])
                        if k >= 1:
                            p.wait("scalar", sDVE, dve_cB[seq[k - 1]])
                            p.wait("scalar", sPOOL, pool_cB[seq[k - 1]])
                        p.op("scalar", "activation", zbufB[:, :], pb_gB[b][:, :], AF.Sigmoid)
                        act_cB[(h, c)] = p.op("scalar", "activation", gbufB[:, :],
                                              pb_hB[b][:, :], AF.Sigmoid, inc=(sACT, 1))
                        p.wait("gpsimd", sACT, act_cB[(h, c)])
                        if k >= 1:
                            p.wait("gpsimd", sDVE, dve_cB[seq[k - 1]])
                        pool_cB[(h, c)] = p.op("gpsimd", "tensor_scalar", abufB[:, :],
                                               zbufB[:, :], -1.0, 1.0, ALU.mult, ALU.add,
                                               inc=(sPOOL, 1))
                        p.wait("vector", sACT, act_cB[(h, c)])
                        ve = p.op("vector", "scalar_tensor_tensor", gbufB[:, :],
                                  pb_hB[b][:, :], 0.5, gbufB[:, :], ALU.add, ALU.max,
                                  inc=(sDVE, 1))
                        p.wait("vector", sDVE, ve)
                        ve = p.op("vector", "tensor_mul", vbufB[:, :], zbufB[:, :],
                                  gbufB[:, :], inc=(sDVE, 1))
                        p.wait("vector", sDVE, ve)
                        p.wait("vector", sPOOL, pool_cB[(h, c)])
                        if h == 0 and c == 0:
                            p.wait("vector", sDVE, vDVE_carry)
                        if h == 1 and c == 0:
                            p.wait("vector", sDVE, vDVE_hendA)
                        init = carry[:, c:c + 1] if h == 0 else hendA[:, c:c + 1]
                        vscB = p.op("vector", "tensor_tensor_scan",
                                    hgrp[:, ci_ * TH:(ci_ + 1) * TH], abufB[:, :],
                                    vbufB[:, :], init, ALU.mult, ALU.add, inc=(sDVE, 1))
                        if h == 0:
                            p.wait("vector", sDVE, vscB)
                            dve_cB[(h, c)] = p.op("vector", "tensor_copy",
                                                  hendA[:, c:c + 1],
                                                  hgrp[:, (ci_ + 1) * TH - 1:(ci_ + 1) * TH],
                                                  inc=(sDVE, 1))
                        else:
                            dve_cB[(h, c)] = vscB

                    if h == 0 and grp == 1:
                        vDVE_hendA = dve_cB[(0, CI - 1)]

                    vgrp_h = dve_cB[(h, chunks[-1])]
                    wo_hist = []
                    for dset in range(4):
                        for ci_, c in enumerate(chunks):
                            b2 = (dset * GRP + ci_) % 2
                            if len(wo_hist) >= 2:
                                p.wait("scalar", sPE, wo_hist[-2])
                            src = wout.ap()[c * 128:(c + 1) * 128,
                                            dset * TH:(dset + 1) * TH]
                            vwo = p.dma("scalar", woutbuf[:, b2 * TH:(b2 + 1) * TH], src, sX)
                            p.wait("tensor", sX, vwo)
                            if dset == 0 and ci_ == 0:
                                p.wait("tensor", sDVE, vgrp_h)
                            for isub in range(4):
                                p.op("tensor", "matmul", po[isub][:, :],
                                     woutbuf[:, b2 * TH + isub * 128: b2 * TH + (isub + 1) * 128],
                                     hgrp[:, ci_ * TH:(ci_ + 1) * TH],
                                     start=(ci_ == 0), stop=(ci_ == GRP - 1),
                                     inc=(sPE, 1) if isub == 3 else None)
                            wo_hist.append(p.val(sPE))
                        p.wait("vector", sPE, p.val(sPE))
                        for isub in range(4):
                            di = dset * 4 + isub
                            p.op("vector", "tensor_add", chunk(resid, di, h),
                                 chunk(resid, di, h), po[isub][:, :],
                                 inc=(sDVE, 1) if isub == 3 else None)
                        p.wait("tensor", sDVE, p.val(sDVE))
            barrier()

            dbg_big("dres5", resid)

        with ExitStack() as ph:
            pe_u1a = ph.enter_context(nc.psum_tensor("pe_u1a", [128, TH], F32))
            pe_u3a = ph.enter_context(nc.psum_tensor("pe_u3a", [128, TH], F32))
            pe_u1b = ph.enter_context(nc.psum_tensor("pe_u1b", [128, TH], F32))
            pe_u3b = ph.enter_context(nc.psum_tensor("pe_u3b", [128, TH], F32))
            pe_d0 = ph.enter_context(nc.psum_tensor("pe_d0", [128, TH], F32))
            pe_d1 = ph.enter_context(nc.psum_tensor("pe_d1", [128, TH], F32))
            pe_d2 = ph.enter_context(nc.psum_tensor("pe_d2", [128, TH], F32))
            pe_d3 = ph.enter_context(nc.psum_tensor("pe_d3", [128, TH], F32))
            w1e = ph.enter_context(nc.sbuf_tensor("w1e", [128, 2 * D], BF16))
            w3e = ph.enter_context(nc.sbuf_tensor("w3e", [128, 2 * D], BF16))
            w1f = ph.enter_context(nc.sbuf_tensor("w1f", [128, D], F32))
            w3f = ph.enter_context(nc.sbuf_tensor("w3f", [128, D], F32))
            w2e = ph.enter_context(nc.sbuf_tensor("w2e", [128, 2 * TH], BF16))
            w2f = ph.enter_context(nc.sbuf_tensor("w2f", [128, TH], F32))
            xe = ph.enter_context(nc.sbuf_tensor("xe", [128, DJ * TH], BF16))
            bcast = ph.enter_context(nc.sbuf_tensor("bcast", [128, TH], F32))
            s1e = ph.enter_context(nc.sbuf_tensor("s1e", [128, TH], F32))
            g_e = ph.enter_context(nc.sbuf_tensor("g_e", [128, HJ * TH], BF16))
            pu1 = [pe_u1a, pe_u1b]
            pu3 = [pe_u3a, pe_u3b]
            pdg = [pe_d0, pe_d1, pe_d2, pe_d3]

            hist_i = []
            hist_g = []
            vact_s1_last = 0
            vpool_xe_prev = 0
            vc1_prev = 0
            vc3_prev = 0
            he = 0
            for h in range(2):
                for e in range(E):
                    if he >= 1:
                        p.wait("tensor", sACT, vact_s1_last)
                    p.op("tensor", "matmul", pe_u1a[:, :], sel8[:, e * 128:(e + 1) * 128],
                         seT[:, h * TH:(h + 1) * TH], start=True, stop=True,
                         inc=(sPE, 1))
                    vpe_bc = p.val(sPE)
                    p.wait("scalar", sPE, vpe_bc)
                    if he >= 1:
                        p.wait("scalar", sPOOL, vpool_xe_prev)
                    vact_bc = p.op("scalar", "copy", bcast[:, :], pe_u1a[:, :],
                                   inc=(sACT, 1))
                    p.wait("gpsimd", sACT, vact_bc)
                    if he >= 1:
                        p.wait("gpsimd", sPE, hist_i[-1])
                    for j in range(DJ):
                        p.op("gpsimd", "tensor_mul", xe[:, j * TH:(j + 1) * TH],
                             chunk(xn, j, h), bcast[:, :],
                             inc=(sPOOL, 1) if j == DJ - 1 else None)
                    vpool_xe = p.val(sPOOL)
                    vpool_xe_prev = vpool_xe
                    p.wait("tensor", sPOOL, vpool_xe)

                    for i in range(HJ):
                        b = (he * HJ + i) % 2
                        p.wait("sync", sDVE, vc1_prev)
                        p.wait("sync", sPOOL, vc3_prev)
                        for hf in range(2):
                            src = tile3(w1Tf.ap()[e * D + hf * 1024: e * D + (hf + 1) * 1024,
                                                  i * 128:(i + 1) * 128], 128)
                            p.dma("sync", sb3(w1f[:, hf * 1024:(hf + 1) * 1024], 128), src, sW)
                        for hf in range(2):
                            src = tile3(w3Tf.ap()[e * D + hf * 1024: e * D + (hf + 1) * 1024,
                                                  i * 128:(i + 1) * 128], 128)
                            p.dma("sync", sb3(w3f[:, hf * 1024:(hf + 1) * 1024], 128), src, sW)
                        vwf = p.val(sW)
                        p.wait("vector", sW, vwf)
                        p.wait("gpsimd", sW, vwf)
                        if len(hist_i) >= 2:
                            p.wait("vector", sPE, hist_i[-2])
                            p.wait("gpsimd", sPE, hist_i[-2])
                        vc1 = p.op("vector", "tensor_copy", w1e[:, b * D:(b + 1) * D],
                                   w1f[:, :], inc=(sDVE, 1))
                        vc3 = p.op("gpsimd", "tensor_copy", w3e[:, b * D:(b + 1) * D],
                                   w3f[:, :], inc=(sPOOL, 1))
                        vc1_prev, vc3_prev = vc1, vc3
                        p.wait("tensor", sDVE, vc1)
                        p.wait("tensor", sPOOL, vc3)
                        if len(hist_i) >= 2:
                            p.wait("tensor", sDVE, hist_g[-2])
                        for j in range(DJ):
                            p.op("tensor", "matmul", pu1[b][:, :],
                                 w1e[:, b * D + j * 128: b * D + (j + 1) * 128],
                                 xe[:, j * TH:(j + 1) * TH],
                                 start=(j == 0), stop=(j == DJ - 1))
                        for j in range(DJ):
                            p.op("tensor", "matmul", pu3[b][:, :],
                                 w3e[:, b * D + j * 128: b * D + (j + 1) * 128],
                                 xe[:, j * TH:(j + 1) * TH],
                                 start=(j == 0), stop=(j == DJ - 1),
                                 inc=(sPE, 1) if j == DJ - 1 else None)
                        vpe_i = p.val(sPE)
                        hist_i.append(vpe_i)

                        p.wait("scalar", sPE, vpe_i)
                        if len(hist_g) >= 1:
                            p.wait("scalar", sDVE, hist_g[-1])
                        vact_s1 = p.op("scalar", "activation", s1e[:, :], pu1[b][:, :],
                                       AF.Silu, inc=(sACT, 1))
                        p.wait("vector", sACT, vact_s1)
                        vact_s1_last = vact_s1
                        hist_g.append(p.op("vector", "tensor_mul",
                                           g_e[:, i * TH:(i + 1) * TH], s1e[:, :],
                                           pu3[b][:, :], inc=(sDVE, 1)))

                    w2_hist = []
                    for dg in range(4):
                        for j in range(HJ):
                            b2 = (dg * HJ + j) % 2
                            if w2_hist:
                                p.wait("sync", sDVE, w2_hist[-1][0])
                            src = w2Tf.ap()[e * HID + j * 128: e * HID + (j + 1) * 128,
                                            dg * TH:(dg + 1) * TH]
                            vw2 = p.dma("sync", w2f[:, :], src, sW)
                            p.wait("vector", sW, vw2)
                            if len(w2_hist) >= 2:
                                p.wait("vector", sPE, w2_hist[-2][1])
                            vcw2 = p.op("vector", "tensor_copy",
                                        w2e[:, b2 * TH:(b2 + 1) * TH], w2f[:, :],
                                        inc=(sDVE, 1))
                            p.wait("tensor", sDVE, vcw2)
                            if dg == 0 and j == 0:
                                p.wait("tensor", sDVE, hist_g[-1])
                            for isub in range(4):
                                p.op("tensor", "matmul", pdg[isub][:, :],
                                     w2e[:, b2 * TH + isub * 128: b2 * TH + (isub + 1) * 128],
                                     g_e[:, j * TH:(j + 1) * TH],
                                     start=(j == 0), stop=(j == HJ - 1),
                                     inc=(sPE, 1) if isub == 3 else None)
                            w2_hist.append((vcw2, p.val(sPE)))
                        p.wait("vector", sPE, p.val(sPE))
                        for isub in range(4):
                            di = dg * 4 + isub
                            p.op("vector", "tensor_add", chunk(resid, di, h),
                                 chunk(resid, di, h), pdg[isub][:, :],
                                 inc=(sDVE, 1) if isub == 3 else None)
                        p.wait("tensor", sDVE, p.val(sDVE))
                    he += 1
            barrier()

        for gi in range(4):
            p.dma("scalar", tile3(yT.ap()[gi * 512:(gi + 1) * 512, :], T),
                  sb3(resid[:, gi * 4 * T:(gi + 1) * 4 * T], T), sOUT)
        vOUT = p.val(sOUT)
        p.wait("scalar", sOUT, vOUT)
        p.wait("sync", sOUT, vOUT)
        if DEBUG:
            p.wait("gpsimd", sG, p.val(sG))

        with nc.Block() as block:
            @block.sync
            def _(eng):
                for fn in p.q["sync"]:
                    fn(eng)

            @block.tensor
            def _(eng):
                for fn in p.q["tensor"]:
                    fn(eng)

            @block.vector
            def _(eng):
                for fn in p.q["vector"]:
                    fn(eng)

            @block.scalar
            def _(eng):
                for fn in p.q["scalar"]:
                    fn(eng)

            @block.gpsimd
            def _(eng):
                for fn in p.q["gpsimd"]:
                    fn(eng)

    return nc


def _host_stage(inputs):
    x = np.asarray(inputs["L0_attn_x"], np.float32)
    xT = np.ascontiguousarray(x[0].T)
    whg = np.ascontiguousarray(np.asarray(inputs["L0_attn_w_hg"], np.float32))
    wout = np.ascontiguousarray(np.asarray(inputs["L0_attn_w_out"], np.float32))
    gwT = np.ascontiguousarray(np.asarray(inputs["L0_ffn_gate_w"], np.float32).T)
    w1Tf = np.ascontiguousarray(
        np.swapaxes(np.asarray(inputs["L0_ffn_w1"], np.float32), 1, 2).reshape(E * D, HID))
    w3Tf = np.ascontiguousarray(
        np.swapaxes(np.asarray(inputs["L0_ffn_w3"], np.float32), 1, 2).reshape(E * D, HID))
    w2Tf = np.ascontiguousarray(
        np.swapaxes(np.asarray(inputs["L0_ffn_w2"], np.float32), 1, 2).reshape(E * HID, D))
    sw1T = np.ascontiguousarray(np.asarray(inputs["L0_ffn_sw1"], np.float32).T)
    sw3T = np.ascontiguousarray(np.asarray(inputs["L0_ffn_sw3"], np.float32).T)
    sw2T = np.ascontiguousarray(np.asarray(inputs["L0_ffn_sw2"], np.float32).T)
    biasd = np.ascontiguousarray(
        np.asarray(inputs["L0_ffn_expert_bias"], np.float32).reshape(1, E))
    onesd = np.ones((128, 128), np.float32)
    identd = np.eye(128, dtype=np.float32)
    sel8d = np.zeros((E, E * 128), np.float32)
    for e in range(E):
        sel8d[e, e * 128:(e + 1) * 128] = 1.0
    shared = dict(whg=whg, wout=wout, gwT=gwT, w1Tf=w1Tf, w3Tf=w3Tf, w2Tf=w2Tf,
                  sw1T=sw1T, sw3T=sw3T, sw2T=sw2T, biasd=biasd, onesd=onesd,
                  identd=identd, sel8d=sel8d)
    in_maps = []
    for c in range(NCORES):
        mM = np.zeros((NCORES * 128, 48), np.float32)
        mB = np.zeros((NCORES * 128, 48), np.float32)
        for r in range(NCORES):
            if r < c:
                mM[r * 128:(r + 1) * 128, :] = 1.0
            else:
                mB[r * 128:(r + 1) * 128, 0:CI] = 1.0
        m = dict(shared)
        m["xT"] = np.ascontiguousarray(xT[:, c * T:(c + 1) * T])
        m["maskMd"] = mM
        m["maskBd"] = mB
        in_maps.append(m)
    return in_maps


def kernel(**inputs):
    in_maps = _host_stage(inputs)
    nc = build_nc()
    res = run_bass_kernel_spmd(nc, in_maps, core_ids=list(range(NCORES)))
    yT = np.concatenate([res.results[c]["yT"] for c in range(NCORES)], axis=1)
    y = np.ascontiguousarray(yT.T).reshape(1, S, D).astype(np.float32)
    return y


# version 03
# speedup vs baseline: 1.3734x; 1.3734x over previous
import numpy as np
from contextlib import ExitStack

import concourse.bass as bass
import concourse.mybir as mybir
from concourse.bass_utils import run_bass_kernel_spmd

F32 = mybir.dt.float32
BF16 = mybir.dt.bfloat16
AF = mybir.ActivationFunctionType
ALU = mybir.AluOpType
AX = mybir.AxisListType

NCORES = 8
D = 2048
DI = 3072
HID = 1408
E = 8
S = 8192
T = S // NCORES
TH = T // 2
DJ = D // 128
CI = DI // 128
HJ = HID // 128

ENGS = ("sync", "tensor", "vector", "scalar", "gpsimd")
DEBUG = False


class Plan:
    def __init__(self):
        self.q = {e: [] for e in ENGS}
        self.cnt = {}
        self.last_cell = {e: None for e in ENGS}
        self.last_inc = {e: None for e in ENGS}

    def inc(self, sem, by=1):
        self.cnt[sem] = self.cnt.get(sem, 0) + by
        return self.cnt[sem]

    def val(self, sem):
        return self.cnt.get(sem, 0)

    def op(self, eng, meth, *args, inc=None, **kw):
        v = None
        if inc is not None:
            sem, n = inc
            v = self.inc(sem, n)
        cell = [inc]

        def run(e, meth=meth, args=args, kw=kw, cell=cell):
            r = getattr(e, meth)(*args, **kw)
            if cell[0] is not None:
                r.then_inc(*cell[0])
        self.q[eng].append(run)
        self.last_cell[eng] = cell
        self.last_inc[eng] = (inc[0], v) if inc is not None else None
        return v

    def milestone(self, eng, default_sem):
        if self.last_cell[eng] is None:
            return None
        if self.last_inc[eng] is None:
            v = self.inc(default_sem, 1)
            self.last_cell[eng][0] = (default_sem, 1)
            self.last_inc[eng] = (default_sem, v)
        return self.last_inc[eng]

    def wait(self, eng, sem, v):
        if v <= 0:
            return
        self.q[eng].append(lambda e, s=sem, vv=v: e.wait_ge(s, vv))

    def dma(self, eng, out_ap, in_ap, sem):
        return self.op(eng, "dma_start", inc=(sem, 16), out=out_ap, in_=in_ap)


def build_nc():
    nc = bass.Bass("TRN2", target_bir_lowering=False)

    xT = nc.dram_tensor("xT", [D, T], F32, kind="ExternalInput")
    whg = nc.dram_tensor("whg", [D, 2 * DI], F32, kind="ExternalInput")
    wout = nc.dram_tensor("wout", [DI, D], F32, kind="ExternalInput")
    gwT = nc.dram_tensor("gwT", [D, E], F32, kind="ExternalInput")
    w1Tf = nc.dram_tensor("w1Tf", [E * D, HID], F32, kind="ExternalInput")
    w3Tf = nc.dram_tensor("w3Tf", [E * D, HID], F32, kind="ExternalInput")
    w2Tf = nc.dram_tensor("w2Tf", [E * HID, D], F32, kind="ExternalInput")
    sw1T = nc.dram_tensor("sw1T", [D, HID], F32, kind="ExternalInput")
    sw3T = nc.dram_tensor("sw3T", [D, HID], F32, kind="ExternalInput")
    sw2T = nc.dram_tensor("sw2T", [HID, D], F32, kind="ExternalInput")
    biasd = nc.dram_tensor("biasd", [1, E], F32, kind="ExternalInput")
    onesd = nc.dram_tensor("onesd", [128, 128], F32, kind="ExternalInput")
    identd = nc.dram_tensor("identd", [128, 128], F32, kind="ExternalInput")
    sel8d = nc.dram_tensor("sel8d", [E, E * 128], F32, kind="ExternalInput")
    maskMd = nc.dram_tensor("maskMd", [NCORES * 128, 48], F32, kind="ExternalInput")
    maskBd = nc.dram_tensor("maskBd", [NCORES * 128, 48], F32, kind="ExternalInput")
    yT = nc.dram_tensor("yT", [D, T], F32, kind="ExternalOutput")
    cc_in = nc.dram_tensor("cc_in", [128, 48], F32)
    cc_out = nc.dram_tensor("cc_out", [NCORES * 128, 48], F32)
    dbg = {}
    if DEBUG:
        for nm in ("dxn", "dres3", "dres5"):
            dbg[nm] = nc.dram_tensor(nm, [D, T], F32, kind="ExternalOutput")
        dbg["dch0"] = nc.dram_tensor("dch0", [128, 4 * T], F32, kind="ExternalOutput")
        dbg["dse"] = nc.dram_tensor("dse", [E, T], F32, kind="ExternalOutput")
        dbg["dagg"] = nc.dram_tensor("dagg", [128, 2 * CI], F32, kind="ExternalOutput")
        dbg["dcarry"] = nc.dram_tensor("dcarry", [128, CI], F32, kind="ExternalOutput")
        dbg["dgath"] = nc.dram_tensor("dgath", [128, NCORES * 48], F32, kind="ExternalOutput")
        dbg["dmskM"] = nc.dram_tensor("dmskM", [128, NCORES * 48], F32, kind="ExternalOutput")

    p = Plan()
    es = ExitStack()
    with es:
        ec = es.enter_context
        sW = ec(nc.semaphore("sW"))
        sX = ec(nc.semaphore("sX"))
        sG = ec(nc.semaphore("sG"))
        sCC = ec(nc.semaphore("sCC"))
        sPE = ec(nc.semaphore("sPE"))
        sACT = ec(nc.semaphore("sACT"))
        sDVE = ec(nc.semaphore("sDVE"))
        sPOOL = ec(nc.semaphore("sPOOL"))
        sOUT = ec(nc.semaphore("sOUT"))
        sBar = {e: ec(nc.semaphore("sB_" + e)) for e in ENGS}

        resid = ec(nc.sbuf_tensor("resid", [128, DJ * T], F32))
        seT = ec(nc.sbuf_tensor("seT", [E, T], F32))
        ones = ec(nc.sbuf_tensor("ones", [128, 128], F32))
        sel8 = ec(nc.sbuf_tensor("sel8", [E, E * 128], F32))
        biasb = ec(nc.sbuf_tensor("biasb", [128, E], F32))
        bias1 = ec(nc.sbuf_tensor("bias1", [1, E], F32))
        carry = ec(nc.sbuf_tensor("carry", [128, CI], F32))
        hendA = ec(nc.sbuf_tensor("hendA", [128, CI], F32))
        aggO = ec(nc.sbuf_tensor("aggO", [128, 2 * CI], F32))
        barscr = ec(nc.sbuf_tensor("barscr", [1, 16], F32))
        xnstack = ExitStack()
        xn = xnstack.enter_context(nc.sbuf_tensor("xn", [128, DJ * T], F32))
        zstack = ExitStack()
        zeros = zstack.enter_context(nc.sbuf_tensor("zeros", [128, T], F32))

        dsem = {"sync": sW, "tensor": sPE, "vector": sDVE,
                "scalar": sACT, "gpsimd": sPOOL}

        def barrier():
            marks = {}
            for eng in ENGS:
                m = p.milestone(eng, dsem[eng])
                if m is not None:
                    marks[eng] = m
            for eng in ENGS:
                for oth, (sem, v) in marks.items():
                    if oth != eng:
                        p.wait(eng, sem, v)

        def chunk(t, j, h=None):
            if h is None:
                return t[:, j * T:(j + 1) * T]
            return t[:, j * T + h * TH: j * T + (h + 1) * TH]

        def tile3(dram_slice, inner):
            return dram_slice.rearrange("(j p) x -> p j x", p=128)

        def sb3(sbuf_slice, inner):
            return sbuf_slice.rearrange("p (j x) -> p j x", x=inner)

        def dbg_big(nm, sb):
            if not DEBUG:
                return
            for gi in range(4):
                p.dma("gpsimd", tile3(dbg[nm].ap()[gi * 512:(gi + 1) * 512, :], T),
                      sb3(sb[:, gi * 4 * T:(gi + 1) * 4 * T], T), sG)

        def dbg_small(nm, sb_ap):
            if not DEBUG:
                return
            p.dma("gpsimd", dbg[nm].ap()[:, :], sb_ap, sG)

        p.dma("scalar", ones[:, :], onesd[0:128, :], sX)
        p.dma("scalar", sel8[:, :], sel8d[0:E, :], sX)
        p.dma("scalar", bias1[:, :], biasd[0:1, :], sX)
        for gi in range(4):
            p.dma("scalar", sb3(resid[:, gi * 4 * T:(gi + 1) * 4 * T], T),
                  tile3(xT.ap()[gi * 512:(gi + 1) * 512, :], T), sX)
        vX_pro = p.val(sX)
        p.op("gpsimd", "memset", zeros[:, :], 0.0, inc=(sPOOL, 1))
        p.op("vector", "memset", carry[:, :], 0.0, inc=(sDVE, 1))

        with ExitStack() as ph:
            pb_sum = ph.enter_context(nc.psum_tensor("pb_sum", [1, T], F32))
            pb_sumsq = ph.enter_context(nc.psum_tensor("pb_sumsq", [1, T], F32))
            pb_rstd = ph.enter_context(nc.psum_tensor("pb_rstd", [128, T], F32))
            pb_mrstd = ph.enter_context(nc.psum_tensor("pb_mrstd", [128, T], F32))
            sq_scr = ph.enter_context(nc.sbuf_tensor("sq_scr", [128, 2 * T], F32))
            mean_r = ph.enter_context(nc.sbuf_tensor("mean_r", [1, T], F32))
            var_r = ph.enter_context(nc.sbuf_tensor("var_r", [1, T], F32))
            sd_r = ph.enter_context(nc.sbuf_tensor("sd_r", [1, T], F32))
            rstd_r = ph.enter_context(nc.sbuf_tensor("rstd_r", [1, T], F32))
            mrstd_r = ph.enter_context(nc.sbuf_tensor("mrstd_r", [1, T], F32))

            p.wait("scalar", sX, vX_pro)
            p.wait("tensor", sX, vX_pro)
            act_sq = []
            pe_j = []
            for j in range(DJ):
                if j >= 2:
                    p.wait("scalar", sPE, pe_j[j - 2])
                act_sq.append(p.op("scalar", "square",
                                   sq_scr[:, (j % 2) * T:((j % 2) + 1) * T],
                                   chunk(resid, j), inc=(sACT, 1)))
                p.wait("tensor", sACT, act_sq[j])
                for h in range(2):
                    p.op("tensor", "matmul", pb_sum[:, h * TH:(h + 1) * TH],
                         ones[:, 0:1], chunk(resid, j, h),
                         start=(j == 0), stop=(j == DJ - 1))
                for h in range(2):
                    last = (j == DJ - 1) and (h == 1)
                    p.op("tensor", "matmul", pb_sumsq[:, h * TH:(h + 1) * TH],
                         ones[:, 0:1],
                         sq_scr[:, (j % 2) * T + h * TH:(j % 2) * T + (h + 1) * TH],
                         start=(j == 0), stop=(j == DJ - 1),
                         inc=(sPE, 1))
                pe_j.append(p.val(sPE))
            vPE_sums = p.val(sPE)

            p.wait("scalar", sPE, vPE_sums)
            p.op("scalar", "mul", mean_r[0:1, :], pb_sum[0:1, :], 1.0 / D)
            vACT_rows = p.op("scalar", "mul", var_r[0:1, :], pb_sumsq[0:1, :], 1.0 / D,
                             inc=(sACT, 1))
            p.wait("vector", sACT, vACT_rows)
            ve = p.op("vector", "tensor_mul", sd_r[0:1, :], mean_r[0:1, :],
                      mean_r[0:1, :], inc=(sDVE, 1))
            p.wait("vector", sDVE, ve)
            ve = p.op("vector", "tensor_sub", var_r[0:1, :], var_r[0:1, :], sd_r[0:1, :],
                      inc=(sDVE, 1))
            p.wait("vector", sDVE, ve)
            vDVE_var = p.op("vector", "tensor_scalar_add", var_r[0:1, :], var_r[0:1, :],
                            1e-5, inc=(sDVE, 1))
            p.wait("scalar", sDVE, vDVE_var)
            vACT_sd = p.op("scalar", "activation", sd_r[0:1, :], var_r[0:1, :],
                           AF.Sqrt, inc=(sACT, 1))
            p.wait("vector", sACT, vACT_sd)
            ve = p.op("vector", "reciprocal", rstd_r[0:1, :], sd_r[0:1, :],
                      inc=(sDVE, 1))
            p.wait("vector", sDVE, ve)
            vDVE_rstd = p.op("vector", "tensor_mul", mrstd_r[0:1, :], mean_r[0:1, :],
                             rstd_r[0:1, :], inc=(sDVE, 1))
            p.wait("tensor", sDVE, vDVE_rstd)
            for h in range(2):
                p.op("tensor", "matmul", pb_rstd[:, h * TH:(h + 1) * TH],
                     ones[0:1, :], rstd_r[0:1, h * TH:(h + 1) * TH], start=True, stop=True)
                last = (h == 1)
                p.op("tensor", "matmul", pb_mrstd[:, h * TH:(h + 1) * TH],
                     ones[0:1, :], mrstd_r[0:1, h * TH:(h + 1) * TH], start=True, stop=True,
                     inc=(sPE, 1) if last else None)
            vPE_bcast = p.val(sPE)
            p.wait("vector", sPE, vPE_bcast)
            for j in range(DJ):
                p.op("vector", "tensor_mul", chunk(xn, j), chunk(resid, j), pb_rstd[:, :])
                p.op("vector", "tensor_sub", chunk(xn, j), chunk(xn, j), pb_mrstd[:, :],
                     inc=(sDVE, 1) if j == DJ - 1 else None)
            barrier()

            dbg_big("dxn", xn)

        with ExitStack() as ph:
            pb_g0f = ph.enter_context(nc.psum_tensor("pb_g0", [128, 512], F32))
            pb_g1f = ph.enter_context(nc.psum_tensor("pb_g1", [128, 512], F32))
            pb_bbf = ph.enter_context(nc.psum_tensor("pb_bb", [128, 512], F32))
            pb_t0f = ph.enter_context(nc.psum_tensor("pb_t0", [128, 512], F32))
            pb_t1f = ph.enter_context(nc.psum_tensor("pb_t1", [128, 512], F32))
            pb_g0 = pb_g0f[:, 0:E]
            pb_g1 = pb_g1f[:, 0:E]
            pb_bb = pb_bbf[:, 0:E]
            pb_t0 = pb_t0f[0:E, 0:128]
            pb_t1 = pb_t1f[0:E, 0:128]
            gw_sb = ph.enter_context(nc.sbuf_tensor("gw_sb", [128, DJ * E], F32))
            ident = ph.enter_context(nc.sbuf_tensor("ident", [128, 128], F32))
            sc_sb = ph.enter_context(nc.sbuf_tensor("sc_sb", [128, 2 * E], F32))
            adj = ph.enter_context(nc.sbuf_tensor("adj", [128, E], F32))
            m1c = ph.enter_context(nc.sbuf_tensor("m1c", [128, 1], F32))
            mk1 = ph.enter_context(nc.sbuf_tensor("mk1", [128, E], F32))
            mk2 = ph.enter_context(nc.sbuf_tensor("mk2", [128, E], F32))
            ssel = ph.enter_context(nc.sbuf_tensor("ssel", [128, 2 * E], F32))
            lg_sb = ph.enter_context(nc.sbuf_tensor("lg_sb", [E, T], F32))

            vX_gw = p.dma("scalar", sb3(gw_sb[:, :], E),
                          tile3(gwT.ap()[:, :], E), sX)
            vX_id = p.dma("scalar", ident[:, :], identd[0:128, :], sX)

            p.op("tensor", "matmul", pb_bb, ones[0:1, :], bias1[0:1, :],
                 start=True, stop=True, inc=(sPE, 1))
            vPE_bb = p.val(sPE)
            p.wait("scalar", sPE, vPE_bb)
            vACT_bb = p.op("scalar", "copy", biasb[:, :], pb_bb, inc=(sACT, 1))

            pbg = [pb_g0f, pb_g1f]
            pbt = [pb_t0f, pb_t1f]
            lg_done = []
            p.wait("tensor", sX, vX_gw)
            for tb in range(2):
                bank = pbg[tb][0:E, 0:TH]
                for j in range(DJ):
                    p.op("tensor", "matmul", bank,
                         gw_sb[:, j * E:(j + 1) * E],
                         xn[:, j * T + tb * TH: j * T + (tb + 1) * TH],
                         start=(j == 0), stop=(j == DJ - 1),
                         inc=(sPE, 1) if j == DJ - 1 else None)
                vpe = p.val(sPE)
                p.wait("scalar", sPE, vpe)
                lg_done.append(p.op("scalar", "copy", lg_sb[:, tb * TH:(tb + 1) * TH],
                                    bank, inc=(sACT, 1)))
            act_sc = []
            dve_sel = []
            pe_tr1 = []
            pe_tr2 = []
            dve_cp = []
            p.wait("tensor", sX, vX_id)
            p.wait("tensor", sACT, lg_done[-1])
            for tc in range(8):
                t1 = pbg[tc % 2][0:128, 0:E]
                t2 = pbt[tc % 2][0:E, 0:128]
                if tc < 2:
                    pass
                else:
                    p.wait("tensor", sACT, act_sc[tc - 2])
                pe_tr1.append(p.op("tensor", "transpose", t1,
                                   lg_sb[0:E, tc * 128:(tc + 1) * 128],
                                   ident[0:E, 0:E], inc=(sPE, 1)))
                p.wait("scalar", sPE, pe_tr1[tc])
                scb = sc_sb[:, (tc % 2) * E:((tc % 2) + 1) * E]
                act_sc.append(p.op("scalar", "activation", scb, t1, AF.Sigmoid,
                                   inc=(sACT, 1)))
                p.wait("vector", sACT, act_sc[tc])
                if tc == 0:
                    p.wait("vector", sACT, vACT_bb)
                sselb = ssel[:, (tc % 2) * E:((tc % 2) + 1) * E]
                if tc >= 2:
                    p.wait("vector", sPE, pe_tr2[tc - 2])
                ve = p.op("vector", "tensor_add", adj[:, :], scb, biasb[:, :],
                          inc=(sDVE, 1))
                p.wait("vector", sDVE, ve)
                ve = p.op("vector", "tensor_reduce", m1c[:, :], adj[:, :], AX.X, ALU.max,
                          inc=(sDVE, 1))
                p.wait("vector", sDVE, ve)
                ve = p.op("vector", "tensor_single_scalar", mk1[:, :], adj[:, :],
                          m1c[:, 0:1], ALU.is_equal, inc=(sDVE, 1))
                p.wait("vector", sDVE, ve)
                ve = p.op("vector", "scalar_tensor_tensor", adj[:, :], mk1[:, :], -1e30,
                          adj[:, :], ALU.mult, ALU.add, inc=(sDVE, 1))
                p.wait("vector", sDVE, ve)
                ve = p.op("vector", "tensor_reduce", m1c[:, :], adj[:, :], AX.X, ALU.max,
                          inc=(sDVE, 1))
                p.wait("vector", sDVE, ve)
                ve = p.op("vector", "tensor_single_scalar", mk2[:, :], adj[:, :],
                          m1c[:, 0:1], ALU.is_equal, inc=(sDVE, 1))
                p.wait("vector", sDVE, ve)
                ve = p.op("vector", "tensor_add", mk1[:, :], mk1[:, :], mk2[:, :],
                          inc=(sDVE, 1))
                p.wait("vector", sDVE, ve)
                dve_sel.append(p.op("vector", "tensor_mul", sselb, mk1[:, :], scb,
                                    inc=(sDVE, 1)))
                p.wait("tensor", sDVE, dve_sel[tc])
                if tc >= 2:
                    p.wait("tensor", sDVE, dve_cp[tc - 2])
                pe_tr2.append(p.op("tensor", "transpose", t2, sselb, ident[:, :],
                                   inc=(sPE, 1)))
                p.wait("vector", sPE, pe_tr2[tc])
                dve_cp.append(p.op("vector", "tensor_copy",
                                   seT[:, tc * 128:(tc + 1) * 128], t2,
                                   inc=(sDVE, 1)))
            barrier()

            dbg_small("dse", seT[:, :])

        with ExitStack() as ph:
            pb_u1a = ph.enter_context(nc.psum_tensor("pb_u1a", [128, TH], F32))
            pb_u3a = ph.enter_context(nc.psum_tensor("pb_u3a", [128, TH], F32))
            pb_u1b = ph.enter_context(nc.psum_tensor("pb_u1b", [128, TH], F32))
            pb_u3b = ph.enter_context(nc.psum_tensor("pb_u3b", [128, TH], F32))
            pb_d0 = ph.enter_context(nc.psum_tensor("pb_d0", [128, TH], F32))
            pb_d1 = ph.enter_context(nc.psum_tensor("pb_d1", [128, TH], F32))
            pb_d2 = ph.enter_context(nc.psum_tensor("pb_d2", [128, TH], F32))
            pb_d3 = ph.enter_context(nc.psum_tensor("pb_d3", [128, TH], F32))
            w1s = ph.enter_context(nc.sbuf_tensor("w1s", [128, 2 * D], F32))
            w3s = ph.enter_context(nc.sbuf_tensor("w3s", [128, 2 * D], F32))
            w2s = ph.enter_context(nc.sbuf_tensor("w2s", [128, 2 * TH], F32))
            s1_scr = ph.enter_context(nc.sbuf_tensor("s1_scr", [128, TH], F32))
            g_s = ph.enter_context(nc.sbuf_tensor("g_s", [128, HJ * TH], F32))
            pb_u1 = [pb_u1a, pb_u1b]
            pb_u3 = [pb_u3a, pb_u3b]
            pb_dg = [pb_d0, pb_d1, pb_d2, pb_d3]

            def slab_dma(dram2d, buf, colbase, i, row0=0):
                vv = 0
                for hf in range(2):
                    src = tile3(dram2d.ap()[row0 + hf * 1024: row0 + (hf + 1) * 1024,
                                            i * 128:(i + 1) * 128], 128)
                    dst = sb3(buf[:, colbase + hf * 1024: colbase + (hf + 1) * 1024], 128)
                    vv = p.dma("sync", dst, src, sW)
                return vv

            pe_hist = []
            g_hist = []
            for h in range(2):
                for i in range(HJ):
                    k = h * HJ + i
                    b = k % 2
                    if k >= 2:
                        p.wait("sync", sPE, pe_hist[k - 2])
                    slab_dma(sw1T, w1s, b * D, i)
                    slabv = slab_dma(sw3T, w3s, b * D, i)
                    p.wait("tensor", sW, slabv)
                    if k >= 2:
                        p.wait("tensor", sDVE, g_hist[k - 2])
                    for j in range(DJ):
                        p.op("tensor", "matmul", pb_u1[b][:, :],
                             w1s[:, b * D + j * 128: b * D + (j + 1) * 128],
                             chunk(xn, j, h), start=(j == 0), stop=(j == DJ - 1))
                    for j in range(DJ):
                        p.op("tensor", "matmul", pb_u3[b][:, :],
                             w3s[:, b * D + j * 128: b * D + (j + 1) * 128],
                             chunk(xn, j, h), start=(j == 0), stop=(j == DJ - 1),
                             inc=(sPE, 1) if j == DJ - 1 else None)
                    pe_hist.append(p.val(sPE))
                    p.wait("scalar", sPE, pe_hist[k])
                    if k >= 1:
                        p.wait("scalar", sDVE, g_hist[k - 1])
                    vact = p.op("scalar", "activation", s1_scr[:, :], pb_u1[b][:, :],
                                AF.Silu, inc=(sACT, 1))
                    p.wait("vector", sACT, vact)
                    g_hist.append(p.op("vector", "tensor_mul",
                                       g_s[:, i * TH:(i + 1) * TH], s1_scr[:, :],
                                       pb_u3[b][:, :], inc=(sDVE, 1)))
                wq = []
                for dg in range(4):
                    for j in range(HJ):
                        b2 = (dg * HJ + j) % 2
                        if len(wq) >= 2:
                            p.wait("sync", sPE, wq[-2][1])
                        src = sw2T.ap()[j * 128:(j + 1) * 128, dg * TH:(dg + 1) * TH]
                        vw = p.dma("sync", w2s[:, b2 * TH:(b2 + 1) * TH], src, sW)
                        p.wait("tensor", sW, vw)
                        if dg == 0 and j == 0:
                            p.wait("tensor", sDVE, g_hist[h * HJ + HJ - 1])
                        for isub in range(4):
                            p.op("tensor", "matmul", pb_dg[isub][:, :],
                                 w2s[:, b2 * TH + isub * 128: b2 * TH + (isub + 1) * 128],
                                 g_s[:, j * TH:(j + 1) * TH],
                                 start=(j == 0), stop=(j == HJ - 1),
                                 inc=(sPE, 1) if (isub == 3) else None)
                        wq.append((None, p.val(sPE)))
                    vpe_dg = p.val(sPE)
                    p.wait("vector", sPE, vpe_dg)
                    for isub in range(4):
                        di = dg * 4 + isub
                        p.op("vector", "tensor_add", chunk(resid, di, h),
                             chunk(resid, di, h), pb_dg[isub][:, :],
                             inc=(sDVE, 1) if isub == 3 else None)
                    vdg = p.val(sDVE)
                    p.wait("tensor", sDVE, vdg)
            barrier()

            dbg_big("dres3", resid)

        with ExitStack() as ph:
            pa_h0 = ph.enter_context(nc.psum_tensor("pa_h0", [128, T], F32))
            pa_g0 = ph.enter_context(nc.psum_tensor("pa_g0", [128, T], F32))
            pa_h1 = ph.enter_context(nc.psum_tensor("pa_h1", [128, T], F32))
            pa_g1 = ph.enter_context(nc.psum_tensor("pa_g1", [128, T], F32))
            whbuf = ph.enter_context(nc.sbuf_tensor("whbuf", [128, 2 * D], F32))
            wgbuf = ph.enter_context(nc.sbuf_tensor("wgbuf", [128, 2 * D], F32))
            zbuf = ph.enter_context(nc.sbuf_tensor("zbuf", [128, T], F32))
            gbuf = ph.enter_context(nc.sbuf_tensor("gbuf", [128, T], F32))
            abuf = ph.enter_context(nc.sbuf_tensor("abuf", [128, T], F32))
            vbuf = ph.enter_context(nc.sbuf_tensor("vbuf", [128, T], F32))
            hscr = ph.enter_context(nc.sbuf_tensor("hscr", [128, T], F32))
            gscr = ph.enter_context(nc.sbuf_tensor("gscr", [128, T], F32))
            maskM = ph.enter_context(nc.sbuf_tensor("maskM", [128, NCORES * 48], F32))
            maskB = ph.enter_context(nc.sbuf_tensor("maskB", [128, NCORES * 48], F32))
            gathr = ph.enter_context(nc.sbuf_tensor("gathr", [128, NCORES * 48], F32))
            gmr = ph.enter_context(nc.sbuf_tensor("gmr", [128, 48], F32))
            ctmp = ph.enter_context(nc.sbuf_tensor("ctmp", [128, CI], F32))

            pa_h = [pa_h0, pa_h1]
            pa_g = [pa_g0, pa_g1]
            p.dma("gpsimd", sb3(maskM[:, :], 48),
                  tile3(maskMd.ap()[:, :], 48), sG)
            vG_masks = p.dma("gpsimd", sb3(maskB[:, :], 48),
                             tile3(maskBd.ap()[:, :], 48), sG)

            pe_c = []
            act_c = []
            dve_c = []
            pool_c = []
            slab_v = []
            for c in range(CI):
                b = c % 2
                if c >= 2:
                    p.wait("sync", sPE, pe_c[c - 2])
                for hf in range(2):
                    src = tile3(whg.ap()[hf * 1024:(hf + 1) * 1024, c * 128:(c + 1) * 128], 128)
                    p.dma("sync", sb3(whbuf[:, b * D + hf * 1024: b * D + (hf + 1) * 1024], 128), src, sW)
                for hf in range(2):
                    src = tile3(whg.ap()[hf * 1024:(hf + 1) * 1024, DI + c * 128: DI + (c + 1) * 128], 128)
                    p.dma("sync", sb3(wgbuf[:, b * D + hf * 1024: b * D + (hf + 1) * 1024], 128), src, sW)
                slab_v.append(p.val(sW))

                p.wait("tensor", sW, slab_v[c])
                if c >= 2:
                    p.wait("tensor", sDVE, dve_c[c - 2])
                    p.wait("tensor", sACT, act_c[c - 2])
                for h in range(2):
                    for j in range(DJ):
                        p.op("tensor", "matmul", pa_h[b][:, h * TH:(h + 1) * TH],
                             whbuf[:, b * D + j * 128: b * D + (j + 1) * 128],
                             chunk(xn, j, h), start=(j == 0), stop=(j == DJ - 1))
                for h in range(2):
                    for j in range(DJ):
                        p.op("tensor", "matmul", pa_g[b][:, h * TH:(h + 1) * TH],
                             wgbuf[:, b * D + j * 128: b * D + (j + 1) * 128],
                             chunk(xn, j, h), start=(j == 0), stop=(j == DJ - 1),
                             inc=(sPE, 1) if (h == 1 and j == DJ - 1) else None)
                pe_c.append(p.val(sPE))

                p.wait("scalar", sPE, pe_c[c])
                if c >= 1:
                    p.wait("scalar", sDVE, dve_c[c - 1])
                    p.wait("scalar", sPOOL, pool_c[c - 1])
                if DEBUG and c == 1:
                    p.wait("scalar", sG, vdump0)
                p.op("scalar", "activation", zbuf[:, :], pa_g[b][:, :], AF.Sigmoid)
                act_c.append(p.op("scalar", "activation", gbuf[:, :], pa_h[b][:, :],
                                  AF.Sigmoid, inc=(sACT, 1)))
                p.wait("gpsimd", sACT, act_c[c])
                if c >= 1:
                    p.wait("gpsimd", sDVE, dve_c[c - 1])
                pool_c.append(p.op("gpsimd", "tensor_scalar", abuf[:, :], zbuf[:, :],
                                   -1.0, 1.0, ALU.mult, ALU.add, inc=(sPOOL, 1)))
                p.wait("vector", sACT, act_c[c])
                ve = p.op("vector", "scalar_tensor_tensor", gbuf[:, :], pa_h[b][:, :],
                          0.5, gbuf[:, :], ALU.add, ALU.max, inc=(sDVE, 1))
                p.wait("vector", sDVE, ve)
                ve = p.op("vector", "tensor_mul", vbuf[:, :], zbuf[:, :], gbuf[:, :],
                          inc=(sDVE, 1))
                p.wait("vector", sDVE, ve)
                p.wait("vector", sPOOL, pool_c[c])
                ve = p.op("vector", "tensor_tensor_scan", hscr[:, :], abuf[:, :],
                          vbuf[:, :], 0.0, ALU.mult, ALU.add, inc=(sDVE, 1))
                p.wait("vector", sDVE, ve)
                vsc = p.op("vector", "tensor_tensor_scan", gscr[:, :], abuf[:, :],
                           zeros[:, :], 1.0, ALU.mult, ALU.add, inc=(sDVE, 1))
                p.wait("vector", sDVE, vsc)
                p.op("vector", "tensor_copy", aggO[:, c:c + 1], gscr[:, T - 1:T])
                dve_c.append(p.op("vector", "tensor_copy", aggO[:, CI + c: CI + c + 1],
                                  hscr[:, T - 1:T], inc=(sDVE, 1)))
                if DEBUG and c == 0:
                    p.wait("gpsimd", sDVE, dve_c[0])
                    for bi, bt in enumerate((zbuf, abuf, vbuf, gscr)):
                        p.dma("gpsimd", dbg["dch0"].ap()[:, bi * T:(bi + 1) * T],
                              bt[:, :], sG)
                    vdump0 = p.val(sG)

            vdve_agg = dve_c[-1]
            p.wait("gpsimd", sDVE, vdve_agg)
            vG_aggout = p.dma("gpsimd", cc_in.ap(), aggO[:, :], sG)
            p.wait("gpsimd", sG, vG_aggout)
            p.op("gpsimd", "collective_compute", "AllGather", ALU.bypass,
                 replica_groups=[list(range(NCORES))],
                 ins=[cc_in.ap()], outs=[cc_out.ap()], inc=(sCC, 1))
            vCC = p.val(sCC)
            p.wait("gpsimd", sCC, vCC)
            for r in range(NCORES):
                p.dma("gpsimd", gathr[:, r * 48:(r + 1) * 48],
                      cc_out.ap()[r * 128:(r + 1) * 128, :], sG)
            vG_gath = p.val(sG)

            p.wait("vector", sG, vG_gath)
            for r in range(NCORES):
                gr = gathr[:, r * 48:(r + 1) * 48]
                ve = p.op("vector", "tensor_mul", gmr[:, :], gr,
                          maskM[:, r * 48:(r + 1) * 48], inc=(sDVE, 1))
                p.wait("vector", sDVE, ve)
                ve = p.op("vector", "tensor_add", gmr[:, :], gmr[:, :],
                          maskB[:, r * 48:(r + 1) * 48], inc=(sDVE, 1))
                p.wait("vector", sDVE, ve)
                ve = p.op("vector", "tensor_mul", ctmp[:, :], gmr[:, 0:CI], carry[:, :],
                          inc=(sDVE, 1))
                p.wait("vector", sDVE, ve)
                ve = p.op("vector", "tensor_add", carry[:, :], ctmp[:, :],
                          gmr[:, CI:2 * CI], inc=(sDVE, 1))
                p.wait("vector", sDVE, ve)
            vDVE_carry = p.val(sDVE)
            barrier()

            dbg_small("dagg", aggO[:, :])
            dbg_small("dcarry", carry[:, :])
            dbg_small("dgath", gathr[:, :])
            dbg_small("dmskM", maskM[:, :])

        zstack.close()

        GRP = 12
        with ExitStack() as ph:
            pbh0 = ph.enter_context(nc.psum_tensor("pbh0", [128, TH], F32))
            pbg0 = ph.enter_context(nc.psum_tensor("pbg0", [128, TH], F32))
            pbh1 = ph.enter_context(nc.psum_tensor("pbh1", [128, TH], F32))
            pbg1 = ph.enter_context(nc.psum_tensor("pbg1", [128, TH], F32))
            po0 = ph.enter_context(nc.psum_tensor("po0", [128, TH], F32))
            po1 = ph.enter_context(nc.psum_tensor("po1", [128, TH], F32))
            po2 = ph.enter_context(nc.psum_tensor("po2", [128, TH], F32))
            po3 = ph.enter_context(nc.psum_tensor("po3", [128, TH], F32))
            whbufB = ph.enter_context(nc.sbuf_tensor("whbufB", [128, 2 * D], F32))
            wgbufB = ph.enter_context(nc.sbuf_tensor("wgbufB", [128, 2 * D], F32))
            woutbuf = ph.enter_context(nc.sbuf_tensor("woutbuf", [128, 2 * TH], F32))
            zbufB = ph.enter_context(nc.sbuf_tensor("zbufB", [128, TH], F32))
            gbufB = ph.enter_context(nc.sbuf_tensor("gbufB", [128, TH], F32))
            abufB = ph.enter_context(nc.sbuf_tensor("abufB", [128, TH], F32))
            vbufB = ph.enter_context(nc.sbuf_tensor("vbufB", [128, TH], F32))
            hgrp = ph.enter_context(nc.sbuf_tensor("hgrp", [128, GRP * TH], F32))

            pb_hB = [pbh0, pbh1]
            pb_gB = [pbg0, pbg1]
            po = [po0, po1, po2, po3]

            pe_cB = {}
            act_cB = {}
            dve_cB = {}
            pool_cB = {}
            vDVE_hendA = None
            seq = []
            for h in range(2):
                for c in range(CI):
                    seq.append((h, c))
            kidx = {hc: k for k, hc in enumerate(seq)}

            for h in range(2):
                for grp in range(2):
                    chunks = list(range(grp * GRP, (grp + 1) * GRP))
                    for ci_, c in enumerate(chunks):
                        k = kidx[(h, c)]
                        b = k % 2
                        if k >= 2:
                            p.wait("sync", sPE, pe_cB[seq[k - 2]])
                        for hf in range(2):
                            src = tile3(whg.ap()[hf * 1024:(hf + 1) * 1024,
                                                 c * 128:(c + 1) * 128], 128)
                            p.dma("sync", sb3(whbufB[:, b * D + hf * 1024: b * D + (hf + 1) * 1024], 128),
                                  src, sW)
                        for hf in range(2):
                            src = tile3(whg.ap()[hf * 1024:(hf + 1) * 1024,
                                                 DI + c * 128: DI + (c + 1) * 128], 128)
                            p.dma("sync", sb3(wgbufB[:, b * D + hf * 1024: b * D + (hf + 1) * 1024], 128),
                                  src, sW)
                        vw = p.val(sW)

                        p.wait("tensor", sW, vw)
                        if k >= 2:
                            p.wait("tensor", sDVE, dve_cB[seq[k - 2]])
                            p.wait("tensor", sACT, act_cB[seq[k - 2]])
                        for j in range(DJ):
                            p.op("tensor", "matmul", pb_hB[b][:, :],
                                 whbufB[:, b * D + j * 128: b * D + (j + 1) * 128],
                                 chunk(xn, j, h), start=(j == 0), stop=(j == DJ - 1))
                        for j in range(DJ):
                            p.op("tensor", "matmul", pb_gB[b][:, :],
                                 wgbufB[:, b * D + j * 128: b * D + (j + 1) * 128],
                                 chunk(xn, j, h), start=(j == 0), stop=(j == DJ - 1),
                                 inc=(sPE, 1) if j == DJ - 1 else None)
                        pe_cB[(h, c)] = p.val(sPE)

                        p.wait("scalar", sPE, pe_cB[(h, c)])
                        if k >= 1:
                            p.wait("scalar", sDVE, dve_cB[seq[k - 1]])
                            p.wait("scalar", sPOOL, pool_cB[seq[k - 1]])
                        p.op("scalar", "activation", zbufB[:, :], pb_gB[b][:, :], AF.Sigmoid)
                        act_cB[(h, c)] = p.op("scalar", "activation", gbufB[:, :],
                                              pb_hB[b][:, :], AF.Sigmoid, inc=(sACT, 1))
                        p.wait("gpsimd", sACT, act_cB[(h, c)])
                        if k >= 1:
                            p.wait("gpsimd", sDVE, dve_cB[seq[k - 1]])
                        pool_cB[(h, c)] = p.op("gpsimd", "tensor_scalar", abufB[:, :],
                                               zbufB[:, :], -1.0, 1.0, ALU.mult, ALU.add,
                                               inc=(sPOOL, 1))
                        p.wait("vector", sACT, act_cB[(h, c)])
                        ve = p.op("vector", "scalar_tensor_tensor", gbufB[:, :],
                                  pb_hB[b][:, :], 0.5, gbufB[:, :], ALU.add, ALU.max,
                                  inc=(sDVE, 1))
                        p.wait("vector", sDVE, ve)
                        ve = p.op("vector", "tensor_mul", vbufB[:, :], zbufB[:, :],
                                  gbufB[:, :], inc=(sDVE, 1))
                        p.wait("vector", sDVE, ve)
                        p.wait("vector", sPOOL, pool_cB[(h, c)])
                        if h == 0 and c == 0:
                            p.wait("vector", sDVE, vDVE_carry)
                        if h == 1 and c == 0:
                            p.wait("vector", sDVE, vDVE_hendA)
                        init = carry[:, c:c + 1] if h == 0 else hendA[:, c:c + 1]
                        vscB = p.op("vector", "tensor_tensor_scan",
                                    hgrp[:, ci_ * TH:(ci_ + 1) * TH], abufB[:, :],
                                    vbufB[:, :], init, ALU.mult, ALU.add, inc=(sDVE, 1))
                        if h == 0:
                            p.wait("vector", sDVE, vscB)
                            dve_cB[(h, c)] = p.op("vector", "tensor_copy",
                                                  hendA[:, c:c + 1],
                                                  hgrp[:, (ci_ + 1) * TH - 1:(ci_ + 1) * TH],
                                                  inc=(sDVE, 1))
                        else:
                            dve_cB[(h, c)] = vscB

                    if h == 0 and grp == 1:
                        vDVE_hendA = dve_cB[(0, CI - 1)]

                    vgrp_h = dve_cB[(h, chunks[-1])]
                    wo_hist = []
                    for dset in range(4):
                        for ci_, c in enumerate(chunks):
                            b2 = (dset * GRP + ci_) % 2
                            if len(wo_hist) >= 2:
                                p.wait("scalar", sPE, wo_hist[-2])
                            src = wout.ap()[c * 128:(c + 1) * 128,
                                            dset * TH:(dset + 1) * TH]
                            vwo = p.dma("scalar", woutbuf[:, b2 * TH:(b2 + 1) * TH], src, sX)
                            p.wait("tensor", sX, vwo)
                            if dset == 0 and ci_ == 0:
                                p.wait("tensor", sDVE, vgrp_h)
                            for isub in range(4):
                                p.op("tensor", "matmul", po[isub][:, :],
                                     woutbuf[:, b2 * TH + isub * 128: b2 * TH + (isub + 1) * 128],
                                     hgrp[:, ci_ * TH:(ci_ + 1) * TH],
                                     start=(ci_ == 0), stop=(ci_ == GRP - 1),
                                     inc=(sPE, 1) if isub == 3 else None)
                            wo_hist.append(p.val(sPE))
                        p.wait("vector", sPE, p.val(sPE))
                        for isub in range(4):
                            di = dset * 4 + isub
                            p.op("vector", "tensor_add", chunk(resid, di, h),
                                 chunk(resid, di, h), po[isub][:, :],
                                 inc=(sDVE, 1) if isub == 3 else None)
                        p.wait("tensor", sDVE, p.val(sDVE))
            barrier()

            dbg_big("dres5", resid)

        xnstack.close()
        with ExitStack() as ph:
            pu = [ph.enter_context(nc.psum_tensor(f"pe_u{k}", [128, TH], F32))
                  for k in range(8)]
            xnbf = ph.enter_context(nc.sbuf_tensor("xnbf", [128, DJ * T], BF16,
                                                    side="right"))
            g_e = ph.enter_context(nc.sbuf_tensor("g_e", [128, HJ * 2 * TH], BF16))
            w1e = ph.enter_context(nc.sbuf_tensor("w1e", [128, 2 * D], BF16))
            w3e = ph.enter_context(nc.sbuf_tensor("w3e", [128, 2 * D], BF16))
            w1f = ph.enter_context(nc.sbuf_tensor("w1f", [128, D], F32))
            w3f = ph.enter_context(nc.sbuf_tensor("w3f", [128, D], F32))
            w2f = ph.enter_context(nc.sbuf_tensor("w2f", [128, 3 * TH], F32))
            w2e = ph.enter_context(nc.sbuf_tensor("w2e", [128, 3 * TH], BF16))
            bcast = ph.enter_context(nc.sbuf_tensor("bcast", [128, T], F32))
            s1e = ph.enter_context(nc.sbuf_tensor("s1e", [128, TH], F32))
            t1b = ph.enter_context(nc.sbuf_tensor("t1b", [128, TH], F32))
            t3b = ph.enter_context(nc.sbuf_tensor("t3b", [128, TH], F32))

            vcast = []
            for j in range(DJ):
                eng = "vector" if j % 2 == 0 else "gpsimd"
                vcast.append(p.op(eng, "tensor_copy", chunk(xnbf, j), chunk(xn, j),
                                  inc=((sDVE, 1) if eng == "vector" else (sPOOL, 1))))
            vDVE_xnbf = p.val(sDVE)
            vPOOL_xnbf = p.val(sPOOL)
            p.wait("tensor", sDVE, vDVE_xnbf)
            p.wait("tensor", sPOOL, vPOOL_xnbf)
            p.wait("sync", sDVE, vDVE_xnbf)
            p.wait("sync", sPOOL, vPOOL_xnbf)

            def emit_bcast(e, wait_dve=None):
                for hh in range(2):
                    p.op("tensor", "matmul", pu[4 + hh][:, :],
                         sel8[:, e * 128:(e + 1) * 128],
                         seT[:, hh * TH:(hh + 1) * TH], start=True, stop=True,
                         inc=(sPE, 1) if hh == 1 else None)
                vpe_bc = p.val(sPE)
                p.wait("scalar", sPE, vpe_bc)
                if wait_dve is not None:
                    p.wait("scalar", sDVE, wait_dve)
                for hh in range(2):
                    p.op("scalar", "copy", bcast[:, hh * TH:(hh + 1) * TH],
                         pu[4 + hh][:, :], inc=(sACT, 1) if hh == 1 else None)
                return p.val(sACT)

            vact_bc = emit_bcast(0)

            last_g = 0
            prev_set_g = {}
            vc_prev = {1: 0, 3: 0}
            hist_pe_i = []
            for e in range(E):
                for i in range(HJ):
                    b = (e * HJ + i) % 2
                    st = b * 4
                    p.wait("sync", sDVE, vc_prev[1])
                    p.wait("sync", sPOOL, vc_prev[3])
                    for hf in range(2):
                        srcw = tile3(w1Tf.ap()[e * D + hf * 1024: e * D + (hf + 1) * 1024,
                                               i * 128:(i + 1) * 128], 128)
                        p.dma("sync", sb3(w1f[:, hf * 1024:(hf + 1) * 1024], 128), srcw, sW)
                    for hf in range(2):
                        srcw = tile3(w3Tf.ap()[e * D + hf * 1024: e * D + (hf + 1) * 1024,
                                               i * 128:(i + 1) * 128], 128)
                        p.dma("sync", sb3(w3f[:, hf * 1024:(hf + 1) * 1024], 128), srcw, sW)
                    vwf = p.val(sW)
                    p.wait("vector", sW, vwf)
                    p.wait("gpsimd", sW, vwf)
                    if len(hist_pe_i) >= 2:
                        p.wait("vector", sPE, hist_pe_i[-2])
                        p.wait("gpsimd", sPE, hist_pe_i[-2])
                    vc1 = p.op("vector", "tensor_copy", w1e[:, b * D:(b + 1) * D],
                               w1f[:, :], inc=(sDVE, 1))
                    vc3 = p.op("gpsimd", "tensor_copy", w3e[:, b * D:(b + 1) * D],
                               w3f[:, :], inc=(sPOOL, 1))
                    vc_prev = {1: vc1, 3: vc3}
                    p.wait("tensor", sDVE, vc1)
                    p.wait("tensor", sPOOL, vc3)
                    k = e * HJ + i
                    if k >= 2:
                        p.wait("tensor", sDVE, prev_set_g[k - 2])
                    if k <= 1:
                        p.wait("tensor", sACT, vact_bc)
                    for ui, wsb in ((0, w1e), (1, w3e)):
                        for hh in range(2):
                            bank = pu[st + 2 * ui + hh]
                            for j in range(DJ):
                                p.op("tensor", "matmul", bank[:, :],
                                     wsb[:, b * D + j * 128: b * D + (j + 1) * 128],
                                     xnbf[:, j * T + hh * TH: j * T + (hh + 1) * TH],
                                     start=(j == 0), stop=(j == DJ - 1),
                                     inc=(sPE, 1) if (ui, hh, j) == (1, 1, DJ - 1) else None)
                    hist_pe_i.append(p.val(sPE))
                    p.wait("vector", sPE, hist_pe_i[-1])
                    if k == 0:
                        p.wait("vector", sACT, vact_bc)
                    for hh in range(2):
                        u1 = pu[st + hh]
                        u3 = pu[st + 2 + hh]
                        bsl = bcast[:, hh * TH:(hh + 1) * TH]
                        vt1 = p.op("vector", "tensor_mul", t1b[:, :], u1[:, :], bsl,
                                   inc=(sDVE, 1))
                        p.wait("scalar", sDVE, vt1)
                        va = p.op("scalar", "activation", s1e[:, :], t1b[:, :], AF.Silu,
                                  inc=(sACT, 1))
                        vt3 = p.op("vector", "tensor_tensor", t3b[:, :], u3[:, :], bsl,
                                   ALU.mult, inc=(sDVE, 1))
                        p.wait("vector", sDVE, vt3)
                        p.wait("vector", sACT, va)
                        vg = p.op("vector", "tensor_mul",
                                  g_e[:, (2 * i + hh) * TH:(2 * i + hh + 1) * TH],
                                  s1e[:, :], t3b[:, :], inc=(sDVE, 1))
                        p.wait("vector", sACT, va)
                    prev_set_g[k] = p.val(sDVE)
                last_g = p.val(sDVE)

                if e + 1 < E:
                    p.wait("tensor", sDVE, last_g)
                    vact_bc = emit_bcast(e + 1, wait_dve=last_g)

                w2hist = []
                for dg in range(4):
                    for j in range(HJ):
                        b2 = (dg * HJ + j) % 3
                        if len(w2hist) >= 3:
                            p.wait("sync", sDVE, w2hist[-3][0])
                        srcw = w2Tf.ap()[e * HID + j * 128: e * HID + (j + 1) * 128,
                                         dg * TH:(dg + 1) * TH]
                        vw2 = p.dma("sync", w2f[:, b2 * TH:(b2 + 1) * TH], srcw, sW)
                        p.wait("vector", sW, vw2)
                        if len(w2hist) >= 3:
                            p.wait("vector", sPE, w2hist[-3][1])
                        vcw2 = p.op("vector", "tensor_copy",
                                    w2e[:, b2 * TH:(b2 + 1) * TH],
                                    w2f[:, b2 * TH:(b2 + 1) * TH], inc=(sDVE, 1))
                        p.wait("tensor", sDVE, vcw2)
                        if dg == 0 and j == 0:
                            p.wait("tensor", sDVE, last_g)
                            p.wait("tensor", sACT, p.val(sACT))
                        for isub in range(4):
                            for hh in range(2):
                                bank = pu[isub * 2 + hh]
                                p.op("tensor", "matmul", bank[:, :],
                                     w2e[:, b2 * TH + isub * 128: b2 * TH + (isub + 1) * 128],
                                     g_e[:, (2 * j + hh) * TH:(2 * j + hh + 1) * TH],
                                     start=(j == 0), stop=(j == HJ - 1),
                                     inc=(sPE, 1) if (isub, hh) == (3, 1) else None)
                        w2hist.append((vcw2, p.val(sPE)))
                    vpe_dg = p.val(sPE)
                    p.wait("vector", sPE, vpe_dg)
                    for isub in range(4):
                        for hh in range(2):
                            di = dg * 4 + isub
                            p.op("vector", "tensor_add", chunk(resid, di, hh),
                                 chunk(resid, di, hh), pu[isub * 2 + hh][:, :],
                                 inc=(sDVE, 1) if (isub, hh) == (3, 1) else None)
                    p.wait("tensor", sDVE, p.val(sDVE))
                prev_set_g[e * HJ + HJ - 1] = p.val(sDVE)
                prev_set_g[e * HJ + HJ - 2] = p.val(sDVE)
            barrier()


        for gi in range(4):
            p.dma("scalar", tile3(yT.ap()[gi * 512:(gi + 1) * 512, :], T),
                  sb3(resid[:, gi * 4 * T:(gi + 1) * 4 * T], T), sOUT)
        vOUT = p.val(sOUT)
        p.wait("scalar", sOUT, vOUT)
        p.wait("sync", sOUT, vOUT)
        if DEBUG:
            p.wait("gpsimd", sG, p.val(sG))

        with nc.Block() as block:
            @block.sync
            def _(eng):
                for fn in p.q["sync"]:
                    fn(eng)

            @block.tensor
            def _(eng):
                for fn in p.q["tensor"]:
                    fn(eng)

            @block.vector
            def _(eng):
                for fn in p.q["vector"]:
                    fn(eng)

            @block.scalar
            def _(eng):
                for fn in p.q["scalar"]:
                    fn(eng)

            @block.gpsimd
            def _(eng):
                for fn in p.q["gpsimd"]:
                    fn(eng)

    return nc


def _host_stage(inputs):
    x = np.asarray(inputs["L0_attn_x"], np.float32)
    xT = np.ascontiguousarray(x[0].T)
    whg = np.ascontiguousarray(np.asarray(inputs["L0_attn_w_hg"], np.float32))
    wout = np.ascontiguousarray(np.asarray(inputs["L0_attn_w_out"], np.float32))
    gwT = np.ascontiguousarray(np.asarray(inputs["L0_ffn_gate_w"], np.float32).T)
    w1Tf = np.ascontiguousarray(
        np.swapaxes(np.asarray(inputs["L0_ffn_w1"], np.float32), 1, 2).reshape(E * D, HID))
    w3Tf = np.ascontiguousarray(
        np.swapaxes(np.asarray(inputs["L0_ffn_w3"], np.float32), 1, 2).reshape(E * D, HID))
    w2Tf = np.ascontiguousarray(
        np.swapaxes(np.asarray(inputs["L0_ffn_w2"], np.float32), 1, 2).reshape(E * HID, D))
    sw1T = np.ascontiguousarray(np.asarray(inputs["L0_ffn_sw1"], np.float32).T)
    sw3T = np.ascontiguousarray(np.asarray(inputs["L0_ffn_sw3"], np.float32).T)
    sw2T = np.ascontiguousarray(np.asarray(inputs["L0_ffn_sw2"], np.float32).T)
    biasd = np.ascontiguousarray(
        np.asarray(inputs["L0_ffn_expert_bias"], np.float32).reshape(1, E))
    onesd = np.ones((128, 128), np.float32)
    identd = np.eye(128, dtype=np.float32)
    sel8d = np.zeros((E, E * 128), np.float32)
    for e in range(E):
        sel8d[e, e * 128:(e + 1) * 128] = 1.0
    shared = dict(whg=whg, wout=wout, gwT=gwT, w1Tf=w1Tf, w3Tf=w3Tf, w2Tf=w2Tf,
                  sw1T=sw1T, sw3T=sw3T, sw2T=sw2T, biasd=biasd, onesd=onesd,
                  identd=identd, sel8d=sel8d)
    in_maps = []
    for c in range(NCORES):
        mM = np.zeros((NCORES * 128, 48), np.float32)
        mB = np.zeros((NCORES * 128, 48), np.float32)
        for r in range(NCORES):
            if r < c:
                mM[r * 128:(r + 1) * 128, :] = 1.0
            else:
                mB[r * 128:(r + 1) * 128, 0:CI] = 1.0
        m = dict(shared)
        m["xT"] = np.ascontiguousarray(xT[:, c * T:(c + 1) * T])
        m["maskMd"] = mM
        m["maskBd"] = mB
        in_maps.append(m)
    return in_maps


def kernel(**inputs):
    in_maps = _host_stage(inputs)
    nc = build_nc()
    res = run_bass_kernel_spmd(nc, in_maps, core_ids=list(range(NCORES)))
    yT = np.concatenate([res.results[c]["yT"] for c in range(NCORES)], axis=1)
    y = np.ascontiguousarray(yT.T).reshape(1, S, D).astype(np.float32)
    return y


# version 04
# speedup vs baseline: 1.9217x; 1.3993x over previous
import numpy as np
from contextlib import ExitStack

import concourse.bass as bass
import concourse.mybir as mybir
from concourse.bass_utils import run_bass_kernel_spmd

F32 = mybir.dt.float32
BF16 = mybir.dt.bfloat16
AF = mybir.ActivationFunctionType
ALU = mybir.AluOpType
AX = mybir.AxisListType

NCORES = 8
D = 2048
DI = 3072
HID = 1408
E = 8
S = 8192
T = S // NCORES
TH = T // 2
DJ = D // 128
CI = DI // 128
HJ = HID // 128

ENGS = ("sync", "tensor", "vector", "scalar", "gpsimd")
DEBUG = False


class Plan:
    def __init__(self):
        self.q = {e: [] for e in ENGS}
        self.cnt = {}
        self.last_cell = {e: None for e in ENGS}
        self.last_inc = {e: None for e in ENGS}

    def inc(self, sem, by=1):
        self.cnt[sem] = self.cnt.get(sem, 0) + by
        return self.cnt[sem]

    def val(self, sem):
        return self.cnt.get(sem, 0)

    def op(self, eng, meth, *args, inc=None, **kw):
        v = None
        if inc is not None:
            sem, n = inc
            v = self.inc(sem, n)
        cell = [inc]

        def run(e, meth=meth, args=args, kw=kw, cell=cell):
            r = getattr(e, meth)(*args, **kw)
            if cell[0] is not None:
                r.then_inc(*cell[0])
        self.q[eng].append(run)
        self.last_cell[eng] = cell
        self.last_inc[eng] = (inc[0], v) if inc is not None else None
        return v

    def milestone(self, eng, default_sem):
        if self.last_cell[eng] is None:
            return None
        if self.last_inc[eng] is None:
            v = self.inc(default_sem, 1)
            self.last_cell[eng][0] = (default_sem, 1)
            self.last_inc[eng] = (default_sem, v)
        return self.last_inc[eng]

    def wait(self, eng, sem, v):
        if v <= 0:
            return
        self.q[eng].append(lambda e, s=sem, vv=v: e.wait_ge(s, vv))

    def dma(self, eng, out_ap, in_ap, sem):
        return self.op(eng, "dma_start", inc=(sem, 16), out=out_ap, in_=in_ap)


def build_nc():
    nc = bass.Bass("TRN2", target_bir_lowering=False)

    xT = nc.dram_tensor("xT", [D, T], F32, kind="ExternalInput")
    whg = nc.dram_tensor("whg", [D, 2 * DI], F32, kind="ExternalInput")
    wout = nc.dram_tensor("wout", [DI, D], F32, kind="ExternalInput")
    gwT = nc.dram_tensor("gwT", [D, E], F32, kind="ExternalInput")
    w1Tf = nc.dram_tensor("w1Tf", [E * D, HID], F32, kind="ExternalInput")
    w3Tf = nc.dram_tensor("w3Tf", [E * D, HID], F32, kind="ExternalInput")
    w2Tf = nc.dram_tensor("w2Tf", [E * HID, D], F32, kind="ExternalInput")
    sw1T = nc.dram_tensor("sw1T", [D, HID], F32, kind="ExternalInput")
    sw3T = nc.dram_tensor("sw3T", [D, HID], F32, kind="ExternalInput")
    sw2T = nc.dram_tensor("sw2T", [HID, D], F32, kind="ExternalInput")
    biasd = nc.dram_tensor("biasd", [1, E], F32, kind="ExternalInput")
    onesd = nc.dram_tensor("onesd", [128, 128], F32, kind="ExternalInput")
    identd = nc.dram_tensor("identd", [128, 128], F32, kind="ExternalInput")
    sel8d = nc.dram_tensor("sel8d", [E, E * 128], F32, kind="ExternalInput")
    maskMd = nc.dram_tensor("maskMd", [NCORES * 128, 48], F32, kind="ExternalInput")
    maskBd = nc.dram_tensor("maskBd", [NCORES * 128, 48], F32, kind="ExternalInput")
    yT = nc.dram_tensor("yT", [D, T], F32, kind="ExternalOutput")
    cc_in = nc.dram_tensor("cc_in", [128, 48], F32)
    cc_out = nc.dram_tensor("cc_out", [NCORES * 128, 48], F32)
    dbg = {}
    if DEBUG:
        for nm in ("dxn", "dres3", "dres4", "dres5"):
            dbg[nm] = nc.dram_tensor(nm, [D, T], F32, kind="ExternalOutput")
        dbg["dch0"] = nc.dram_tensor("dch0", [128, 4 * T], F32, kind="ExternalOutput")
        dbg["dse"] = nc.dram_tensor("dse", [E, T], F32, kind="ExternalOutput")
        dbg["dhg"] = nc.dram_tensor("dhg", [128, 12 * TH], F32, kind="ExternalOutput")
        dbg["dagg"] = nc.dram_tensor("dagg", [128, 2 * CI], F32, kind="ExternalOutput")
        dbg["dcarry"] = nc.dram_tensor("dcarry", [128, CI], F32, kind="ExternalOutput")
        dbg["dgath"] = nc.dram_tensor("dgath", [128, NCORES * 48], F32, kind="ExternalOutput")
        dbg["dmskM"] = nc.dram_tensor("dmskM", [128, NCORES * 48], F32, kind="ExternalOutput")

    p = Plan()
    es = ExitStack()
    with es:
        ec = es.enter_context
        sW = ec(nc.semaphore("sW"))
        sX = ec(nc.semaphore("sX"))
        sG = ec(nc.semaphore("sG"))
        sCC = ec(nc.semaphore("sCC"))
        sPE = ec(nc.semaphore("sPE"))
        sACT = ec(nc.semaphore("sACT"))
        sDVE = ec(nc.semaphore("sDVE"))
        sPOOL = ec(nc.semaphore("sPOOL"))
        sOUT = ec(nc.semaphore("sOUT"))
        sBar = {e: ec(nc.semaphore("sB_" + e)) for e in ENGS}

        resid = ec(nc.sbuf_tensor("resid", [128, DJ * T], F32))
        xnbf = ec(nc.sbuf_tensor("xnbf", [128, DJ * T], BF16))
        seT = ec(nc.sbuf_tensor("seT", [E, T], F32))
        ones = ec(nc.sbuf_tensor("ones", [128, 128], F32))
        sel8 = ec(nc.sbuf_tensor("sel8", [E, E * 128], F32))
        biasb = ec(nc.sbuf_tensor("biasb", [128, E], F32))
        bias1 = ec(nc.sbuf_tensor("bias1", [1, E], F32))
        carry = ec(nc.sbuf_tensor("carry", [128, CI], F32))
        hendA = ec(nc.sbuf_tensor("hendA", [128, CI], F32))
        aggO = ec(nc.sbuf_tensor("aggO", [128, 2 * CI], F32))
        barscr = ec(nc.sbuf_tensor("barscr", [1, 16], F32))
        zstack = ExitStack()
        zeros = zstack.enter_context(nc.sbuf_tensor("zeros", [128, T], F32))
        xnstack = ExitStack()
        xn = xnstack.enter_context(nc.sbuf_tensor("xn", [128, DJ * T], F32))

        dsem = {"sync": sW, "tensor": sPE, "vector": sDVE,
                "scalar": sACT, "gpsimd": sPOOL}

        def barrier():
            marks = {}
            for eng in ENGS:
                m = p.milestone(eng, dsem[eng])
                if m is not None:
                    marks[eng] = m
            for eng in ENGS:
                for oth, (sem, v) in marks.items():
                    if oth != eng:
                        p.wait(eng, sem, v)

        def chunk(t, j, h=None):
            if h is None:
                return t[:, j * T:(j + 1) * T]
            return t[:, j * T + h * TH: j * T + (h + 1) * TH]

        def tile3(dram_slice, inner):
            return dram_slice.rearrange("(j p) x -> p j x", p=128)

        def sb3(sbuf_slice, inner):
            return sbuf_slice.rearrange("p (j x) -> p j x", x=inner)

        def dbg_big(nm, sb):
            if not DEBUG:
                return
            for gi in range(4):
                p.dma("gpsimd", tile3(dbg[nm].ap()[gi * 512:(gi + 1) * 512, :], T),
                      sb3(sb[:, gi * 4 * T:(gi + 1) * 4 * T], T), sG)

        def dbg_small(nm, sb_ap):
            if not DEBUG:
                return
            p.dma("gpsimd", dbg[nm].ap()[:, :], sb_ap, sG)

        p.dma("scalar", ones[:, :], onesd[0:128, :], sX)
        p.dma("scalar", sel8[:, :], sel8d[0:E, :], sX)
        p.dma("scalar", bias1[:, :], biasd[0:1, :], sX)
        for gi in range(4):
            p.dma("scalar", sb3(resid[:, gi * 4 * T:(gi + 1) * 4 * T], T),
                  tile3(xT.ap()[gi * 512:(gi + 1) * 512, :], T), sX)
        vX_pro = p.val(sX)
        p.op("gpsimd", "memset", zeros[:, :], 0.0, inc=(sPOOL, 1))
        p.op("vector", "memset", carry[:, :], 0.0, inc=(sDVE, 1))

        with ExitStack() as ph:
            pb_sum = ph.enter_context(nc.psum_tensor("pb_sum", [1, T], F32))
            pb_sumsq = ph.enter_context(nc.psum_tensor("pb_sumsq", [1, T], F32))
            pb_rstd = ph.enter_context(nc.psum_tensor("pb_rstd", [128, T], F32))
            pb_mrstd = ph.enter_context(nc.psum_tensor("pb_mrstd", [128, T], F32))
            sq_scr = ph.enter_context(nc.sbuf_tensor("sq_scr", [128, 2 * T], F32))
            mean_r = ph.enter_context(nc.sbuf_tensor("mean_r", [1, T], F32))
            var_r = ph.enter_context(nc.sbuf_tensor("var_r", [1, T], F32))
            sd_r = ph.enter_context(nc.sbuf_tensor("sd_r", [1, T], F32))
            rstd_r = ph.enter_context(nc.sbuf_tensor("rstd_r", [1, T], F32))
            mrstd_r = ph.enter_context(nc.sbuf_tensor("mrstd_r", [1, T], F32))

            p.wait("scalar", sX, vX_pro)
            p.wait("tensor", sX, vX_pro)
            act_sq = []
            pe_j = []
            for j in range(DJ):
                if j >= 2:
                    p.wait("scalar", sPE, pe_j[j - 2])
                act_sq.append(p.op("scalar", "square",
                                   sq_scr[:, (j % 2) * T:((j % 2) + 1) * T],
                                   chunk(resid, j), inc=(sACT, 1)))
                p.wait("tensor", sACT, act_sq[j])
                for h in range(2):
                    p.op("tensor", "matmul", pb_sum[:, h * TH:(h + 1) * TH],
                         ones[:, 0:1], chunk(resid, j, h),
                         start=(j == 0), stop=(j == DJ - 1))
                for h in range(2):
                    p.op("tensor", "matmul", pb_sumsq[:, h * TH:(h + 1) * TH],
                         ones[:, 0:1],
                         sq_scr[:, (j % 2) * T + h * TH:(j % 2) * T + (h + 1) * TH],
                         start=(j == 0), stop=(j == DJ - 1),
                         inc=(sPE, 1) if h == 1 else None)
                pe_j.append(p.val(sPE))
            vPE_sums = p.val(sPE)

            p.wait("scalar", sPE, vPE_sums)
            p.op("scalar", "mul", mean_r[0:1, :], pb_sum[0:1, :], 1.0 / D)
            vACT_rows = p.op("scalar", "mul", var_r[0:1, :], pb_sumsq[0:1, :], 1.0 / D,
                             inc=(sACT, 1))
            p.wait("vector", sACT, vACT_rows)
            ve = p.op("vector", "tensor_mul", sd_r[0:1, :], mean_r[0:1, :],
                      mean_r[0:1, :], inc=(sDVE, 1))
            p.wait("vector", sDVE, ve)
            ve = p.op("vector", "tensor_sub", var_r[0:1, :], var_r[0:1, :], sd_r[0:1, :],
                      inc=(sDVE, 1))
            p.wait("vector", sDVE, ve)
            vDVE_var = p.op("vector", "tensor_scalar_add", var_r[0:1, :], var_r[0:1, :],
                            1e-5, inc=(sDVE, 1))
            p.wait("scalar", sDVE, vDVE_var)
            vACT_sd = p.op("scalar", "activation", sd_r[0:1, :], var_r[0:1, :],
                           AF.Sqrt, inc=(sACT, 1))
            p.wait("vector", sACT, vACT_sd)
            ve = p.op("vector", "reciprocal", rstd_r[0:1, :], sd_r[0:1, :],
                      inc=(sDVE, 1))
            p.wait("vector", sDVE, ve)
            vDVE_rstd = p.op("vector", "tensor_mul", mrstd_r[0:1, :], mean_r[0:1, :],
                             rstd_r[0:1, :], inc=(sDVE, 1))
            p.wait("tensor", sDVE, vDVE_rstd)
            for h in range(2):
                p.op("tensor", "matmul", pb_rstd[:, h * TH:(h + 1) * TH],
                     ones[0:1, :], rstd_r[0:1, h * TH:(h + 1) * TH], start=True, stop=True)
                p.op("tensor", "matmul", pb_mrstd[:, h * TH:(h + 1) * TH],
                     ones[0:1, :], mrstd_r[0:1, h * TH:(h + 1) * TH], start=True,
                     stop=True, inc=(sPE, 1) if h == 1 else None)
            vPE_bcast = p.val(sPE)
            p.wait("vector", sPE, vPE_bcast)
            for j in range(DJ):
                p.op("vector", "tensor_mul", chunk(xn, j), chunk(resid, j), pb_rstd[:, :])
                ve = p.op("vector", "tensor_sub", chunk(xn, j), chunk(xn, j),
                          pb_mrstd[:, :], inc=(sDVE, 1))
                p.wait("gpsimd", sDVE, ve)
                p.op("gpsimd", "tensor_copy", chunk(xnbf, j), chunk(xn, j),
                     inc=(sPOOL, 1))
            barrier()

            dbg_big("dxn", xn)

        with ExitStack() as ph:
            pb_g0f = ph.enter_context(nc.psum_tensor("pb_g0", [128, 512], F32))
            pb_g1f = ph.enter_context(nc.psum_tensor("pb_g1", [128, 512], F32))
            pb_bbf = ph.enter_context(nc.psum_tensor("pb_bb", [128, 512], F32))
            pb_t0f = ph.enter_context(nc.psum_tensor("pb_t0", [128, 512], F32))
            pb_t1f = ph.enter_context(nc.psum_tensor("pb_t1", [128, 512], F32))
            pb_bb = pb_bbf[:, 0:E]
            gw_sb = ph.enter_context(nc.sbuf_tensor("gw_sb", [128, DJ * E], F32))
            ident = ph.enter_context(nc.sbuf_tensor("ident", [128, 128], F32))
            sc_sb = ph.enter_context(nc.sbuf_tensor("sc_sb", [128, 2 * E], F32))
            adj = ph.enter_context(nc.sbuf_tensor("adj", [128, E], F32))
            m1c = ph.enter_context(nc.sbuf_tensor("m1c", [128, 1], F32))
            mk1 = ph.enter_context(nc.sbuf_tensor("mk1", [128, E], F32))
            mk2 = ph.enter_context(nc.sbuf_tensor("mk2", [128, E], F32))
            ssel = ph.enter_context(nc.sbuf_tensor("ssel", [128, 2 * E], F32))
            lg_sb = ph.enter_context(nc.sbuf_tensor("lg_sb", [E, T], F32))

            vX_gw = p.dma("scalar", sb3(gw_sb[:, :], E), tile3(gwT.ap()[:, :], E), sX)
            vX_id = p.dma("scalar", ident[:, :], identd[0:128, :], sX)

            p.op("tensor", "matmul", pb_bb, ones[0:1, :], bias1[0:1, :],
                 start=True, stop=True, inc=(sPE, 1))
            vPE_bb = p.val(sPE)
            p.wait("scalar", sPE, vPE_bb)
            vACT_bb = p.op("scalar", "copy", biasb[:, :], pb_bb, inc=(sACT, 1))

            pbg = [pb_g0f, pb_g1f]
            pbt = [pb_t0f, pb_t1f]
            lg_done = []
            p.wait("tensor", sX, vX_gw)
            for tb in range(2):
                bank = pbg[tb][0:E, 0:TH]
                for j in range(DJ):
                    p.op("tensor", "matmul", bank,
                         gw_sb[:, j * E:(j + 1) * E],
                         xn[:, j * T + tb * TH: j * T + (tb + 1) * TH],
                         start=(j == 0), stop=(j == DJ - 1),
                         inc=(sPE, 1) if j == DJ - 1 else None)
                vpe = p.val(sPE)
                p.wait("scalar", sPE, vpe)
                lg_done.append(p.op("scalar", "copy", lg_sb[:, tb * TH:(tb + 1) * TH],
                                    bank, inc=(sACT, 1)))
            act_sc = []
            dve_sel = []
            pe_tr1 = []
            pe_tr2 = []
            dve_cp = []
            p.wait("tensor", sX, vX_id)
            p.wait("tensor", sACT, lg_done[-1])
            for tc in range(8):
                t1 = pbg[tc % 2][0:128, 0:E]
                t2 = pbt[tc % 2][0:E, 0:128]
                if tc >= 2:
                    p.wait("tensor", sACT, act_sc[tc - 2])
                pe_tr1.append(p.op("tensor", "transpose", t1,
                                   lg_sb[0:E, tc * 128:(tc + 1) * 128],
                                   ident[0:E, 0:E], inc=(sPE, 1)))
                p.wait("scalar", sPE, pe_tr1[tc])
                scb = sc_sb[:, (tc % 2) * E:((tc % 2) + 1) * E]
                act_sc.append(p.op("scalar", "activation", scb, t1, AF.Sigmoid,
                                   inc=(sACT, 1)))
                p.wait("vector", sACT, act_sc[tc])
                if tc == 0:
                    p.wait("vector", sACT, vACT_bb)
                sselb = ssel[:, (tc % 2) * E:((tc % 2) + 1) * E]
                if tc >= 2:
                    p.wait("vector", sPE, pe_tr2[tc - 2])
                ve = p.op("vector", "tensor_add", adj[:, :], scb, biasb[:, :],
                          inc=(sDVE, 1))
                p.wait("vector", sDVE, ve)
                ve = p.op("vector", "tensor_reduce", m1c[:, :], adj[:, :], AX.X, ALU.max,
                          inc=(sDVE, 1))
                p.wait("vector", sDVE, ve)
                ve = p.op("vector", "tensor_single_scalar", mk1[:, :], adj[:, :],
                          m1c[:, 0:1], ALU.is_equal, inc=(sDVE, 1))
                p.wait("vector", sDVE, ve)
                ve = p.op("vector", "scalar_tensor_tensor", adj[:, :], mk1[:, :], -1e30,
                          adj[:, :], ALU.mult, ALU.add, inc=(sDVE, 1))
                p.wait("vector", sDVE, ve)
                ve = p.op("vector", "tensor_reduce", m1c[:, :], adj[:, :], AX.X, ALU.max,
                          inc=(sDVE, 1))
                p.wait("vector", sDVE, ve)
                ve = p.op("vector", "tensor_single_scalar", mk2[:, :], adj[:, :],
                          m1c[:, 0:1], ALU.is_equal, inc=(sDVE, 1))
                p.wait("vector", sDVE, ve)
                ve = p.op("vector", "tensor_add", mk1[:, :], mk1[:, :], mk2[:, :],
                          inc=(sDVE, 1))
                p.wait("vector", sDVE, ve)
                dve_sel.append(p.op("vector", "tensor_mul", sselb, mk1[:, :], scb,
                                    inc=(sDVE, 1)))
                p.wait("tensor", sDVE, dve_sel[tc])
                if tc >= 2:
                    p.wait("tensor", sDVE, dve_cp[tc - 2])
                pe_tr2.append(p.op("tensor", "transpose", t2, sselb, ident[:, :],
                                   inc=(sPE, 1)))
                p.wait("vector", sPE, pe_tr2[tc])
                dve_cp.append(p.op("vector", "tensor_copy",
                                   seT[:, tc * 128:(tc + 1) * 128], t2,
                                   inc=(sDVE, 1)))
            barrier()

            dbg_small("dse", seT[:, :])
        xnstack.close()

        with ExitStack() as ph:
            pb_u1a = ph.enter_context(nc.psum_tensor("pb_u1a", [128, TH], F32))
            pb_u3a = ph.enter_context(nc.psum_tensor("pb_u3a", [128, TH], F32))
            pb_u1b = ph.enter_context(nc.psum_tensor("pb_u1b", [128, TH], F32))
            pb_u3b = ph.enter_context(nc.psum_tensor("pb_u3b", [128, TH], F32))
            pb_dg = [ph.enter_context(nc.psum_tensor(f"pb_d{k}", [128, TH], F32))
                     for k in range(4)]
            w1sf = ph.enter_context(nc.sbuf_tensor("w1sf", [128, D], F32))
            w3sf = ph.enter_context(nc.sbuf_tensor("w3sf", [128, D], F32))
            w1s = ph.enter_context(nc.sbuf_tensor("w1s", [128, 2 * D], BF16))
            w3s = ph.enter_context(nc.sbuf_tensor("w3s", [128, 2 * D], BF16))
            w2sf = ph.enter_context(nc.sbuf_tensor("w2sf", [128, 3 * TH], F32))
            w2s = ph.enter_context(nc.sbuf_tensor("w2s", [128, 3 * TH], BF16))
            s1_scr = ph.enter_context(nc.sbuf_tensor("s1_scr", [128, TH], F32))
            g_s = ph.enter_context(nc.sbuf_tensor("g_s", [128, HJ * 2 * TH], BF16))
            pb_u1 = [pb_u1a, pb_u1b]
            pb_u3 = [pb_u3a, pb_u3b]

            pe_hist = []
            g_hist = []
            vc_prev = {1: 0, 3: 0}
            for i in range(HJ):
                b = i % 2
                p.wait("sync", sDVE, vc_prev[1])
                p.wait("sync", sPOOL, vc_prev[3])
                for hf in range(2):
                    srcw = tile3(sw1T.ap()[hf * 1024:(hf + 1) * 1024,
                                           i * 128:(i + 1) * 128], 128)
                    p.dma("sync", sb3(w1sf[:, hf * 1024:(hf + 1) * 1024], 128), srcw, sW)
                for hf in range(2):
                    srcw = tile3(sw3T.ap()[hf * 1024:(hf + 1) * 1024,
                                           i * 128:(i + 1) * 128], 128)
                    p.dma("sync", sb3(w3sf[:, hf * 1024:(hf + 1) * 1024], 128), srcw, sW)
                vwf = p.val(sW)
                p.wait("vector", sW, vwf)
                p.wait("gpsimd", sW, vwf)
                if len(pe_hist) >= 2:
                    p.wait("vector", sPE, pe_hist[-2])
                    p.wait("gpsimd", sPE, pe_hist[-2])
                vc1 = p.op("vector", "tensor_copy", w1s[:, b * D:(b + 1) * D],
                           w1sf[:, :], inc=(sDVE, 1))
                vc3 = p.op("gpsimd", "tensor_copy", w3s[:, b * D:(b + 1) * D],
                           w3sf[:, :], inc=(sPOOL, 1))
                vc_prev = {1: vc1, 3: vc3}
                p.wait("tensor", sDVE, vc1)
                p.wait("tensor", sPOOL, vc3)
                if i >= 2:
                    p.wait("tensor", sDVE, g_hist[2 * (i - 2) + 1])
                for hh in range(2):
                    for j in range(DJ):
                        p.op("tensor", "matmul", pb_u1[b][:, :],
                             w1s[:, b * D + j * 128: b * D + (j + 1) * 128],
                             xnbf[:, j * T + hh * TH: j * T + (hh + 1) * TH],
                             start=(j == 0), stop=(j == DJ - 1))
                    for j in range(DJ):
                        p.op("tensor", "matmul", pb_u3[b][:, :],
                             w3s[:, b * D + j * 128: b * D + (j + 1) * 128],
                             xnbf[:, j * T + hh * TH: j * T + (hh + 1) * TH],
                             start=(j == 0), stop=(j == DJ - 1),
                             inc=(sPE, 1) if j == DJ - 1 else None)
                    vpe_i = p.val(sPE)
                    p.wait("scalar", sPE, vpe_i)
                    if g_hist:
                        p.wait("scalar", sDVE, g_hist[-1])
                    vact = p.op("scalar", "activation", s1_scr[:, :], pb_u1[b][:, :],
                                AF.Silu, inc=(sACT, 1))
                    p.wait("vector", sACT, vact)
                    g_hist.append(p.op("vector", "tensor_mul",
                                       g_s[:, (2 * i + hh) * TH:(2 * i + hh + 1) * TH],
                                       s1_scr[:, :], pb_u3[b][:, :], inc=(sDVE, 1)))
                    if hh == 0:
                        p.wait("tensor", sDVE, g_hist[-1])
                pe_hist.append(vpe_i)

            w2hist = []
            for dg in range(4):
                for j in range(HJ):
                    b2 = (dg * HJ + j) % 3
                    if len(w2hist) >= 3:
                        p.wait("sync", sDVE, w2hist[-3][0])
                    srcw = sw2T.ap()[j * 128:(j + 1) * 128, dg * TH:(dg + 1) * TH]
                    vw2 = p.dma("sync", w2sf[:, b2 * TH:(b2 + 1) * TH], srcw, sW)
                    p.wait("vector", sW, vw2)
                    if len(w2hist) >= 3:
                        p.wait("vector", sPE, w2hist[-3][1])
                    vcw2 = p.op("vector", "tensor_copy", w2s[:, b2 * TH:(b2 + 1) * TH],
                                w2sf[:, b2 * TH:(b2 + 1) * TH], inc=(sDVE, 1))
                    p.wait("tensor", sDVE, vcw2)
                    if dg == 0 and j == 0:
                        p.wait("tensor", sDVE, g_hist[-1])
                    for isub in range(4):
                        for hh in range(2):
                            bank = pb_dg[isub] if hh == 0 else pb_u1[isub % 2] \
                                if False else None
                    for isub in range(4):
                        for hh in range(2):
                            bank = (pb_dg + [pb_u1a, pb_u3a, pb_u1b, pb_u3b])[isub * 2 + hh]
                            p.op("tensor", "matmul", bank[:, :],
                                 w2s[:, b2 * TH + isub * 128: b2 * TH + (isub + 1) * 128],
                                 g_s[:, (2 * j + hh) * TH:(2 * j + hh + 1) * TH],
                                 start=(j == 0), stop=(j == HJ - 1),
                                 inc=(sPE, 1) if (isub, hh) == (3, 1) else None)
                    w2hist.append((vcw2, p.val(sPE)))
                vpe_dg = p.val(sPE)
                p.wait("vector", sPE, vpe_dg)
                for isub in range(4):
                    for hh in range(2):
                        di = dg * 4 + isub
                        bank = (pb_dg + [pb_u1a, pb_u3a, pb_u1b, pb_u3b])[isub * 2 + hh]
                        p.op("vector", "tensor_add", chunk(resid, di, hh),
                             chunk(resid, di, hh), bank[:, :],
                             inc=(sDVE, 1) if (isub, hh) == (3, 1) else None)
                p.wait("tensor", sDVE, p.val(sDVE))
            barrier()

            dbg_big("dres3", resid)

        with ExitStack() as ph:
            pa_h0 = ph.enter_context(nc.psum_tensor("pa_h0", [128, T], F32))
            pa_g0 = ph.enter_context(nc.psum_tensor("pa_g0", [128, T], F32))
            pa_h1 = ph.enter_context(nc.psum_tensor("pa_h1", [128, T], F32))
            pa_g1 = ph.enter_context(nc.psum_tensor("pa_g1", [128, T], F32))
            whf = ph.enter_context(nc.sbuf_tensor("whf", [128, D], F32))
            wgf = ph.enter_context(nc.sbuf_tensor("wgf", [128, D], F32))
            whbuf = ph.enter_context(nc.sbuf_tensor("whbuf", [128, 2 * D], BF16))
            wgbuf = ph.enter_context(nc.sbuf_tensor("wgbuf", [128, 2 * D], BF16))
            zbuf = ph.enter_context(nc.sbuf_tensor("zbuf", [128, T], F32))
            gbuf = ph.enter_context(nc.sbuf_tensor("gbuf", [128, T], F32))
            abuf = ph.enter_context(nc.sbuf_tensor("abuf", [128, T], F32))
            vbuf = ph.enter_context(nc.sbuf_tensor("vbuf", [128, T], F32))
            hscr = ph.enter_context(nc.sbuf_tensor("hscr", [128, T], F32))
            gscr = ph.enter_context(nc.sbuf_tensor("gscr", [128, T], F32))

            pa_h = [pa_h0, pa_h1]
            pa_g = [pa_g0, pa_g1]
            pe_c = []
            act_c = []
            dve_c = []
            pool_c = []
            vc_prev = {1: 0, 3: 0}
            for c in range(CI):
                b = c % 2
                p.wait("sync", sDVE, vc_prev[1])
                p.wait("sync", sPOOL, vc_prev[3])
                for hf in range(2):
                    srcw = tile3(whg.ap()[hf * 1024:(hf + 1) * 1024,
                                          c * 128:(c + 1) * 128], 128)
                    p.dma("sync", sb3(whf[:, hf * 1024:(hf + 1) * 1024], 128), srcw, sW)
                for hf in range(2):
                    srcw = tile3(whg.ap()[hf * 1024:(hf + 1) * 1024,
                                          DI + c * 128: DI + (c + 1) * 128], 128)
                    p.dma("sync", sb3(wgf[:, hf * 1024:(hf + 1) * 1024], 128), srcw, sW)
                vwf = p.val(sW)
                p.wait("vector", sW, vwf)
                p.wait("gpsimd", sW, vwf)
                if len(pe_c) >= 2:
                    p.wait("vector", sPE, pe_c[-2])
                    p.wait("gpsimd", sPE, pe_c[-2])
                vc1 = p.op("vector", "tensor_copy", whbuf[:, b * D:(b + 1) * D],
                           whf[:, :], inc=(sDVE, 1))
                vc3 = p.op("gpsimd", "tensor_copy", wgbuf[:, b * D:(b + 1) * D],
                           wgf[:, :], inc=(sPOOL, 1))
                vc_prev = {1: vc1, 3: vc3}

                p.wait("tensor", sDVE, vc1)
                p.wait("tensor", sPOOL, vc3)
                if c >= 2:
                    p.wait("tensor", sDVE, dve_c[c - 2])
                    p.wait("tensor", sACT, act_c[c - 2])
                for h in range(2):
                    for j in range(DJ):
                        p.op("tensor", "matmul", pa_h[b][:, h * TH:(h + 1) * TH],
                             whbuf[:, b * D + j * 128: b * D + (j + 1) * 128],
                             xnbf[:, j * T + h * TH: j * T + (h + 1) * TH],
                             start=(j == 0), stop=(j == DJ - 1))
                for h in range(2):
                    for j in range(DJ):
                        p.op("tensor", "matmul", pa_g[b][:, h * TH:(h + 1) * TH],
                             wgbuf[:, b * D + j * 128: b * D + (j + 1) * 128],
                             xnbf[:, j * T + h * TH: j * T + (h + 1) * TH],
                             start=(j == 0), stop=(j == DJ - 1),
                             inc=(sPE, 1) if (h == 1 and j == DJ - 1) else None)
                pe_c.append(p.val(sPE))

                p.wait("scalar", sPE, pe_c[c])
                if c >= 1:
                    p.wait("scalar", sDVE, dve_c[c - 1])
                    p.wait("scalar", sPOOL, pool_c[c - 1])
                p.op("scalar", "activation", zbuf[:, :], pa_g[b][:, :], AF.Sigmoid)
                act_c.append(p.op("scalar", "activation", gbuf[:, :], pa_h[b][:, :],
                                  AF.Sigmoid, inc=(sACT, 1)))
                p.wait("gpsimd", sACT, act_c[c])
                if c >= 1:
                    p.wait("gpsimd", sDVE, dve_c[c - 1])
                pool_c.append(p.op("gpsimd", "tensor_scalar", abuf[:, :], zbuf[:, :],
                                   -1.0, 1.0, ALU.mult, ALU.add, inc=(sPOOL, 1)))
                p.wait("vector", sACT, act_c[c])
                ve = p.op("vector", "scalar_tensor_tensor", gbuf[:, :], pa_h[b][:, :],
                          0.5, gbuf[:, :], ALU.add, ALU.max, inc=(sDVE, 1))
                p.wait("vector", sDVE, ve)
                ve = p.op("vector", "tensor_mul", vbuf[:, :], zbuf[:, :], gbuf[:, :],
                          inc=(sDVE, 1))
                p.wait("vector", sDVE, ve)
                p.wait("vector", sPOOL, pool_c[c])
                ve = p.op("vector", "tensor_tensor_scan", hscr[:, :], abuf[:, :],
                          vbuf[:, :], 0.0, ALU.mult, ALU.add, inc=(sDVE, 1))
                p.wait("vector", sDVE, ve)
                vsc = p.op("vector", "tensor_tensor_scan", gscr[:, :], abuf[:, :],
                           zeros[:, :], 1.0, ALU.mult, ALU.add, inc=(sDVE, 1))
                p.wait("vector", sDVE, vsc)
                p.op("vector", "tensor_copy", aggO[:, c:c + 1], gscr[:, T - 1:T])
                dve_c.append(p.op("vector", "tensor_copy", aggO[:, CI + c: CI + c + 1],
                                  hscr[:, T - 1:T], inc=(sDVE, 1)))

            vdve_agg = dve_c[-1]
            p.wait("gpsimd", sDVE, vdve_agg)
            vG_aggout = p.dma("gpsimd", cc_in.ap(), aggO[:, :], sG)
            p.wait("gpsimd", sG, vG_aggout)
            p.op("gpsimd", "collective_compute", "AllGather", ALU.bypass,
                 replica_groups=[list(range(NCORES))],
                 ins=[cc_in.ap()], outs=[cc_out.ap()], inc=(sCC, 1))
            vCC = p.val(sCC)
            barrier()
        zstack.close()

        with ExitStack() as ph:
            pu = [ph.enter_context(nc.psum_tensor(f"pe_u{k}", [128, TH], F32))
                  for k in range(8)]
            g_e = ph.enter_context(nc.sbuf_tensor("g_e", [128, HJ * 2 * TH], BF16))
            w1e = ph.enter_context(nc.sbuf_tensor("w1e", [128, 2 * D], BF16))
            w3e = ph.enter_context(nc.sbuf_tensor("w3e", [128, 2 * D], BF16))
            w1f = ph.enter_context(nc.sbuf_tensor("w1f", [128, D], F32))
            w3f = ph.enter_context(nc.sbuf_tensor("w3f", [128, D], F32))
            w2f = ph.enter_context(nc.sbuf_tensor("w2f", [128, 3 * TH], F32))
            w2e = ph.enter_context(nc.sbuf_tensor("w2e", [128, 3 * TH], BF16))
            bcast = ph.enter_context(nc.sbuf_tensor("bcast", [128, T], F32))
            s1e = ph.enter_context(nc.sbuf_tensor("s1e", [128, TH], F32))
            t1b = ph.enter_context(nc.sbuf_tensor("t1b", [128, TH], F32))
            t3b = ph.enter_context(nc.sbuf_tensor("t3b", [128, TH], F32))

            def emit_bcast(e, wait_dve=None):
                for hh in range(2):
                    p.op("tensor", "matmul", pu[4 + hh][:, :],
                         sel8[:, e * 128:(e + 1) * 128],
                         seT[:, hh * TH:(hh + 1) * TH], start=True, stop=True,
                         inc=(sPE, 1) if hh == 1 else None)
                vpe_bc = p.val(sPE)
                p.wait("scalar", sPE, vpe_bc)
                if wait_dve is not None:
                    p.wait("scalar", sDVE, wait_dve)
                for hh in range(2):
                    p.op("scalar", "copy", bcast[:, hh * TH:(hh + 1) * TH],
                         pu[4 + hh][:, :], inc=(sACT, 1) if hh == 1 else None)
                return p.val(sACT)

            vact_bc = emit_bcast(0)

            last_g = 0
            prev_set_g = {}
            vc_prev = {1: 0, 3: 0}
            hist_pe_i = []
            for e in range(E):
                for i in range(HJ):
                    b = (e * HJ + i) % 2
                    st = b * 4
                    p.wait("sync", sDVE, vc_prev[1])
                    p.wait("sync", sPOOL, vc_prev[3])
                    for hf in range(2):
                        srcw = tile3(w1Tf.ap()[e * D + hf * 1024: e * D + (hf + 1) * 1024,
                                               i * 128:(i + 1) * 128], 128)
                        p.dma("sync", sb3(w1f[:, hf * 1024:(hf + 1) * 1024], 128), srcw, sW)
                    for hf in range(2):
                        srcw = tile3(w3Tf.ap()[e * D + hf * 1024: e * D + (hf + 1) * 1024,
                                               i * 128:(i + 1) * 128], 128)
                        p.dma("sync", sb3(w3f[:, hf * 1024:(hf + 1) * 1024], 128), srcw, sW)
                    vwf = p.val(sW)
                    p.wait("vector", sW, vwf)
                    p.wait("gpsimd", sW, vwf)
                    if len(hist_pe_i) >= 2:
                        p.wait("vector", sPE, hist_pe_i[-2])
                        p.wait("gpsimd", sPE, hist_pe_i[-2])
                    vc1 = p.op("vector", "tensor_copy", w1e[:, b * D:(b + 1) * D],
                               w1f[:, :], inc=(sDVE, 1))
                    vc3 = p.op("gpsimd", "tensor_copy", w3e[:, b * D:(b + 1) * D],
                               w3f[:, :], inc=(sPOOL, 1))
                    vc_prev = {1: vc1, 3: vc3}
                    p.wait("tensor", sDVE, vc1)
                    p.wait("tensor", sPOOL, vc3)
                    k = e * HJ + i
                    if k >= 2:
                        p.wait("tensor", sDVE, prev_set_g[k - 2])
                    if k <= 1:
                        p.wait("tensor", sACT, vact_bc)
                    for ui, wsb in ((0, w1e), (1, w3e)):
                        for hh in range(2):
                            bank = pu[st + 2 * ui + hh]
                            for j in range(DJ):
                                p.op("tensor", "matmul", bank[:, :],
                                     wsb[:, b * D + j * 128: b * D + (j + 1) * 128],
                                     xnbf[:, j * T + hh * TH: j * T + (hh + 1) * TH],
                                     start=(j == 0), stop=(j == DJ - 1),
                                     inc=(sPE, 1) if (ui, hh, j) == (1, 1, DJ - 1) else None)
                    hist_pe_i.append(p.val(sPE))
                    p.wait("vector", sPE, hist_pe_i[-1])
                    if k == 0:
                        p.wait("vector", sACT, vact_bc)
                    for hh in range(2):
                        u1 = pu[st + hh]
                        u3 = pu[st + 2 + hh]
                        bsl = bcast[:, hh * TH:(hh + 1) * TH]
                        vt1 = p.op("vector", "tensor_mul", t1b[:, :], u1[:, :], bsl,
                                   inc=(sDVE, 1))
                        p.wait("scalar", sDVE, vt1)
                        va = p.op("scalar", "activation", s1e[:, :], t1b[:, :], AF.Silu,
                                  inc=(sACT, 1))
                        vt3 = p.op("vector", "tensor_tensor", t3b[:, :], u3[:, :], bsl,
                                   ALU.mult, inc=(sDVE, 1))
                        p.wait("vector", sDVE, vt3)
                        p.wait("vector", sACT, va)
                        vg = p.op("vector", "tensor_mul",
                                  g_e[:, (2 * i + hh) * TH:(2 * i + hh + 1) * TH],
                                  s1e[:, :], t3b[:, :], inc=(sDVE, 1))
                        p.wait("vector", sACT, va)
                    prev_set_g[k] = p.val(sDVE)
                last_g = p.val(sDVE)

                if e + 1 < E:
                    p.wait("tensor", sDVE, last_g)
                    vact_bc = emit_bcast(e + 1, wait_dve=last_g)

                w2hist = []
                for dg in range(4):
                    for j in range(HJ):
                        b2 = (dg * HJ + j) % 3
                        if len(w2hist) >= 3:
                            p.wait("sync", sDVE, w2hist[-3][0])
                        srcw = w2Tf.ap()[e * HID + j * 128: e * HID + (j + 1) * 128,
                                         dg * TH:(dg + 1) * TH]
                        vw2 = p.dma("sync", w2f[:, b2 * TH:(b2 + 1) * TH], srcw, sW)
                        p.wait("vector", sW, vw2)
                        if len(w2hist) >= 3:
                            p.wait("vector", sPE, w2hist[-3][1])
                        vcw2 = p.op("vector", "tensor_copy",
                                    w2e[:, b2 * TH:(b2 + 1) * TH],
                                    w2f[:, b2 * TH:(b2 + 1) * TH], inc=(sDVE, 1))
                        p.wait("tensor", sDVE, vcw2)
                        if dg == 0 and j == 0:
                            p.wait("tensor", sDVE, last_g)
                            p.wait("tensor", sACT, p.val(sACT))
                        for isub in range(4):
                            for hh in range(2):
                                bank = pu[isub * 2 + hh]
                                p.op("tensor", "matmul", bank[:, :],
                                     w2e[:, b2 * TH + isub * 128: b2 * TH + (isub + 1) * 128],
                                     g_e[:, (2 * j + hh) * TH:(2 * j + hh + 1) * TH],
                                     start=(j == 0), stop=(j == HJ - 1),
                                     inc=(sPE, 1) if (isub, hh) == (3, 1) else None)
                        w2hist.append((vcw2, p.val(sPE)))
                    vpe_dg = p.val(sPE)
                    p.wait("vector", sPE, vpe_dg)
                    for isub in range(4):
                        for hh in range(2):
                            di = dg * 4 + isub
                            p.op("vector", "tensor_add", chunk(resid, di, hh),
                                 chunk(resid, di, hh), pu[isub * 2 + hh][:, :],
                                 inc=(sDVE, 1) if (isub, hh) == (3, 1) else None)
                    p.wait("tensor", sDVE, p.val(sDVE))
                prev_set_g[e * HJ + HJ - 1] = p.val(sDVE)
                prev_set_g[e * HJ + HJ - 2] = p.val(sDVE)
            barrier()

        dbg_big("dres4", resid)

        GRP = 12
        with ExitStack() as ph:
            pbh0 = ph.enter_context(nc.psum_tensor("pbh0", [128, TH], F32))
            pbg0 = ph.enter_context(nc.psum_tensor("pbg0", [128, TH], F32))
            pbh1 = ph.enter_context(nc.psum_tensor("pbh1", [128, TH], F32))
            pbg1 = ph.enter_context(nc.psum_tensor("pbg1", [128, TH], F32))
            po = [ph.enter_context(nc.psum_tensor(f"po{k}", [128, TH], F32))
                  for k in range(4)]
            whfB = ph.enter_context(nc.sbuf_tensor("whfB", [128, D], F32))
            wgfB = ph.enter_context(nc.sbuf_tensor("wgfB", [128, D], F32))
            whbufB = ph.enter_context(nc.sbuf_tensor("whbufB", [128, 2 * D], BF16))
            wgbufB = ph.enter_context(nc.sbuf_tensor("wgbufB", [128, 2 * D], BF16))
            woutf = ph.enter_context(nc.sbuf_tensor("woutf", [128, 3 * TH], F32))
            woutb = ph.enter_context(nc.sbuf_tensor("woutb", [128, 3 * TH], BF16))
            zbufB = ph.enter_context(nc.sbuf_tensor("zbufB", [128, TH], F32))
            gbufB = ph.enter_context(nc.sbuf_tensor("gbufB", [128, TH], F32))
            abufB = ph.enter_context(nc.sbuf_tensor("abufB", [128, TH], F32))
            vbufB = ph.enter_context(nc.sbuf_tensor("vbufB", [128, TH], F32))
            hgrp = ph.enter_context(nc.sbuf_tensor("hgrp", [128, GRP * TH], BF16))
            maskM = ph.enter_context(nc.sbuf_tensor("maskM", [128, NCORES * 48], F32))
            maskB = ph.enter_context(nc.sbuf_tensor("maskB", [128, NCORES * 48], F32))
            gathr = ph.enter_context(nc.sbuf_tensor("gathr", [128, NCORES * 48], F32))
            gmr = ph.enter_context(nc.sbuf_tensor("gmr", [128, 48], F32))
            ctmp = ph.enter_context(nc.sbuf_tensor("ctmp", [128, CI], F32))

            p.dma("gpsimd", sb3(maskM[:, :], 48), tile3(maskMd.ap()[:, :], 48), sG)
            p.dma("gpsimd", sb3(maskB[:, :], 48), tile3(maskBd.ap()[:, :], 48), sG)
            p.wait("gpsimd", sCC, vCC)
            for r in range(NCORES):
                p.dma("gpsimd", gathr[:, r * 48:(r + 1) * 48],
                      cc_out.ap()[r * 128:(r + 1) * 128, :], sG)
            vG_gath = p.val(sG)

            p.wait("vector", sG, vG_gath)
            for r in range(NCORES):
                gr = gathr[:, r * 48:(r + 1) * 48]
                ve = p.op("vector", "tensor_mul", gmr[:, :], gr,
                          maskM[:, r * 48:(r + 1) * 48], inc=(sDVE, 1))
                p.wait("vector", sDVE, ve)
                ve = p.op("vector", "tensor_add", gmr[:, :], gmr[:, :],
                          maskB[:, r * 48:(r + 1) * 48], inc=(sDVE, 1))
                p.wait("vector", sDVE, ve)
                ve = p.op("vector", "tensor_mul", ctmp[:, :], gmr[:, 0:CI], carry[:, :],
                          inc=(sDVE, 1))
                p.wait("vector", sDVE, ve)
                ve = p.op("vector", "tensor_add", carry[:, :], ctmp[:, :],
                          gmr[:, CI:2 * CI], inc=(sDVE, 1))
                p.wait("vector", sDVE, ve)
            vDVE_carry = p.val(sDVE)

            if DEBUG:
                p.wait("gpsimd", sDVE, vDVE_carry)
            dbg_small("dcarry", carry[:, :])
            dbg_small("dagg", aggO[:, :])
            dbg_small("dgath", gathr[:, :])
            dbg_small("dmskM", maskM[:, :])

            pb_hB = [pbh0, pbh1]
            pb_gB = [pbg0, pbg1]
            pe_cB = {}
            act_cB = {}
            dve_cB = {}
            pool_cB = {}
            vDVE_hendA = None
            seq = [(h, c) for h in range(2) for c in range(CI)]
            kidx = {hc: k for k, hc in enumerate(seq)}
            vc_prev = {1: 0, 3: 0}

            for h in range(2):
                for grp in range(2):
                    chunks = list(range(grp * GRP, (grp + 1) * GRP))
                    for ci_, c in enumerate(chunks):
                        k = kidx[(h, c)]
                        b = k % 2
                        p.wait("sync", sDVE, vc_prev[1])
                        p.wait("sync", sPOOL, vc_prev[3])
                        for hf in range(2):
                            srcw = tile3(whg.ap()[hf * 1024:(hf + 1) * 1024,
                                                  c * 128:(c + 1) * 128], 128)
                            p.dma("sync", sb3(whfB[:, hf * 1024:(hf + 1) * 1024], 128),
                                  srcw, sW)
                        for hf in range(2):
                            srcw = tile3(whg.ap()[hf * 1024:(hf + 1) * 1024,
                                                  DI + c * 128: DI + (c + 1) * 128], 128)
                            p.dma("sync", sb3(wgfB[:, hf * 1024:(hf + 1) * 1024], 128),
                                  srcw, sW)
                        vwf = p.val(sW)
                        p.wait("vector", sW, vwf)
                        p.wait("gpsimd", sW, vwf)
                        if k >= 2:
                            p.wait("vector", sPE, pe_cB[seq[k - 2]])
                            p.wait("gpsimd", sPE, pe_cB[seq[k - 2]])
                        vc1 = p.op("vector", "tensor_copy", whbufB[:, b * D:(b + 1) * D],
                                   whfB[:, :], inc=(sDVE, 1))
                        vc3 = p.op("gpsimd", "tensor_copy", wgbufB[:, b * D:(b + 1) * D],
                                   wgfB[:, :], inc=(sPOOL, 1))
                        vc_prev = {1: vc1, 3: vc3}

                        p.wait("tensor", sDVE, vc1)
                        p.wait("tensor", sPOOL, vc3)
                        if k >= 2:
                            p.wait("tensor", sDVE, dve_cB[seq[k - 2]])
                            p.wait("tensor", sACT, act_cB[seq[k - 2]])
                        for j in range(DJ):
                            p.op("tensor", "matmul", pb_hB[b][:, :],
                                 whbufB[:, b * D + j * 128: b * D + (j + 1) * 128],
                                 xnbf[:, j * T + h * TH: j * T + (h + 1) * TH],
                                 start=(j == 0), stop=(j == DJ - 1))
                        for j in range(DJ):
                            p.op("tensor", "matmul", pb_gB[b][:, :],
                                 wgbufB[:, b * D + j * 128: b * D + (j + 1) * 128],
                                 xnbf[:, j * T + h * TH: j * T + (h + 1) * TH],
                                 start=(j == 0), stop=(j == DJ - 1),
                                 inc=(sPE, 1) if j == DJ - 1 else None)
                        pe_cB[(h, c)] = p.val(sPE)

                        p.wait("scalar", sPE, pe_cB[(h, c)])
                        if k >= 1:
                            p.wait("scalar", sDVE, dve_cB[seq[k - 1]])
                            p.wait("scalar", sPOOL, pool_cB[seq[k - 1]])
                        p.op("scalar", "activation", zbufB[:, :], pb_gB[b][:, :],
                             AF.Sigmoid)
                        act_cB[(h, c)] = p.op("scalar", "activation", gbufB[:, :],
                                              pb_hB[b][:, :], AF.Sigmoid, inc=(sACT, 1))
                        p.wait("gpsimd", sACT, act_cB[(h, c)])
                        if k >= 1:
                            p.wait("gpsimd", sDVE, dve_cB[seq[k - 1]])
                        pool_cB[(h, c)] = p.op("gpsimd", "tensor_scalar", abufB[:, :],
                                               zbufB[:, :], -1.0, 1.0, ALU.mult, ALU.add,
                                               inc=(sPOOL, 1))
                        p.wait("vector", sACT, act_cB[(h, c)])
                        ve = p.op("vector", "scalar_tensor_tensor", gbufB[:, :],
                                  pb_hB[b][:, :], 0.5, gbufB[:, :], ALU.add, ALU.max,
                                  inc=(sDVE, 1))
                        p.wait("vector", sDVE, ve)
                        ve = p.op("vector", "tensor_mul", vbufB[:, :], zbufB[:, :],
                                  gbufB[:, :], inc=(sDVE, 1))
                        p.wait("vector", sDVE, ve)
                        p.wait("vector", sPOOL, pool_cB[(h, c)])
                        if h == 0 and c == 0:
                            p.wait("vector", sDVE, vDVE_carry)
                        if h == 1 and c == 0:
                            p.wait("vector", sDVE, vDVE_hendA)
                        init = carry[:, c:c + 1] if h == 0 else hendA[:, c:c + 1]
                        vscB = p.op("vector", "tensor_tensor_scan",
                                    hgrp[:, ci_ * TH:(ci_ + 1) * TH], abufB[:, :],
                                    vbufB[:, :], init, ALU.mult, ALU.add, inc=(sDVE, 1))
                        if h == 0:
                            p.wait("vector", sDVE, vscB)
                            dve_cB[(h, c)] = p.op("vector", "tensor_copy",
                                                  hendA[:, c:c + 1],
                                                  hgrp[:, (ci_ + 1) * TH - 1:(ci_ + 1) * TH],
                                                  inc=(sDVE, 1))
                        else:
                            dve_cB[(h, c)] = vscB

                    if h == 0 and grp == 1:
                        vDVE_hendA = dve_cB[(0, CI - 1)]

                    vgrp_h = dve_cB[(h, chunks[-1])]
                    if DEBUG and h == 0 and grp == 0:
                        p.wait("gpsimd", sDVE, vgrp_h)
                        p.dma("gpsimd", dbg["dhg"].ap()[:, :],
                              hgrp[:, :].bitcast(F32) if False else hgrp[:, :], sG)
                    wo_hist = []
                    for dset in range(4):
                        for ci_, c in enumerate(chunks):
                            b2 = (dset * GRP + ci_) % 3
                            if len(wo_hist) >= 3:
                                p.wait("scalar", sPOOL, wo_hist[-3][0])
                            srcw = wout.ap()[c * 128:(c + 1) * 128,
                                             dset * TH:(dset + 1) * TH]
                            vwo = p.dma("scalar", woutf[:, b2 * TH:(b2 + 1) * TH],
                                        srcw, sX)
                            p.wait("gpsimd", sX, vwo)
                            if len(wo_hist) >= 3:
                                p.wait("gpsimd", sPE, wo_hist[-3][1])
                            vco = p.op("gpsimd", "tensor_copy",
                                       woutb[:, b2 * TH:(b2 + 1) * TH],
                                       woutf[:, b2 * TH:(b2 + 1) * TH], inc=(sPOOL, 1))
                            p.wait("tensor", sPOOL, vco)
                            if dset == 0 and ci_ == 0:
                                p.wait("tensor", sDVE, vgrp_h)
                            for isub in range(4):
                                p.op("tensor", "matmul", po[isub][:, :],
                                     woutb[:, b2 * TH + isub * 128: b2 * TH + (isub + 1) * 128],
                                     hgrp[:, ci_ * TH:(ci_ + 1) * TH],
                                     start=(ci_ == 0), stop=(ci_ == GRP - 1),
                                     inc=(sPE, 1) if isub == 3 else None)
                            wo_hist.append((vco, p.val(sPE)))
                        p.wait("vector", sPE, p.val(sPE))
                        for isub in range(4):
                            di = dset * 4 + isub
                            p.op("vector", "tensor_add", chunk(resid, di, h),
                                 chunk(resid, di, h), po[isub][:, :],
                                 inc=(sDVE, 1) if isub == 3 else None)
                        p.wait("tensor", sDVE, p.val(sDVE))
            barrier()

            dbg_big("dres5", resid)

        for gi in range(4):
            p.dma("scalar", tile3(yT.ap()[gi * 512:(gi + 1) * 512, :], T),
                  sb3(resid[:, gi * 4 * T:(gi + 1) * 4 * T], T), sOUT)
        vOUT = p.val(sOUT)
        p.wait("scalar", sOUT, vOUT)
        p.wait("sync", sOUT, vOUT)
        if DEBUG:
            p.wait("gpsimd", sG, p.val(sG))

        with nc.Block() as block:
            @block.sync
            def _(eng):
                for fn in p.q["sync"]:
                    fn(eng)

            @block.tensor
            def _(eng):
                for fn in p.q["tensor"]:
                    fn(eng)

            @block.vector
            def _(eng):
                for fn in p.q["vector"]:
                    fn(eng)

            @block.scalar
            def _(eng):
                for fn in p.q["scalar"]:
                    fn(eng)

            @block.gpsimd
            def _(eng):
                for fn in p.q["gpsimd"]:
                    fn(eng)

    return nc


def _host_stage(inputs):
    x = np.asarray(inputs["L0_attn_x"], np.float32)
    xT = np.ascontiguousarray(x[0].T)
    whg = np.ascontiguousarray(np.asarray(inputs["L0_attn_w_hg"], np.float32))
    wout = np.ascontiguousarray(np.asarray(inputs["L0_attn_w_out"], np.float32))
    gwT = np.ascontiguousarray(np.asarray(inputs["L0_ffn_gate_w"], np.float32).T)
    w1Tf = np.ascontiguousarray(
        np.swapaxes(np.asarray(inputs["L0_ffn_w1"], np.float32), 1, 2).reshape(E * D, HID))
    w3Tf = np.ascontiguousarray(
        np.swapaxes(np.asarray(inputs["L0_ffn_w3"], np.float32), 1, 2).reshape(E * D, HID))
    w2Tf = np.ascontiguousarray(
        np.swapaxes(np.asarray(inputs["L0_ffn_w2"], np.float32), 1, 2).reshape(E * HID, D))
    sw1T = np.ascontiguousarray(np.asarray(inputs["L0_ffn_sw1"], np.float32).T)
    sw3T = np.ascontiguousarray(np.asarray(inputs["L0_ffn_sw3"], np.float32).T)
    sw2T = np.ascontiguousarray(np.asarray(inputs["L0_ffn_sw2"], np.float32).T)
    biasd = np.ascontiguousarray(
        np.asarray(inputs["L0_ffn_expert_bias"], np.float32).reshape(1, E))
    onesd = np.ones((128, 128), np.float32)
    identd = np.eye(128, dtype=np.float32)
    sel8d = np.zeros((E, E * 128), np.float32)
    for e in range(E):
        sel8d[e, e * 128:(e + 1) * 128] = 1.0
    shared = dict(whg=whg, wout=wout, gwT=gwT, w1Tf=w1Tf, w3Tf=w3Tf, w2Tf=w2Tf,
                  sw1T=sw1T, sw3T=sw3T, sw2T=sw2T, biasd=biasd, onesd=onesd,
                  identd=identd, sel8d=sel8d)
    in_maps = []
    for c in range(NCORES):
        mM = np.zeros((NCORES * 128, 48), np.float32)
        mB = np.zeros((NCORES * 128, 48), np.float32)
        for r in range(NCORES):
            if r < c:
                mM[r * 128:(r + 1) * 128, :] = 1.0
            else:
                mB[r * 128:(r + 1) * 128, 0:CI] = 1.0
        m = dict(shared)
        m["xT"] = np.ascontiguousarray(xT[:, c * T:(c + 1) * T])
        m["maskMd"] = mM
        m["maskBd"] = mB
        in_maps.append(m)
    return in_maps


def kernel(**inputs):
    in_maps = _host_stage(inputs)
    nc = build_nc()
    res = run_bass_kernel_spmd(nc, in_maps, core_ids=list(range(NCORES)))
    yT = np.concatenate([res.results[c]["yT"] for c in range(NCORES)], axis=1)
    y = np.ascontiguousarray(yT.T).reshape(1, S, D).astype(np.float32)
    return y


# version 06
# speedup vs baseline: 2.0945x; 1.0899x over previous
import numpy as np
from contextlib import ExitStack

import concourse.bass as bass
import concourse.mybir as mybir
from concourse.bass_utils import run_bass_kernel_spmd

F32 = mybir.dt.float32
BF16 = mybir.dt.bfloat16
AF = mybir.ActivationFunctionType
ALU = mybir.AluOpType
AX = mybir.AxisListType

NCORES = 8
D = 2048
DI = 3072
HID = 1408
E = 8
S = 8192
T = S // NCORES
TH = T // 2
DJ = D // 128
CI = DI // 128
HJ = HID // 128

ENGS = ("sync", "tensor", "vector", "scalar", "gpsimd")
DEBUG = False


class Plan:
    def __init__(self):
        self.q = {e: [] for e in ENGS}
        self.cnt = {}
        self.last_cell = {e: None for e in ENGS}
        self.last_inc = {e: None for e in ENGS}

    def inc(self, sem, by=1):
        self.cnt[sem] = self.cnt.get(sem, 0) + by
        return self.cnt[sem]

    def val(self, sem):
        return self.cnt.get(sem, 0)

    def op(self, eng, meth, *args, inc=None, **kw):
        v = None
        if inc is not None:
            sem, n = inc
            v = self.inc(sem, n)
        cell = [inc]

        def run(e, meth=meth, args=args, kw=kw, cell=cell):
            r = getattr(e, meth)(*args, **kw)
            if cell[0] is not None:
                r.then_inc(*cell[0])
        self.q[eng].append(run)
        self.last_cell[eng] = cell
        self.last_inc[eng] = (inc[0], v) if inc is not None else None
        return v

    def milestone(self, eng, default_sem):
        if self.last_cell[eng] is None:
            return None
        if self.last_inc[eng] is None:
            v = self.inc(default_sem, 1)
            self.last_cell[eng][0] = (default_sem, 1)
            self.last_inc[eng] = (default_sem, v)
        return self.last_inc[eng]

    def wait(self, eng, sem, v):
        if v <= 0:
            return
        self.q[eng].append(lambda e, s=sem, vv=v: e.wait_ge(s, vv))

    def dma(self, eng, out_ap, in_ap, sem):
        return self.op(eng, "dma_start", inc=(sem, 16), out=out_ap, in_=in_ap)


def build_nc():
    nc = bass.Bass("TRN2", target_bir_lowering=False)

    xT = nc.dram_tensor("xT", [D, T], F32, kind="ExternalInput")
    whg = nc.dram_tensor("whg", [D, 2 * DI], F32, kind="ExternalInput")
    wout = nc.dram_tensor("wout", [DI, D], F32, kind="ExternalInput")
    gwT = nc.dram_tensor("gwT", [D, E], F32, kind="ExternalInput")
    w1Tf = nc.dram_tensor("w1Tf", [E * D, HID], F32, kind="ExternalInput")
    w3Tf = nc.dram_tensor("w3Tf", [E * D, HID], F32, kind="ExternalInput")
    w2Tf = nc.dram_tensor("w2Tf", [E * HID, D], F32, kind="ExternalInput")
    sw1T = nc.dram_tensor("sw1T", [D, HID], F32, kind="ExternalInput")
    sw3T = nc.dram_tensor("sw3T", [D, HID], F32, kind="ExternalInput")
    sw2T = nc.dram_tensor("sw2T", [HID, D], F32, kind="ExternalInput")
    biasd = nc.dram_tensor("biasd", [1, E], F32, kind="ExternalInput")
    onesd = nc.dram_tensor("onesd", [128, 128], F32, kind="ExternalInput")
    identd = nc.dram_tensor("identd", [128, 128], F32, kind="ExternalInput")
    sel8d = nc.dram_tensor("sel8d", [E, E * 128], F32, kind="ExternalInput")
    maskMd = nc.dram_tensor("maskMd", [NCORES * 128, 48], F32, kind="ExternalInput")
    maskBd = nc.dram_tensor("maskBd", [NCORES * 128, 48], F32, kind="ExternalInput")
    yT = nc.dram_tensor("yT", [D, T], F32, kind="ExternalOutput")
    cc_in = nc.dram_tensor("cc_in", [128, 48], F32)
    cc_out = nc.dram_tensor("cc_out", [NCORES * 128, 48], F32)
    dbg = {}
    if DEBUG:
        for nm in ("dxn", "dres3", "dres4", "dres5"):
            dbg[nm] = nc.dram_tensor(nm, [D, T], F32, kind="ExternalOutput")
        dbg["dch0"] = nc.dram_tensor("dch0", [128, 4 * T], F32, kind="ExternalOutput")
        dbg["dse"] = nc.dram_tensor("dse", [E, T], F32, kind="ExternalOutput")
        dbg["dhg"] = nc.dram_tensor("dhg", [128, 12 * TH], F32, kind="ExternalOutput")
        dbg["dagg"] = nc.dram_tensor("dagg", [128, 2 * CI], F32, kind="ExternalOutput")
        dbg["dcarry"] = nc.dram_tensor("dcarry", [128, CI], F32, kind="ExternalOutput")
        dbg["dgath"] = nc.dram_tensor("dgath", [128, NCORES * 48], F32, kind="ExternalOutput")
        dbg["dmskM"] = nc.dram_tensor("dmskM", [128, NCORES * 48], F32, kind="ExternalOutput")

    p = Plan()
    es = ExitStack()
    with es:
        ec = es.enter_context
        sW = ec(nc.semaphore("sW"))
        sX = ec(nc.semaphore("sX"))
        sG = ec(nc.semaphore("sG"))
        sCC = ec(nc.semaphore("sCC"))
        sPE = ec(nc.semaphore("sPE"))
        sACT = ec(nc.semaphore("sACT"))
        sDVE = ec(nc.semaphore("sDVE"))
        sPOOL = ec(nc.semaphore("sPOOL"))
        sOUT = ec(nc.semaphore("sOUT"))
        sBar = {e: ec(nc.semaphore("sB_" + e)) for e in ENGS}

        resid = ec(nc.sbuf_tensor("resid", [128, DJ * T], F32))
        xnbf = ec(nc.sbuf_tensor("xnbf", [128, DJ * T], BF16))
        seT = ec(nc.sbuf_tensor("seT", [E, T], F32))
        ones = ec(nc.sbuf_tensor("ones", [128, 128], F32))
        sel8 = ec(nc.sbuf_tensor("sel8", [E, E * 128], F32))
        biasb = ec(nc.sbuf_tensor("biasb", [128, E], F32))
        bias1 = ec(nc.sbuf_tensor("bias1", [1, E], F32))
        carry = ec(nc.sbuf_tensor("carry", [128, CI], F32))
        hendA = ec(nc.sbuf_tensor("hendA", [128, CI], F32))
        aggO = ec(nc.sbuf_tensor("aggO", [128, 2 * CI], F32))
        barscr = ec(nc.sbuf_tensor("barscr", [1, 16], F32))
        zstack = ExitStack()
        zeros = zstack.enter_context(nc.sbuf_tensor("zeros", [128, T], F32))
        xnstack = ExitStack()
        xn = xnstack.enter_context(nc.sbuf_tensor("xn", [128, DJ * T], F32))

        dsem = {"sync": sW, "tensor": sPE, "vector": sDVE,
                "scalar": sACT, "gpsimd": sPOOL}

        def barrier():
            marks = {}
            for eng in ENGS:
                m = p.milestone(eng, dsem[eng])
                if m is not None:
                    marks[eng] = m
            for eng in ENGS:
                for oth, (sem, v) in marks.items():
                    if oth != eng:
                        p.wait(eng, sem, v)

        def chunk(t, j, h=None):
            if h is None:
                return t[:, j * T:(j + 1) * T]
            return t[:, j * T + h * TH: j * T + (h + 1) * TH]

        def tile3(dram_slice, inner):
            return dram_slice.rearrange("(j p) x -> p j x", p=128)

        def sb3(sbuf_slice, inner):
            return sbuf_slice.rearrange("p (j x) -> p j x", x=inner)

        def dbg_big(nm, sb):
            if not DEBUG:
                return
            for gi in range(4):
                p.dma("gpsimd", tile3(dbg[nm].ap()[gi * 512:(gi + 1) * 512, :], T),
                      sb3(sb[:, gi * 4 * T:(gi + 1) * 4 * T], T), sG)

        def dbg_small(nm, sb_ap):
            if not DEBUG:
                return
            p.dma("gpsimd", dbg[nm].ap()[:, :], sb_ap, sG)

        p.dma("scalar", ones[:, :], onesd[0:128, :], sX)
        p.dma("scalar", sel8[:, :], sel8d[0:E, :], sX)
        p.dma("scalar", bias1[:, :], biasd[0:1, :], sX)
        for gi in range(4):
            p.dma("scalar", sb3(resid[:, gi * 4 * T:(gi + 1) * 4 * T], T),
                  tile3(xT.ap()[gi * 512:(gi + 1) * 512, :], T), sX)
        vX_pro = p.val(sX)
        p.op("gpsimd", "memset", zeros[:, :], 0.0, inc=(sPOOL, 1))
        p.op("vector", "memset", carry[:, :], 0.0, inc=(sDVE, 1))

        with ExitStack() as ph:
            pb_sum = ph.enter_context(nc.psum_tensor("pb_sum", [1, T], F32))
            pb_sumsq = ph.enter_context(nc.psum_tensor("pb_sumsq", [1, T], F32))
            pb_rstd = ph.enter_context(nc.psum_tensor("pb_rstd", [128, T], F32))
            pb_mrstd = ph.enter_context(nc.psum_tensor("pb_mrstd", [128, T], F32))
            sq_scr = ph.enter_context(nc.sbuf_tensor("sq_scr", [128, 2 * T], F32))
            mean_r = ph.enter_context(nc.sbuf_tensor("mean_r", [1, T], F32))
            var_r = ph.enter_context(nc.sbuf_tensor("var_r", [1, T], F32))
            sd_r = ph.enter_context(nc.sbuf_tensor("sd_r", [1, T], F32))
            rstd_r = ph.enter_context(nc.sbuf_tensor("rstd_r", [1, T], F32))
            mrstd_r = ph.enter_context(nc.sbuf_tensor("mrstd_r", [1, T], F32))

            p.wait("scalar", sX, vX_pro)
            p.wait("tensor", sX, vX_pro)
            act_sq = []
            pe_j = []
            for j in range(DJ):
                if j >= 2:
                    p.wait("scalar", sPE, pe_j[j - 2])
                act_sq.append(p.op("scalar", "square",
                                   sq_scr[:, (j % 2) * T:((j % 2) + 1) * T],
                                   chunk(resid, j), inc=(sACT, 1)))
                p.wait("tensor", sACT, act_sq[j])
                for h in range(2):
                    p.op("tensor", "matmul", pb_sum[:, h * TH:(h + 1) * TH],
                         ones[:, 0:1], chunk(resid, j, h),
                         start=(j == 0), stop=(j == DJ - 1))
                for h in range(2):
                    p.op("tensor", "matmul", pb_sumsq[:, h * TH:(h + 1) * TH],
                         ones[:, 0:1],
                         sq_scr[:, (j % 2) * T + h * TH:(j % 2) * T + (h + 1) * TH],
                         start=(j == 0), stop=(j == DJ - 1),
                         inc=(sPE, 1) if h == 1 else None)
                pe_j.append(p.val(sPE))
            vPE_sums = p.val(sPE)

            p.wait("scalar", sPE, vPE_sums)
            p.op("scalar", "mul", mean_r[0:1, :], pb_sum[0:1, :], 1.0 / D)
            vACT_rows = p.op("scalar", "mul", var_r[0:1, :], pb_sumsq[0:1, :], 1.0 / D,
                             inc=(sACT, 1))
            p.wait("vector", sACT, vACT_rows)
            ve = p.op("vector", "tensor_mul", sd_r[0:1, :], mean_r[0:1, :],
                      mean_r[0:1, :], inc=(sDVE, 1))
            p.wait("vector", sDVE, ve)
            ve = p.op("vector", "tensor_sub", var_r[0:1, :], var_r[0:1, :], sd_r[0:1, :],
                      inc=(sDVE, 1))
            p.wait("vector", sDVE, ve)
            vDVE_var = p.op("vector", "tensor_scalar_add", var_r[0:1, :], var_r[0:1, :],
                            1e-5, inc=(sDVE, 1))
            p.wait("scalar", sDVE, vDVE_var)
            vACT_sd = p.op("scalar", "activation", sd_r[0:1, :], var_r[0:1, :],
                           AF.Sqrt, inc=(sACT, 1))
            p.wait("vector", sACT, vACT_sd)
            ve = p.op("vector", "reciprocal", rstd_r[0:1, :], sd_r[0:1, :],
                      inc=(sDVE, 1))
            p.wait("vector", sDVE, ve)
            vDVE_rstd = p.op("vector", "tensor_mul", mrstd_r[0:1, :], mean_r[0:1, :],
                             rstd_r[0:1, :], inc=(sDVE, 1))
            p.wait("tensor", sDVE, vDVE_rstd)
            for h in range(2):
                p.op("tensor", "matmul", pb_rstd[:, h * TH:(h + 1) * TH],
                     ones[0:1, :], rstd_r[0:1, h * TH:(h + 1) * TH], start=True, stop=True)
                p.op("tensor", "matmul", pb_mrstd[:, h * TH:(h + 1) * TH],
                     ones[0:1, :], mrstd_r[0:1, h * TH:(h + 1) * TH], start=True,
                     stop=True, inc=(sPE, 1) if h == 1 else None)
            vPE_bcast = p.val(sPE)
            p.wait("vector", sPE, vPE_bcast)
            for j in range(DJ):
                p.op("vector", "tensor_mul", chunk(xn, j), chunk(resid, j), pb_rstd[:, :])
                ve = p.op("vector", "tensor_sub", chunk(xn, j), chunk(xn, j),
                          pb_mrstd[:, :], inc=(sDVE, 1))
                p.wait("gpsimd", sDVE, ve)
                p.op("gpsimd", "tensor_copy", chunk(xnbf, j), chunk(xn, j),
                     inc=(sPOOL, 1))
            barrier()

            dbg_big("dxn", xn)

        with ExitStack() as ph:
            pb_g0f = ph.enter_context(nc.psum_tensor("pb_g0", [128, 512], F32))
            pb_g1f = ph.enter_context(nc.psum_tensor("pb_g1", [128, 512], F32))
            pb_bbf = ph.enter_context(nc.psum_tensor("pb_bb", [128, 512], F32))
            pb_t0f = ph.enter_context(nc.psum_tensor("pb_t0", [128, 512], F32))
            pb_t1f = ph.enter_context(nc.psum_tensor("pb_t1", [128, 512], F32))
            pb_bb = pb_bbf[:, 0:E]
            gw_sb = ph.enter_context(nc.sbuf_tensor("gw_sb", [128, DJ * E], F32))
            ident = ph.enter_context(nc.sbuf_tensor("ident", [128, 128], F32))
            sc_sb = ph.enter_context(nc.sbuf_tensor("sc_sb", [128, 2 * E], F32))
            adj = ph.enter_context(nc.sbuf_tensor("adj", [128, E], F32))
            m1c = ph.enter_context(nc.sbuf_tensor("m1c", [128, 1], F32))
            mk1 = ph.enter_context(nc.sbuf_tensor("mk1", [128, E], F32))
            mk2 = ph.enter_context(nc.sbuf_tensor("mk2", [128, E], F32))
            ssel = ph.enter_context(nc.sbuf_tensor("ssel", [128, 2 * E], F32))
            lg_sb = ph.enter_context(nc.sbuf_tensor("lg_sb", [E, T], F32))

            vX_gw = p.dma("scalar", sb3(gw_sb[:, :], E), tile3(gwT.ap()[:, :], E), sX)
            vX_id = p.dma("scalar", ident[:, :], identd[0:128, :], sX)

            p.op("tensor", "matmul", pb_bb, ones[0:1, :], bias1[0:1, :],
                 start=True, stop=True, inc=(sPE, 1))
            vPE_bb = p.val(sPE)
            p.wait("scalar", sPE, vPE_bb)
            vACT_bb = p.op("scalar", "copy", biasb[:, :], pb_bb, inc=(sACT, 1))

            pbg = [pb_g0f, pb_g1f]
            pbt = [pb_t0f, pb_t1f]
            lg_done = []
            p.wait("tensor", sX, vX_gw)
            for tb in range(2):
                bank = pbg[tb][0:E, 0:TH]
                for j in range(DJ):
                    p.op("tensor", "matmul", bank,
                         gw_sb[:, j * E:(j + 1) * E],
                         xn[:, j * T + tb * TH: j * T + (tb + 1) * TH],
                         start=(j == 0), stop=(j == DJ - 1),
                         inc=(sPE, 1) if j == DJ - 1 else None)
                vpe = p.val(sPE)
                p.wait("scalar", sPE, vpe)
                lg_done.append(p.op("scalar", "copy", lg_sb[:, tb * TH:(tb + 1) * TH],
                                    bank, inc=(sACT, 1)))
            act_sc = []
            dve_sel = []
            pe_tr1 = []
            pe_tr2 = []
            dve_cp = []
            p.wait("tensor", sX, vX_id)
            p.wait("tensor", sACT, lg_done[-1])
            for tc in range(8):
                t1 = pbg[tc % 2][0:128, 0:E]
                t2 = pbt[tc % 2][0:E, 0:128]
                if tc >= 2:
                    p.wait("tensor", sACT, act_sc[tc - 2])
                pe_tr1.append(p.op("tensor", "transpose", t1,
                                   lg_sb[0:E, tc * 128:(tc + 1) * 128],
                                   ident[0:E, 0:E], inc=(sPE, 1)))
                p.wait("scalar", sPE, pe_tr1[tc])
                scb = sc_sb[:, (tc % 2) * E:((tc % 2) + 1) * E]
                act_sc.append(p.op("scalar", "activation", scb, t1, AF.Sigmoid,
                                   inc=(sACT, 1)))
                p.wait("vector", sACT, act_sc[tc])
                if tc == 0:
                    p.wait("vector", sACT, vACT_bb)
                sselb = ssel[:, (tc % 2) * E:((tc % 2) + 1) * E]
                if tc >= 2:
                    p.wait("vector", sPE, pe_tr2[tc - 2])
                ve = p.op("vector", "tensor_add", adj[:, :], scb, biasb[:, :],
                          inc=(sDVE, 1))
                p.wait("vector", sDVE, ve)
                ve = p.op("vector", "tensor_reduce", m1c[:, :], adj[:, :], AX.X, ALU.max,
                          inc=(sDVE, 1))
                p.wait("vector", sDVE, ve)
                ve = p.op("vector", "tensor_single_scalar", mk1[:, :], adj[:, :],
                          m1c[:, 0:1], ALU.is_equal, inc=(sDVE, 1))
                p.wait("vector", sDVE, ve)
                ve = p.op("vector", "scalar_tensor_tensor", adj[:, :], mk1[:, :], -1e30,
                          adj[:, :], ALU.mult, ALU.add, inc=(sDVE, 1))
                p.wait("vector", sDVE, ve)
                ve = p.op("vector", "tensor_reduce", m1c[:, :], adj[:, :], AX.X, ALU.max,
                          inc=(sDVE, 1))
                p.wait("vector", sDVE, ve)
                ve = p.op("vector", "tensor_single_scalar", mk2[:, :], adj[:, :],
                          m1c[:, 0:1], ALU.is_equal, inc=(sDVE, 1))
                p.wait("vector", sDVE, ve)
                ve = p.op("vector", "tensor_add", mk1[:, :], mk1[:, :], mk2[:, :],
                          inc=(sDVE, 1))
                p.wait("vector", sDVE, ve)
                dve_sel.append(p.op("vector", "tensor_mul", sselb, mk1[:, :], scb,
                                    inc=(sDVE, 1)))
                p.wait("tensor", sDVE, dve_sel[tc])
                if tc >= 2:
                    p.wait("tensor", sDVE, dve_cp[tc - 2])
                pe_tr2.append(p.op("tensor", "transpose", t2, sselb, ident[:, :],
                                   inc=(sPE, 1)))
                p.wait("vector", sPE, pe_tr2[tc])
                dve_cp.append(p.op("vector", "tensor_copy",
                                   seT[:, tc * 128:(tc + 1) * 128], t2,
                                   inc=(sDVE, 1)))
            barrier()

            dbg_small("dse", seT[:, :])
        xnstack.close()

        with ExitStack() as ph:
            pb_u1a = ph.enter_context(nc.psum_tensor("pb_u1a", [128, TH], F32))
            pb_u3a = ph.enter_context(nc.psum_tensor("pb_u3a", [128, TH], F32))
            pb_u1b = ph.enter_context(nc.psum_tensor("pb_u1b", [128, TH], F32))
            pb_u3b = ph.enter_context(nc.psum_tensor("pb_u3b", [128, TH], F32))
            pb_dg = [ph.enter_context(nc.psum_tensor(f"pb_d{k}", [128, TH], F32))
                     for k in range(4)]
            w1sf = ph.enter_context(nc.sbuf_tensor("w1sf", [128, D], F32))
            w3sf = ph.enter_context(nc.sbuf_tensor("w3sf", [128, D], F32))
            w1s = ph.enter_context(nc.sbuf_tensor("w1s", [128, 2 * D], BF16))
            w3s = ph.enter_context(nc.sbuf_tensor("w3s", [128, 2 * D], BF16))
            w2sf = ph.enter_context(nc.sbuf_tensor("w2sf", [128, 3 * TH], F32))
            w2s = ph.enter_context(nc.sbuf_tensor("w2s", [128, 3 * TH], BF16))
            s1_scr = ph.enter_context(nc.sbuf_tensor("s1_scr", [128, TH], F32))
            g_s = ph.enter_context(nc.sbuf_tensor("g_s", [128, HJ * 2 * TH], BF16))
            pb_u1 = [pb_u1a, pb_u1b]
            pb_u3 = [pb_u3a, pb_u3b]

            pe_hist = []
            g_hist = []
            cast1_h = []
            cast3_h = []
            for i in range(HJ):
                b = i % 2
                for hf in range(2):
                    kk = 2 * i + hf
                    if kk >= 2:
                        p.wait("sync", sDVE, cast1_h[kk - 2])
                    srcw = tile3(sw1T.ap()[hf * 1024:(hf + 1) * 1024,
                                           i * 128:(i + 1) * 128], 128)
                    v1 = p.dma("sync", sb3(w1sf[:, (kk % 2) * 1024:(kk % 2 + 1) * 1024],
                                           128), srcw, sW)
                    if kk >= 2:
                        p.wait("sync", sPOOL, cast3_h[kk - 2])
                    srcw = tile3(sw3T.ap()[hf * 1024:(hf + 1) * 1024,
                                           i * 128:(i + 1) * 128], 128)
                    v3 = p.dma("sync", sb3(w3sf[:, (kk % 2) * 1024:(kk % 2 + 1) * 1024],
                                           128), srcw, sW)
                    p.wait("vector", sW, v1)
                    p.wait("gpsimd", sW, v3)
                    if len(pe_hist) >= 2:
                        p.wait("vector", sPE, pe_hist[-2])
                        p.wait("gpsimd", sPE, pe_hist[-2])
                    cast1_h.append(p.op("vector", "tensor_copy",
                                        w1s[:, b * D + hf * 1024: b * D + (hf + 1) * 1024],
                                        w1sf[:, (kk % 2) * 1024:(kk % 2 + 1) * 1024],
                                        inc=(sDVE, 1)))
                    cast3_h.append(p.op("gpsimd", "tensor_copy",
                                        w3s[:, b * D + hf * 1024: b * D + (hf + 1) * 1024],
                                        w3sf[:, (kk % 2) * 1024:(kk % 2 + 1) * 1024],
                                        inc=(sPOOL, 1)))
                p.wait("tensor", sDVE, cast1_h[-1])
                p.wait("tensor", sPOOL, cast3_h[-1])
                for hh in range(2):
                    s = (2 * i + hh) % 2
                    kq = 2 * i + hh
                    if kq >= 2:
                        p.wait("tensor", sDVE, g_hist[kq - 2])
                    for j in range(DJ):
                        p.op("tensor", "matmul", pb_u1[s][:, :],
                             w1s[:, b * D + j * 128: b * D + (j + 1) * 128],
                             xnbf[:, j * T + hh * TH: j * T + (hh + 1) * TH],
                             start=(j == 0), stop=(j == DJ - 1))
                    for j in range(DJ):
                        p.op("tensor", "matmul", pb_u3[s][:, :],
                             w3s[:, b * D + j * 128: b * D + (j + 1) * 128],
                             xnbf[:, j * T + hh * TH: j * T + (hh + 1) * TH],
                             start=(j == 0), stop=(j == DJ - 1),
                             inc=(sPE, 1) if j == DJ - 1 else None)
                    vpe_i = p.val(sPE)
                    p.wait("scalar", sPE, vpe_i)
                    if g_hist:
                        p.wait("scalar", sDVE, g_hist[-1])
                    vact = p.op("scalar", "activation", s1_scr[:, :], pb_u1[s][:, :],
                                AF.Silu, inc=(sACT, 1))
                    p.wait("vector", sACT, vact)
                    g_hist.append(p.op("vector", "tensor_mul",
                                       g_s[:, (2 * i + hh) * TH:(2 * i + hh + 1) * TH],
                                       s1_scr[:, :], pb_u3[s][:, :], inc=(sDVE, 1)))
                pe_hist.append(vpe_i)

            w2hist = []
            for dg in range(4):
                for j in range(HJ):
                    b2 = (dg * HJ + j) % 3
                    if len(w2hist) >= 3:
                        p.wait("sync", sDVE, w2hist[-3][0])
                    srcw = sw2T.ap()[j * 128:(j + 1) * 128, dg * TH:(dg + 1) * TH]
                    vw2 = p.dma("sync", w2sf[:, b2 * TH:(b2 + 1) * TH], srcw, sW)
                    p.wait("vector", sW, vw2)
                    if len(w2hist) >= 3:
                        p.wait("vector", sPE, w2hist[-3][1])
                    vcw2 = p.op("vector", "tensor_copy", w2s[:, b2 * TH:(b2 + 1) * TH],
                                w2sf[:, b2 * TH:(b2 + 1) * TH], inc=(sDVE, 1))
                    p.wait("tensor", sDVE, vcw2)
                    if dg == 0 and j == 0:
                        p.wait("tensor", sDVE, g_hist[-1])
                    for isub in range(4):
                        for hh in range(2):
                            bank = pb_dg[isub] if hh == 0 else pb_u1[isub % 2] \
                                if False else None
                    for isub in range(4):
                        for hh in range(2):
                            bank = (pb_dg + [pb_u1a, pb_u3a, pb_u1b, pb_u3b])[isub * 2 + hh]
                            p.op("tensor", "matmul", bank[:, :],
                                 w2s[:, b2 * TH + isub * 128: b2 * TH + (isub + 1) * 128],
                                 g_s[:, (2 * j + hh) * TH:(2 * j + hh + 1) * TH],
                                 start=(j == 0), stop=(j == HJ - 1),
                                 inc=(sPE, 1) if (isub, hh) == (3, 1) else None)
                    w2hist.append((vcw2, p.val(sPE)))
                vpe_dg = p.val(sPE)
                p.wait("vector", sPE, vpe_dg)
                for isub in range(4):
                    for hh in range(2):
                        di = dg * 4 + isub
                        bank = (pb_dg + [pb_u1a, pb_u3a, pb_u1b, pb_u3b])[isub * 2 + hh]
                        p.op("vector", "tensor_add", chunk(resid, di, hh),
                             chunk(resid, di, hh), bank[:, :],
                             inc=(sDVE, 1) if (isub, hh) == (3, 1) else None)
                p.wait("tensor", sDVE, p.val(sDVE))
            barrier()

            dbg_big("dres3", resid)

        with ExitStack() as ph:
            pa_h0 = ph.enter_context(nc.psum_tensor("pa_h0", [128, T], F32))
            pa_g0 = ph.enter_context(nc.psum_tensor("pa_g0", [128, T], F32))
            pa_h1 = ph.enter_context(nc.psum_tensor("pa_h1", [128, T], F32))
            pa_g1 = ph.enter_context(nc.psum_tensor("pa_g1", [128, T], F32))
            whf = ph.enter_context(nc.sbuf_tensor("whf", [128, D], F32))
            wgf = ph.enter_context(nc.sbuf_tensor("wgf", [128, D], F32))
            whbuf = ph.enter_context(nc.sbuf_tensor("whbuf", [128, 2 * D], BF16))
            wgbuf = ph.enter_context(nc.sbuf_tensor("wgbuf", [128, 2 * D], BF16))
            zbuf = ph.enter_context(nc.sbuf_tensor("zbuf", [128, T], F32))
            gbuf = ph.enter_context(nc.sbuf_tensor("gbuf", [128, T], F32))
            abuf = ph.enter_context(nc.sbuf_tensor("abuf", [128, T], F32))
            vbuf = ph.enter_context(nc.sbuf_tensor("vbuf", [128, T], F32))
            hscr = ph.enter_context(nc.sbuf_tensor("hscr", [128, T], F32))
            gscr = ph.enter_context(nc.sbuf_tensor("gscr", [128, T], F32))

            pa_h = [pa_h0, pa_h1]
            pa_g = [pa_g0, pa_g1]
            pe_c = []
            act_c = []
            dve_c = []
            pool_c = []
            vc_prev = {1: 0, 3: 0}
            for c in range(CI):
                b = c % 2
                p.wait("sync", sDVE, vc_prev[1])
                p.wait("sync", sPOOL, vc_prev[3])
                for hf in range(2):
                    srcw = tile3(whg.ap()[hf * 1024:(hf + 1) * 1024,
                                          c * 128:(c + 1) * 128], 128)
                    p.dma("sync", sb3(whf[:, hf * 1024:(hf + 1) * 1024], 128), srcw, sW)
                for hf in range(2):
                    srcw = tile3(whg.ap()[hf * 1024:(hf + 1) * 1024,
                                          DI + c * 128: DI + (c + 1) * 128], 128)
                    p.dma("sync", sb3(wgf[:, hf * 1024:(hf + 1) * 1024], 128), srcw, sW)
                vwf = p.val(sW)
                p.wait("vector", sW, vwf)
                p.wait("gpsimd", sW, vwf)
                if len(pe_c) >= 2:
                    p.wait("vector", sPE, pe_c[-2])
                    p.wait("gpsimd", sPE, pe_c[-2])
                vc1 = p.op("vector", "tensor_copy", whbuf[:, b * D:(b + 1) * D],
                           whf[:, :], inc=(sDVE, 1))
                vc3 = p.op("gpsimd", "tensor_copy", wgbuf[:, b * D:(b + 1) * D],
                           wgf[:, :], inc=(sPOOL, 1))
                vc_prev = {1: vc1, 3: vc3}

                p.wait("tensor", sDVE, vc1)
                p.wait("tensor", sPOOL, vc3)
                if c >= 2:
                    p.wait("tensor", sDVE, dve_c[c - 2])
                    p.wait("tensor", sACT, act_c[c - 2])
                for h in range(2):
                    for j in range(DJ):
                        p.op("tensor", "matmul", pa_h[b][:, h * TH:(h + 1) * TH],
                             whbuf[:, b * D + j * 128: b * D + (j + 1) * 128],
                             xnbf[:, j * T + h * TH: j * T + (h + 1) * TH],
                             start=(j == 0), stop=(j == DJ - 1))
                for h in range(2):
                    for j in range(DJ):
                        p.op("tensor", "matmul", pa_g[b][:, h * TH:(h + 1) * TH],
                             wgbuf[:, b * D + j * 128: b * D + (j + 1) * 128],
                             xnbf[:, j * T + h * TH: j * T + (h + 1) * TH],
                             start=(j == 0), stop=(j == DJ - 1),
                             inc=(sPE, 1) if (h == 1 and j == DJ - 1) else None)
                pe_c.append(p.val(sPE))

                p.wait("scalar", sPE, pe_c[c])
                if c >= 1:
                    p.wait("scalar", sDVE, dve_c[c - 1])
                    p.wait("scalar", sPOOL, pool_c[c - 1])
                p.op("scalar", "activation", zbuf[:, :], pa_g[b][:, :], AF.Sigmoid)
                act_c.append(p.op("scalar", "activation", gbuf[:, :], pa_h[b][:, :],
                                  AF.Sigmoid, inc=(sACT, 1)))
                p.wait("gpsimd", sACT, act_c[c])
                if c >= 1:
                    p.wait("gpsimd", sDVE, dve_c[c - 1])
                pool_c.append(p.op("gpsimd", "tensor_scalar", abuf[:, :], zbuf[:, :],
                                   -1.0, 1.0, ALU.mult, ALU.add, inc=(sPOOL, 1)))
                p.wait("vector", sACT, act_c[c])
                ve = p.op("vector", "scalar_tensor_tensor", gbuf[:, :], pa_h[b][:, :],
                          0.5, gbuf[:, :], ALU.add, ALU.max, inc=(sDVE, 1))
                p.wait("vector", sDVE, ve)
                ve = p.op("vector", "tensor_mul", vbuf[:, :], zbuf[:, :], gbuf[:, :],
                          inc=(sDVE, 1))
                p.wait("vector", sDVE, ve)
                p.wait("vector", sPOOL, pool_c[c])
                ve = p.op("vector", "tensor_tensor_scan", hscr[:, :], abuf[:, :],
                          vbuf[:, :], 0.0, ALU.mult, ALU.add, inc=(sDVE, 1))
                p.wait("vector", sDVE, ve)
                vsc = p.op("vector", "tensor_tensor_scan", gscr[:, :], abuf[:, :],
                           zeros[:, :], 1.0, ALU.mult, ALU.add, inc=(sDVE, 1))
                p.wait("vector", sDVE, vsc)
                p.op("vector", "tensor_copy", aggO[:, c:c + 1], gscr[:, T - 1:T])
                dve_c.append(p.op("vector", "tensor_copy", aggO[:, CI + c: CI + c + 1],
                                  hscr[:, T - 1:T], inc=(sDVE, 1)))

            vdve_agg = dve_c[-1]
            p.wait("gpsimd", sDVE, vdve_agg)
            vG_aggout = p.dma("gpsimd", cc_in.ap(), aggO[:, :], sG)
            p.wait("gpsimd", sG, vG_aggout)
            p.op("gpsimd", "collective_compute", "AllGather", ALU.bypass,
                 replica_groups=[list(range(NCORES))],
                 ins=[cc_in.ap()], outs=[cc_out.ap()], inc=(sCC, 1))
            vCC = p.val(sCC)
            barrier()
        zstack.close()

        with ExitStack() as ph:
            pu = [ph.enter_context(nc.psum_tensor(f"pe_u{k}", [128, TH], F32))
                  for k in range(8)]
            g_e = ph.enter_context(nc.sbuf_tensor("g_e", [128, HJ * 2 * TH], BF16))
            w1e = ph.enter_context(nc.sbuf_tensor("w1e", [128, 2 * D], BF16))
            w3e = ph.enter_context(nc.sbuf_tensor("w3e", [128, 2 * D], BF16))
            w1f = ph.enter_context(nc.sbuf_tensor("w1f", [128, D], F32))
            w3f = ph.enter_context(nc.sbuf_tensor("w3f", [128, D], F32))
            w2f = ph.enter_context(nc.sbuf_tensor("w2f", [128, 3 * TH], F32))
            w2e = ph.enter_context(nc.sbuf_tensor("w2e", [128, 3 * TH], BF16))
            bcast = ph.enter_context(nc.sbuf_tensor("bcast", [128, T], F32))
            s1e = ph.enter_context(nc.sbuf_tensor("s1e", [128, TH], F32))
            t1b = ph.enter_context(nc.sbuf_tensor("t1b", [128, TH], F32))
            t3b = ph.enter_context(nc.sbuf_tensor("t3b", [128, TH], F32))

            def emit_bcast(e, wait_dve=None):
                for hh in range(2):
                    p.op("tensor", "matmul", pu[4 + hh][:, :],
                         sel8[:, e * 128:(e + 1) * 128],
                         seT[:, hh * TH:(hh + 1) * TH], start=True, stop=True,
                         inc=(sPE, 1) if hh == 1 else None)
                vpe_bc = p.val(sPE)
                p.wait("scalar", sPE, vpe_bc)
                if wait_dve is not None:
                    p.wait("scalar", sDVE, wait_dve)
                for hh in range(2):
                    p.op("scalar", "copy", bcast[:, hh * TH:(hh + 1) * TH],
                         pu[4 + hh][:, :], inc=(sACT, 1) if hh == 1 else None)
                return p.val(sACT)

            vact_bc = emit_bcast(0)

            last_g = 0
            prev_set_g = {}
            hist_pe_i = []
            cast1_h = []
            cast3_h = []
            for e in range(E):
                for i in range(HJ):
                    b = (e * HJ + i) % 2
                    st = b * 4
                    for hf in range(2):
                        kk = 2 * (e * HJ + i) + hf
                        if kk >= 2:
                            p.wait("sync", sDVE, cast1_h[kk - 2])
                        srcw = tile3(w1Tf.ap()[e * D + hf * 1024: e * D + (hf + 1) * 1024,
                                               i * 128:(i + 1) * 128], 128)
                        v1 = p.dma("sync",
                                   sb3(w1f[:, (kk % 2) * 1024:(kk % 2 + 1) * 1024], 128),
                                   srcw, sW)
                        if kk >= 2:
                            p.wait("sync", sPOOL, cast3_h[kk - 2])
                        srcw = tile3(w3Tf.ap()[e * D + hf * 1024: e * D + (hf + 1) * 1024,
                                               i * 128:(i + 1) * 128], 128)
                        v3 = p.dma("sync",
                                   sb3(w3f[:, (kk % 2) * 1024:(kk % 2 + 1) * 1024], 128),
                                   srcw, sW)
                        p.wait("vector", sW, v1)
                        p.wait("gpsimd", sW, v3)
                        if len(hist_pe_i) >= 2:
                            p.wait("vector", sPE, hist_pe_i[-2])
                            p.wait("gpsimd", sPE, hist_pe_i[-2])
                        cast1_h.append(p.op(
                            "vector", "tensor_copy",
                            w1e[:, b * D + hf * 1024: b * D + (hf + 1) * 1024],
                            w1f[:, (kk % 2) * 1024:(kk % 2 + 1) * 1024], inc=(sDVE, 1)))
                        cast3_h.append(p.op(
                            "gpsimd", "tensor_copy",
                            w3e[:, b * D + hf * 1024: b * D + (hf + 1) * 1024],
                            w3f[:, (kk % 2) * 1024:(kk % 2 + 1) * 1024], inc=(sPOOL, 1)))
                    p.wait("tensor", sDVE, cast1_h[-1])
                    p.wait("tensor", sPOOL, cast3_h[-1])
                    k = e * HJ + i
                    if k >= 2:
                        p.wait("tensor", sDVE, prev_set_g[k - 2])
                    if k <= 1:
                        p.wait("tensor", sACT, vact_bc)
                    for ui, wsb in ((0, w1e), (1, w3e)):
                        for hh in range(2):
                            bank = pu[st + 2 * ui + hh]
                            for j in range(DJ):
                                p.op("tensor", "matmul", bank[:, :],
                                     wsb[:, b * D + j * 128: b * D + (j + 1) * 128],
                                     xnbf[:, j * T + hh * TH: j * T + (hh + 1) * TH],
                                     start=(j == 0), stop=(j == DJ - 1),
                                     inc=(sPE, 1) if (ui, hh, j) == (1, 1, DJ - 1) else None)
                    hist_pe_i.append(p.val(sPE))
                    p.wait("vector", sPE, hist_pe_i[-1])
                    if k == 0:
                        p.wait("vector", sACT, vact_bc)
                    for hh in range(2):
                        u1 = pu[st + hh]
                        u3 = pu[st + 2 + hh]
                        bsl = bcast[:, hh * TH:(hh + 1) * TH]
                        vt1 = p.op("vector", "tensor_mul", t1b[:, :], u1[:, :], bsl,
                                   inc=(sDVE, 1))
                        p.wait("scalar", sDVE, vt1)
                        va = p.op("scalar", "activation", s1e[:, :], t1b[:, :], AF.Silu,
                                  inc=(sACT, 1))
                        vt3 = p.op("vector", "tensor_tensor", t3b[:, :], u3[:, :], bsl,
                                   ALU.mult, inc=(sDVE, 1))
                        p.wait("vector", sDVE, vt3)
                        p.wait("vector", sACT, va)
                        vg = p.op("vector", "tensor_mul",
                                  g_e[:, (2 * i + hh) * TH:(2 * i + hh + 1) * TH],
                                  s1e[:, :], t3b[:, :], inc=(sDVE, 1))
                        p.wait("vector", sACT, va)
                    prev_set_g[k] = p.val(sDVE)
                last_g = p.val(sDVE)

                if e + 1 < E:
                    p.wait("tensor", sDVE, last_g)
                    vact_bc = emit_bcast(e + 1, wait_dve=last_g)

                w2hist = []
                for dg in range(4):
                    for j in range(HJ):
                        b2 = (dg * HJ + j) % 3
                        if len(w2hist) >= 3:
                            p.wait("sync", sDVE, w2hist[-3][0])
                        srcw = w2Tf.ap()[e * HID + j * 128: e * HID + (j + 1) * 128,
                                         dg * TH:(dg + 1) * TH]
                        vw2 = p.dma("sync", w2f[:, b2 * TH:(b2 + 1) * TH], srcw, sW)
                        p.wait("vector", sW, vw2)
                        if len(w2hist) >= 3:
                            p.wait("vector", sPE, w2hist[-3][1])
                        vcw2 = p.op("vector", "tensor_copy",
                                    w2e[:, b2 * TH:(b2 + 1) * TH],
                                    w2f[:, b2 * TH:(b2 + 1) * TH], inc=(sDVE, 1))
                        p.wait("tensor", sDVE, vcw2)
                        if dg == 0 and j == 0:
                            p.wait("tensor", sDVE, last_g)
                            p.wait("tensor", sACT, p.val(sACT))
                        for isub in range(4):
                            for hh in range(2):
                                bank = pu[isub * 2 + hh]
                                p.op("tensor", "matmul", bank[:, :],
                                     w2e[:, b2 * TH + isub * 128: b2 * TH + (isub + 1) * 128],
                                     g_e[:, (2 * j + hh) * TH:(2 * j + hh + 1) * TH],
                                     start=(j == 0), stop=(j == HJ - 1),
                                     inc=(sPE, 1) if (isub, hh) == (3, 1) else None)
                        w2hist.append((vcw2, p.val(sPE)))
                    vpe_dg = p.val(sPE)
                    p.wait("vector", sPE, vpe_dg)
                    for isub in range(4):
                        for hh in range(2):
                            di = dg * 4 + isub
                            p.op("vector", "tensor_add", chunk(resid, di, hh),
                                 chunk(resid, di, hh), pu[isub * 2 + hh][:, :],
                                 inc=(sDVE, 1) if (isub, hh) == (3, 1) else None)
                    p.wait("tensor", sDVE, p.val(sDVE))
                prev_set_g[e * HJ + HJ - 1] = p.val(sDVE)
                prev_set_g[e * HJ + HJ - 2] = p.val(sDVE)
            barrier()

        dbg_big("dres4", resid)

        GRP = 12
        with ExitStack() as ph:
            pb_hB = [ph.enter_context(nc.psum_tensor(f"pbh{k}", [128, T], F32))
                     for k in range(2)]
            pb_gB = [ph.enter_context(nc.psum_tensor(f"pbg{k}", [128, T], F32))
                     for k in range(2)]
            po = []
            for tsr in (pb_hB[0], pb_gB[0], pb_hB[1], pb_gB[1]):
                po.append(tsr[:, 0:TH])
                po.append(tsr[:, TH:2 * TH])
            whfB = ph.enter_context(nc.sbuf_tensor("whfB", [128, D], F32))
            wgfB = ph.enter_context(nc.sbuf_tensor("wgfB", [128, D], F32))
            whbufB = ph.enter_context(nc.sbuf_tensor("whbufB", [128, 2 * D], BF16))
            wgbufB = ph.enter_context(nc.sbuf_tensor("wgbufB", [128, 2 * D], BF16))
            woutf = ph.enter_context(nc.sbuf_tensor("woutf", [128, 3 * TH], F32))
            woutb = ph.enter_context(nc.sbuf_tensor("woutb", [128, 3 * TH], BF16))
            zbufB = ph.enter_context(nc.sbuf_tensor("zbufB", [128, T], F32))
            gbufB = ph.enter_context(nc.sbuf_tensor("gbufB", [128, T], F32))
            abufB = ph.enter_context(nc.sbuf_tensor("abufB", [128, T], F32))
            vbufB = ph.enter_context(nc.sbuf_tensor("vbufB", [128, T], F32))
            hgrp = ph.enter_context(nc.sbuf_tensor("hgrp", [128, GRP * T], BF16))
            maskM = ph.enter_context(nc.sbuf_tensor("maskM", [128, NCORES * 48], F32))
            maskB = ph.enter_context(nc.sbuf_tensor("maskB", [128, NCORES * 48], F32))
            gathr = ph.enter_context(nc.sbuf_tensor("gathr", [128, NCORES * 48], F32))
            gmr = ph.enter_context(nc.sbuf_tensor("gmr", [128, 48], F32))
            ctmp = ph.enter_context(nc.sbuf_tensor("ctmp", [128, CI], F32))

            p.dma("gpsimd", sb3(maskM[:, :], 48), tile3(maskMd.ap()[:, :], 48), sG)
            p.dma("gpsimd", sb3(maskB[:, :], 48), tile3(maskBd.ap()[:, :], 48), sG)
            p.wait("gpsimd", sCC, vCC)
            for r in range(NCORES):
                p.dma("gpsimd", gathr[:, r * 48:(r + 1) * 48],
                      cc_out.ap()[r * 128:(r + 1) * 128, :], sG)
            vG_gath = p.val(sG)

            p.wait("vector", sG, vG_gath)
            for r in range(NCORES):
                gr = gathr[:, r * 48:(r + 1) * 48]
                ve = p.op("vector", "tensor_mul", gmr[:, :], gr,
                          maskM[:, r * 48:(r + 1) * 48], inc=(sDVE, 1))
                p.wait("vector", sDVE, ve)
                ve = p.op("vector", "tensor_add", gmr[:, :], gmr[:, :],
                          maskB[:, r * 48:(r + 1) * 48], inc=(sDVE, 1))
                p.wait("vector", sDVE, ve)
                ve = p.op("vector", "tensor_mul", ctmp[:, :], gmr[:, 0:CI], carry[:, :],
                          inc=(sDVE, 1))
                p.wait("vector", sDVE, ve)
                ve = p.op("vector", "tensor_add", carry[:, :], ctmp[:, :],
                          gmr[:, CI:2 * CI], inc=(sDVE, 1))
                p.wait("vector", sDVE, ve)
            vDVE_carry = p.val(sDVE)

            if DEBUG:
                p.wait("gpsimd", sDVE, vDVE_carry)
            dbg_small("dcarry", carry[:, :])
            dbg_small("dagg", aggO[:, :])
            dbg_small("dgath", gathr[:, :])
            dbg_small("dmskM", maskM[:, :])

            pe_cB = {}
            act_cB = {}
            dve_cB = {}
            pool_cB = {}
            castB1 = []
            castB3 = []
            vout_evac = 0
            for grp in range(2):
                chunks = list(range(grp * GRP, (grp + 1) * GRP))
                for ci_, c in enumerate(chunks):
                    b = c % 2
                    for hf in range(2):
                        kk = 2 * c + hf
                        if kk >= 2:
                            p.wait("sync", sDVE, castB1[kk - 2])
                        srcw = tile3(whg.ap()[hf * 1024:(hf + 1) * 1024,
                                              c * 128:(c + 1) * 128], 128)
                        v1 = p.dma("sync",
                                   sb3(whfB[:, (kk % 2) * 1024:(kk % 2 + 1) * 1024], 128),
                                   srcw, sW)
                        if kk >= 2:
                            p.wait("sync", sPOOL, castB3[kk - 2])
                        srcw = tile3(whg.ap()[hf * 1024:(hf + 1) * 1024,
                                              DI + c * 128: DI + (c + 1) * 128], 128)
                        v3 = p.dma("sync",
                                   sb3(wgfB[:, (kk % 2) * 1024:(kk % 2 + 1) * 1024], 128),
                                   srcw, sW)
                        p.wait("vector", sW, v1)
                        p.wait("gpsimd", sW, v3)
                        if c >= 2:
                            p.wait("vector", sPE, pe_cB[c - 2])
                            p.wait("gpsimd", sPE, pe_cB[c - 2])
                        castB1.append(p.op(
                            "vector", "tensor_copy",
                            whbufB[:, b * D + hf * 1024: b * D + (hf + 1) * 1024],
                            whfB[:, (kk % 2) * 1024:(kk % 2 + 1) * 1024], inc=(sDVE, 1)))
                        castB3.append(p.op(
                            "gpsimd", "tensor_copy",
                            wgbufB[:, b * D + hf * 1024: b * D + (hf + 1) * 1024],
                            wgfB[:, (kk % 2) * 1024:(kk % 2 + 1) * 1024], inc=(sPOOL, 1)))

                    p.wait("tensor", sDVE, castB1[-1])
                    p.wait("tensor", sPOOL, castB3[-1])
                    if ci_ < 2 and grp > 0:
                        p.wait("tensor", sDVE, vout_evac)
                    if c >= 2:
                        p.wait("tensor", sDVE, dve_cB[c - 2])
                        p.wait("tensor", sACT, act_cB[c - 2])
                    for h in range(2):
                        for j in range(DJ):
                            p.op("tensor", "matmul", pb_hB[b][:, h * TH:(h + 1) * TH],
                                 whbufB[:, b * D + j * 128: b * D + (j + 1) * 128],
                                 xnbf[:, j * T + h * TH: j * T + (h + 1) * TH],
                                 start=(j == 0), stop=(j == DJ - 1))
                    for h in range(2):
                        for j in range(DJ):
                            p.op("tensor", "matmul", pb_gB[b][:, h * TH:(h + 1) * TH],
                                 wgbufB[:, b * D + j * 128: b * D + (j + 1) * 128],
                                 xnbf[:, j * T + h * TH: j * T + (h + 1) * TH],
                                 start=(j == 0), stop=(j == DJ - 1),
                                 inc=(sPE, 1) if (h == 1 and j == DJ - 1) else None)
                    pe_cB[c] = p.val(sPE)

                    p.wait("scalar", sPE, pe_cB[c])
                    if c >= 1:
                        p.wait("scalar", sDVE, dve_cB[c - 1])
                        p.wait("scalar", sPOOL, pool_cB[c - 1])
                    p.op("scalar", "activation", zbufB[:, :], pb_gB[b][:, :], AF.Sigmoid)
                    act_cB[c] = p.op("scalar", "activation", gbufB[:, :],
                                     pb_hB[b][:, :], AF.Sigmoid, inc=(sACT, 1))
                    p.wait("gpsimd", sACT, act_cB[c])
                    if c >= 1:
                        p.wait("gpsimd", sDVE, dve_cB[c - 1])
                    pool_cB[c] = p.op("gpsimd", "tensor_scalar", abufB[:, :],
                                      zbufB[:, :], -1.0, 1.0, ALU.mult, ALU.add,
                                      inc=(sPOOL, 1))
                    p.wait("vector", sACT, act_cB[c])
                    ve = p.op("vector", "scalar_tensor_tensor", gbufB[:, :],
                              pb_hB[b][:, :], 0.5, gbufB[:, :], ALU.add, ALU.max,
                              inc=(sDVE, 1))
                    p.wait("vector", sDVE, ve)
                    ve = p.op("vector", "tensor_mul", vbufB[:, :], zbufB[:, :],
                              gbufB[:, :], inc=(sDVE, 1))
                    p.wait("vector", sDVE, ve)
                    p.wait("vector", sPOOL, pool_cB[c])
                    if c == 0:
                        p.wait("vector", sDVE, vDVE_carry)
                    dve_cB[c] = p.op("vector", "tensor_tensor_scan",
                                     hgrp[:, ci_ * T:(ci_ + 1) * T], abufB[:, :],
                                     vbufB[:, :], carry[:, c:c + 1], ALU.mult, ALU.add,
                                     inc=(sDVE, 1))

                vgrp_h = dve_cB[chunks[-1]]
                wo_hist = []
                for dset in range(4):
                    for ci_, c in enumerate(chunks):
                        b2 = (dset * GRP + ci_) % 3
                        if len(wo_hist) >= 3:
                            p.wait("scalar", sPOOL, wo_hist[-3][0])
                        srcw = wout.ap()[c * 128:(c + 1) * 128,
                                         dset * TH:(dset + 1) * TH]
                        vwo = p.dma("scalar", woutf[:, b2 * TH:(b2 + 1) * TH], srcw, sX)
                        p.wait("gpsimd", sX, vwo)
                        if len(wo_hist) >= 3:
                            p.wait("gpsimd", sPE, wo_hist[-3][1])
                        vco = p.op("gpsimd", "tensor_copy",
                                   woutb[:, b2 * TH:(b2 + 1) * TH],
                                   woutf[:, b2 * TH:(b2 + 1) * TH], inc=(sPOOL, 1))
                        p.wait("tensor", sPOOL, vco)
                        if dset == 0 and ci_ == 0:
                            p.wait("tensor", sDVE, vgrp_h)
                            p.wait("tensor", sACT, act_cB[chunks[-1]])
                        for isub in range(4):
                            for hh in range(2):
                                p.op("tensor", "matmul", po[isub * 2 + hh],
                                     woutb[:, b2 * TH + isub * 128: b2 * TH + (isub + 1) * 128],
                                     hgrp[:, ci_ * T + hh * TH: ci_ * T + (hh + 1) * TH],
                                     start=(ci_ == 0), stop=(ci_ == GRP - 1),
                                     inc=(sPE, 1) if (isub, hh) == (3, 1) else None)
                        wo_hist.append((vco, p.val(sPE)))
                    p.wait("vector", sPE, p.val(sPE))
                    for isub in range(4):
                        for hh in range(2):
                            di = dset * 4 + isub
                            p.op("vector", "tensor_add", chunk(resid, di, hh),
                                 chunk(resid, di, hh), po[isub * 2 + hh],
                                 inc=(sDVE, 1) if (isub, hh) == (3, 1) else None)
                    p.wait("tensor", sDVE, p.val(sDVE))
                vout_evac = p.val(sDVE)
            barrier()

            dbg_big("dres5", resid)


        for gi in range(4):
            p.dma("scalar", tile3(yT.ap()[gi * 512:(gi + 1) * 512, :], T),
                  sb3(resid[:, gi * 4 * T:(gi + 1) * 4 * T], T), sOUT)
        vOUT = p.val(sOUT)
        p.wait("scalar", sOUT, vOUT)
        p.wait("sync", sOUT, vOUT)
        if DEBUG:
            p.wait("gpsimd", sG, p.val(sG))

        with nc.Block() as block:
            @block.sync
            def _(eng):
                for fn in p.q["sync"]:
                    fn(eng)

            @block.tensor
            def _(eng):
                for fn in p.q["tensor"]:
                    fn(eng)

            @block.vector
            def _(eng):
                for fn in p.q["vector"]:
                    fn(eng)

            @block.scalar
            def _(eng):
                for fn in p.q["scalar"]:
                    fn(eng)

            @block.gpsimd
            def _(eng):
                for fn in p.q["gpsimd"]:
                    fn(eng)

    return nc


def _host_stage(inputs):
    x = np.asarray(inputs["L0_attn_x"], np.float32)
    xT = np.ascontiguousarray(x[0].T)
    whg = np.ascontiguousarray(np.asarray(inputs["L0_attn_w_hg"], np.float32))
    wout = np.ascontiguousarray(np.asarray(inputs["L0_attn_w_out"], np.float32))
    gwT = np.ascontiguousarray(np.asarray(inputs["L0_ffn_gate_w"], np.float32).T)
    w1Tf = np.ascontiguousarray(
        np.swapaxes(np.asarray(inputs["L0_ffn_w1"], np.float32), 1, 2).reshape(E * D, HID))
    w3Tf = np.ascontiguousarray(
        np.swapaxes(np.asarray(inputs["L0_ffn_w3"], np.float32), 1, 2).reshape(E * D, HID))
    w2Tf = np.ascontiguousarray(
        np.swapaxes(np.asarray(inputs["L0_ffn_w2"], np.float32), 1, 2).reshape(E * HID, D))
    sw1T = np.ascontiguousarray(np.asarray(inputs["L0_ffn_sw1"], np.float32).T)
    sw3T = np.ascontiguousarray(np.asarray(inputs["L0_ffn_sw3"], np.float32).T)
    sw2T = np.ascontiguousarray(np.asarray(inputs["L0_ffn_sw2"], np.float32).T)
    biasd = np.ascontiguousarray(
        np.asarray(inputs["L0_ffn_expert_bias"], np.float32).reshape(1, E))
    onesd = np.ones((128, 128), np.float32)
    identd = np.eye(128, dtype=np.float32)
    sel8d = np.zeros((E, E * 128), np.float32)
    for e in range(E):
        sel8d[e, e * 128:(e + 1) * 128] = 1.0
    shared = dict(whg=whg, wout=wout, gwT=gwT, w1Tf=w1Tf, w3Tf=w3Tf, w2Tf=w2Tf,
                  sw1T=sw1T, sw3T=sw3T, sw2T=sw2T, biasd=biasd, onesd=onesd,
                  identd=identd, sel8d=sel8d)
    in_maps = []
    for c in range(NCORES):
        mM = np.zeros((NCORES * 128, 48), np.float32)
        mB = np.zeros((NCORES * 128, 48), np.float32)
        for r in range(NCORES):
            if r < c:
                mM[r * 128:(r + 1) * 128, :] = 1.0
            else:
                mB[r * 128:(r + 1) * 128, 0:CI] = 1.0
        m = dict(shared)
        m["xT"] = np.ascontiguousarray(xT[:, c * T:(c + 1) * T])
        m["maskMd"] = mM
        m["maskBd"] = mB
        in_maps.append(m)
    return in_maps


def kernel(**inputs):
    in_maps = _host_stage(inputs)
    nc = build_nc()
    res = run_bass_kernel_spmd(nc, in_maps, core_ids=list(range(NCORES)))
    yT = np.concatenate([res.results[c]["yT"] for c in range(NCORES)], axis=1)
    y = np.ascontiguousarray(yT.T).reshape(1, S, D).astype(np.float32)
    return y
